# version 04
# baseline (speedup 1.0000x reference)
import numpy as np
import concourse.bass as bass
import concourse.mybir as mybir
from concourse.bass_utils import run_bass_kernel_spmd

F32 = mybir.dt.float32
BF16 = mybir.dt.bfloat16
I32 = mybir.dt.int32
ALU = mybir.AluOpType
ACT = mybir.ActivationFunctionType
AXX = mybir.AxisListType.X

T, D, NH, DH, II, NC = 4096, 1024, 4, 256, 3072, 8
QMAX = 32767.0
NT = T // 128
NDC = D // 128
NIC = II // 128
NT5 = T // 512


_SCRAPF = [None]


def vfence(eng, s=None):
    sf = _SCRAPF[0]
    if hasattr(eng, "memset"):
        eng.memset(sf[:, 0:256], 0.0)
    else:
        eng.copy(sf[:, 256:512], sf[:, 0:256])


class Sem:

    def __init__(self, nc, name):
        self.sem = nc.alloc_semaphore(name)
        self.v = 0

    def inc(self, instr, n=1):
        instr.then_inc(self.sem, n)
        self.v += n
        return instr

    def dma(self, instr):
        return self.inc(instr, 16)

    def wait(self, eng, v=None):
        eng.wait_ge(self.sem, self.v if v is None else v)


_FAKE_CC = [False]
_LIM = [NH, NT5]
_DIS = set()


def build_nc(debug=(), upto=99):
    nc = bass.Bass("TRN2", target_bir_lowering=False, debug=False,
                   detect_race_conditions=False)

    x_in = nc.dram_tensor("x_in", [T, D], F32, kind="ExternalInput")
    wqd_in = nc.dram_tensor("wqd_in", [NDC, 128, 128], F32, kind="ExternalInput")
    wkd_in = nc.dram_tensor("wkd_in", [NDC, 128, 128], F32, kind="ExternalInput")
    wvd_in = nc.dram_tensor("wvd_in", [NDC, 128, 128], F32, kind="ExternalInput")
    wi_in = nc.dram_tensor("wi_in", [NH, 3 * D], F32, kind="ExternalInput")
    wf_in = nc.dram_tensor("wf_in", [NH, 3 * D], F32, kind="ExternalInput")
    bi_in = nc.dram_tensor("bi_in", [NH, 1], F32, kind="ExternalInput")
    bf_in = nc.dram_tensor("bf_in", [NH, 1], F32, kind="ExternalInput")
    lnw_in = nc.dram_tensor("lnw_in", [1, D], F32, kind="ExternalInput")
    wgate_in = nc.dram_tensor("wgate_in", [II, D], F32, kind="ExternalInput")
    wup_in = nc.dram_tensor("wup_in", [II, D], F32, kind="ExternalInput")
    wdown_in = nc.dram_tensor("wdown_in", [D, II], F32, kind="ExternalInput")
    consts_in = nc.dram_tensor("consts_in", [128, 642], F32, kind="ExternalInput")
    masks_in = nc.dram_tensor("masks_in", [4, 128, 512], BF16, kind="ExternalInput")

    out_d = nc.dram_tensor("out", [T, D], F32, kind="ExternalOutput")

    dbg = {}

    def dbg_out(name, shape):
        if name in debug:
            dbg[name] = nc.dram_tensor("dbg_" + name, shape, F32, kind="ExternalOutput")
        return dbg.get(name)

    S_tup = nc.dram_tensor("S_tup", [II, T], F32)
    S_tg = nc.dram_tensor("S_tg", [II, T], F32)
    S_p2 = nc.dram_tensor("S_p2", [II, T], F32)
    S_rup = nc.dram_tensor("S_rup", [II, T], F32)
    S_p3 = nc.dram_tensor("S_p3", [II, T], F32)
    S_qh = nc.dram_tensor("S_qh", [D, T], BF16)
    S_ql = nc.dram_tensor("S_ql", [D, T], BF16)
    S_kh = nc.dram_tensor("S_kh", [D, T], BF16)
    S_kl = nc.dram_tensor("S_kl", [D, T], BF16)
    S_v = nc.dram_tensor("S_v", [NH, T, DH], F32)
    S_attn = nc.dram_tensor("S_attn", [NH, T, DH], F32)
    S_st = nc.dram_tensor("S_st", [128, 512], F32)
    S_sm = nc.dram_tensor("S_sm", [64, D], F32)
    AG_in = nc.dram_tensor("AG_in", [1, 1056], F32)
    AG_out = nc.dram_tensor("AG_out", [NC, 1056], F32)

    s_in = Sem(nc, "s_in")
    s_g = Sem(nc, "s_g")
    s_sc = Sem(nc, "s_sc")
    s_pe = Sem(nc, "s_pe")
    s_v = Sem(nc, "s_v")
    s_a = Sem(nc, "s_a")
    s_cc = Sem(nc, "s_cc")
    sx = [Sem(nc, "sx0"), Sem(nc, "sx1")]
    sg2 = [Sem(nc, "sg20"), Sem(nc, "sg21")]
    s_em = Sem(nc, "s_em")

    MU = nc.alloc_sbuf_tensor("MU", [128, NT], F32)
    MSQ = nc.alloc_sbuf_tensor("MSQ", [128, NT], F32)
    MXX = nc.alloc_sbuf_tensor("MXX", [128, NT], F32)
    MNN = nc.alloc_sbuf_tensor("MNN", [128, NT], F32)
    RINV = nc.alloc_sbuf_tensor("RINV", [128, NT], F32)
    NU = nc.alloc_sbuf_tensor("NU", [128, NT], F32)
    RABS = nc.alloc_sbuf_tensor("RABS", [128, NT], F32)
    SC1 = nc.alloc_sbuf_tensor("SC1", [1, 32], F32)
    IDENT = nc.alloc_sbuf_tensor("IDENT", [128, 128], F32)
    IOTA = nc.alloc_sbuf_tensor("IOTA", [128, 64], F32)
    IOTAP = nc.alloc_sbuf_tensor("IOTAP", [128, 1], F32)
    ONESC = nc.alloc_sbuf_tensor("ONESC", [128, 1], F32)
    _cvals = [1e-5, -1.0, 65536.0, 0.5, 0.0625, 1e-8]
    _ctens = []
    for _j, _cv in enumerate(_cvals):
        _t = nc.alloc_sbuf_tensor("CONST%d" % _j, [128, 1], F32)
        _ctens.append(_t)
        nc.const_aps.register(F32, _cv, _t[:, 0:1])
    BNAMES = ["s0", "invs0", "cup", "cg", "sg", "c2", "c3", "s3", "sup", "s2"]
    B = {n: nc.alloc_sbuf_tensor("B_" + n, [128, 1], F32) for n in BNAMES}
    SCR = nc.alloc_sbuf_tensor("SCR", [8, 32], F32)
    SCRI = nc.alloc_sbuf_tensor("SCRI", [1, 8], I32)
    P1A = nc.alloc_sbuf_tensor("P1A", [128, 4], F32)

    XH = nc.alloc_sbuf_tensor("XH", [128, NDC * T], BF16)
    XL = nc.alloc_sbuf_tensor("XL", [128, NDC * T], BF16)
    FS = nc.alloc_sbuf_tensor("FS", [128, 8448], F32)
    GF = nc.alloc_sbuf_tensor("GF", [NH, T], F32)
    GV = FS[0:NH, 4096:8192]
    AGIS = FS[0:1, 0:1056]
    AGOS = FS[0:NC, 1056:2112]
    P1R = FS[0:2, 3136:3264]
    P1B = FS[0:1, 3264:3392]
    CT = nc.alloc_sbuf_tensor("CT", [128, NT * NH], F32)
    ET = nc.alloc_sbuf_tensor("ET", [128, NT * NH], F32)
    WSB = nc.alloc_sbuf_tensor("WSB", [128, 4 * NDC * 4], BF16)
    BIB = nc.alloc_sbuf_tensor("BIB", [NH, 1], F32)
    BFB = nc.alloc_sbuf_tensor("BFB", [NH, 1], F32)
    BSC = nc.alloc_sbuf_tensor("BSC", [128, 4096], BF16)
    WQKL = nc.alloc_sbuf_tensor("WQKL", [128, 1024], BF16)
    SCRAPF = nc.alloc_sbuf_tensor("SCRAPF", [128, 512], BF16)
    _SCRAPF[0] = SCRAPF
    MASKS = nc.alloc_sbuf_tensor("MASKS", [128, 2048], BF16)
    ONESR = nc.alloc_sbuf_tensor("ONESR", [1, 128], F32)
    CFB = [nc.alloc_sbuf_tensor("CFB0", [128, 1], F32),
           nc.alloc_sbuf_tensor("CFB1", [128, 1], F32)]
    CTA = nc.alloc_sbuf_tensor("CTA", [128, 1024], F32)
    SC4 = nc.alloc_sbuf_tensor("SC4", [128, 4], F32)
    MHQ = nc.alloc_sbuf_tensor("MHQ", [128, 4], F32)
    QQ4 = nc.alloc_sbuf_tensor("QQ4", [128, 4], F32)
    ABT = nc.alloc_sbuf_tensor("ABT", [128, 256], F32)
    ATST = nc.alloc_sbuf_tensor("ATST", [128, 1024], F32)
    ATS2 = CTA
    GAMB = nc.alloc_sbuf_tensor("GAMB", [128, 256], F32)
    SCRA = SCRAPF[:, 0:256]
    SCRB = SCRAPF[:, 256:512]

    if True:
        xb0 = FS[:, 0:1024]
        xb1 = FS[:, 1024:2048]
        sqb = FS[:, 2048:3072]
        with nc.Block() as blk:
            xbufs = [xb0, xb1]
            v0 = s_v.v
            in0 = s_in.v
            sxb = [sx[0].v, sx[1].v]

            @blk.sync
            def _(sync):
                s_in.dma(sync.dma_start(IDENT[:], consts_in[:, 0:128]))
                s_in.dma(sync.dma_start(IOTA[:], consts_in[:, 128:192]))
                s_in.dma(sync.dma_start(IOTAP[:], consts_in[:, 640:641],
                                        allow_slow_non_contiguous=True))
                s_in.dma(sync.dma_start(ONESC[:], consts_in[:, 641:642],
                                        allow_slow_non_contiguous=True))
                for i in range(NT):
                    if i >= 2:
                        sync.wait_ge(s_v.sem, v0 + (i - 1))
                    sx[i % 2].dma(sync.dma_start(xbufs[i % 2][:],
                                                 x_in[i * 128:(i + 1) * 128, :]))

            @blk.vector
            def _(v):
                for _j, _cv in enumerate(_cvals):
                    v.memset(_ctens[_j][:], _cv)
                for i in range(NT):
                    xb = xbufs[i % 2]
                    v.wait_ge(sx[i % 2].sem, sxb[i % 2] + 16 * (i // 2 + 1))
                    v.tensor_reduce(MU[:, i:i + 1], xb[:], AXX, ALU.add)
                    v.tensor_reduce(MXX[:, i:i + 1], xb[:], AXX, ALU.max)
                    v.tensor_reduce(MNN[:, i:i + 1], xb[:], AXX, ALU.min)
                    v.tensor_mul(sqb[:], xb[:], xb[:])
                    s_v.inc(v.tensor_reduce(MSQ[:, i:i + 1], sqb[:], AXX, ALU.add))

        with nc.Block() as blk:
            @blk.vector
            def _(v):
                v.tensor_scalar(MU[:], MU[:], 1.0 / D, None, ALU.mult)
                v.tensor_scalar(MSQ[:], MSQ[:], 1.0 / D, None, ALU.mult)
                vfence(v, s_v)
                v.tensor_mul(NU[:], MU[:], MU[:])
                vfence(v, s_v)
                v.tensor_sub(MSQ[:], MSQ[:], NU[:])

        with nc.Block() as blk:
            @blk.scalar
            def _(s):
                s.activation(RABS[:], MSQ[:], ACT.Ln, bias=1e-5)
                vfence(s, s_a)
                s_a.inc(s.activation(RINV[:], RABS[:], ACT.Exp, scale=-0.5))

            @blk.vector
            def _(v):
                s_a.wait(v)
                v.tensor_mul(NU[:], MU[:], RINV[:])
                v.tensor_sub(MXX[:], MXX[:], MU[:])
                v.tensor_sub(MNN[:], MU[:], MNN[:])
                vfence(v, s_v)
                v.tensor_scalar(NU[:], NU[:], -1.0, None, ALU.mult)
                v.tensor_max(MXX[:], MXX[:], MNN[:])
                vfence(v, s_v)
                v.tensor_mul(RABS[:], MXX[:], RINV[:])

    if "stats0" in debug:
        dbg["stats0"] = nc.dram_tensor("dbg_stats0", [128, 4 * NT], F32, kind="ExternalOutput")
        with nc.Block() as blk:
            @blk.gpsimd
            def _(g):
                s_g.dma(g.dma_start(dbg["stats0"][:, 0:NT], NU[:]))
                s_g.dma(g.dma_start(dbg["stats0"][:, NT:2 * NT], RINV[:]))
                s_g.dma(g.dma_start(dbg["stats0"][:, 2 * NT:3 * NT], MU[:]))
                s_g.dma(g.dma_start(dbg["stats0"][:, 3 * NT:4 * NT], MSQ[:]))
                g.wait_ge(s_g.sem, s_g.v)

    dtok = None
    if "xntok" in debug:
        dbg["xntok"] = nc.dram_tensor("dbg_xntok", [T, D], F32, kind="ExternalOutput")
        dtok = dbg["xntok"]
    with (
        nc.psum_tensor([128, 128], F32) as pb0,
        nc.psum_tensor([128, 128], F32) as pb1,
        nc.Block() as blk,
    ):
        ybufs = [FS[:, 0:1024], FS[:, 1024:2048]]
        xnbufs = [FS[:, 2048:3072], FS[:, 3072:4096]]
        pbanks = [pb0, pb1]
        a0 = s_a.v
        p0 = s_pe.v
        v0 = s_v.v
        in0 = s_in.v
        sxb = [sx[0].v, sx[1].v]
        av = {}
        lv = {}
        pos = v0
        order = []
        for i in range(min(2, NT)):
            order.append(("af", i, 0))
        for j in range(NT):
            for c in range(NDC):
                order.append(("lo", j, c))
            if j + 2 < NT:
                order.append(("af", j + 2, 0))
        for kind, aa, bb in order:
            pos += 1
            if kind == "af":
                av[aa] = pos
            else:
                lv[(aa, bb)] = pos

        @blk.sync
        def _(sync):
            for i in range(NT):
                if i >= 2:
                    sync.wait_ge(s_v.sem, av[i - 2])
                sx[i % 2].dma(sync.dma_start(ybufs[i % 2][:],
                                             x_in[i * 128:(i + 1) * 128, :]))

        @blk.vector
        def _(v):
            def emit_af(i):
                v.wait_ge(sx[i % 2].sem, sxb[i % 2] + 16 * (i // 2 + 1))
                if i >= 2:
                    v.wait_ge(s_pe.sem, p0 + 8 * (i - 1))
                s_v.inc(v.tensor_scalar(xnbufs[i % 2][:], ybufs[i % 2][:],
                                        RINV[:, i:i + 1], NU[:, i:i + 1],
                                        ALU.mult, ALU.add))

            def emit_lo(j, c):
                k = j * NDC + c
                v.wait_ge(s_a.sem, a0 + k + 1)
                col = c * T + j * 128
                s_v.inc(v.tensor_sub(XL[:, col:col + 128],
                                     pbanks[k % 2][:], XH[:, col:col + 128]))

            for kind, aa, bb in order:
                if kind == "af":
                    emit_af(aa)
                else:
                    emit_lo(aa, bb)

        @blk.tensor
        def _(t):
            for i in range(NT):
                for c in range(NDC):
                    k = i * NDC + c
                    w = av[i]
                    if k >= 2:
                        w = max(w, lv[(i, c)] - 9 if False else lv.get((i * NDC + c - 2) // NDC if False else 0, 0))
                    t.wait_ge(s_v.sem, w)
                    if k >= 2:
                        kk = k - 2
                        t.wait_ge(s_v.sem, lv[(kk // NDC, kk % NDC)])
                    s_pe.inc(t.transpose(pbanks[k % 2][:],
                                         xnbufs[i % 2][:, c * 128:(c + 1) * 128],
                                         IDENT[:]))

        @blk.scalar
        def _(s):
            for i in range(NT):
                for c in range(NDC):
                    k = i * NDC + c
                    s.wait_ge(s_pe.sem, p0 + k + 1)
                    col = c * T + i * 128
                    s_a.inc(s.copy(XH[:, col:col + 128], pbanks[k % 2][:]))

        if dtok is not None:
            @blk.gpsimd
            def _(g):
                for i in range(NT):
                    g.wait_ge(s_v.sem, av[i])
                    s_g.dma(g.dma_start(dtok[i * 128:(i + 1) * 128, :],
                                        xnbufs[i % 2][:]))
                g.wait_ge(s_g.sem, s_g.v)

    if "t16" in debug:
        dbg["t16"] = nc.dram_tensor("dbg_t16", [128, 3 * D + 4], F32, kind="ExternalOutput")
        with (
            nc.sbuf_tensor([128, D], F32) as tyb,
            nc.sbuf_tensor([128, D], F32) as tnb,
            nc.sbuf_tensor([128, D], F32) as tnb2,
            nc.Block() as blk,
        ):
            i16 = 16
            in0b = s_in.v

            @blk.sync
            def _(sync):
                s_in.dma(sync.dma_start(tyb[:], x_in[i16 * 128:(i16 + 1) * 128, :]))

            @blk.vector
            def _(v):
                v.wait_ge(s_in.sem, in0b + 16)
                v.tensor_scalar(tnb[:], tyb[:], RINV[:, i16:i16 + 1],
                                NU[:, i16:i16 + 1], ALU.mult, ALU.add)
                v.tensor_scalar(tnb2[:], tyb[:], RINV[:, i16:i16 + 1], None, ALU.mult)
                s_v.inc(v.tensor_scalar(tnb2[:], tnb2[:], NU[:, i16:i16 + 1], None, ALU.add))

            @blk.gpsimd
            def _(g):
                s_v.wait(g)
                s_g.dma(g.dma_start(dbg["t16"][:, 0:D], tyb[:]))
                s_g.dma(g.dma_start(dbg["t16"][:, D:2 * D], tnb[:]))
                s_g.dma(g.dma_start(dbg["t16"][:, 2 * D:3 * D], tnb2[:]))
                s_g.dma(g.dma_start(dbg["t16"][:, 3 * D:3 * D + 1], RINV[:, i16:i16 + 1],
                                    allow_slow_non_contiguous=True))
                s_g.dma(g.dma_start(dbg["t16"][:, 3 * D + 1:3 * D + 2], NU[:, i16:i16 + 1],
                                    allow_slow_non_contiguous=True))
                s_g.dma(g.dma_start(dbg["t16"][:, 3 * D + 2:3 * D + 3], MU[:, i16:i16 + 1],
                                    allow_slow_non_contiguous=True))
                g.wait_ge(s_g.sem, s_g.v)

    if upto >= 1:
        with (
            nc.psum_tensor([2, 128], F32) as ptr2,
            nc.psum_tensor([1, 8], F32) as ptr8,
        ):
            GTMP = FS[0:NC, 2112:3136]
            with nc.Block() as blk:
                @blk.vector
                def _(v):
                    v.tensor_reduce(P1A[:, 0:1], RABS[:], AXX, ALU.max)
            with nc.Block() as blk:
                @blk.vector
                def _(v):
                    v.tensor_scalar(MSQ[:], RABS[:], P1A[:, 0:1], None, ALU.is_ge)
            with nc.Block() as blk:
                @blk.vector
                def _(v):
                    v.tensor_mul(MSQ[:], MSQ[:], IOTA[:, 0:NT])
            with nc.Block() as blk:
                @blk.vector
                def _(v):
                    v.tensor_reduce(P1A[:, 1:2], MSQ[:], AXX, ALU.max)
            with nc.Block() as blk:
                @blk.vector
                def _(v):
                    v.tensor_scalar(P1A[:, 2:3], P1A[:, 1:2], 128.0, None, ALU.mult)
            with nc.Block() as blk:
                @blk.vector
                def _(v):
                    v.tensor_add(P1A[:, 1:2], P1A[:, 2:3], IOTAP[:])

            with nc.Block() as blk:
                @blk.tensor
                def _(t):
                    t.transpose(ptr2[:], P1A[:, 0:2], IDENT[:])

            with nc.Block() as blk:
                @blk.vector
                def _(v):
                    s_v.inc(v.tensor_copy(P1R[:], ptr2[:]))

            with nc.Block() as blk:
                @blk.sync
                def _(sync):
                    s_in.dma(sync.dma_start(P1B[:], P1R[1:2, :]))

                @blk.vector
                def _(v):
                    s_in.wait(v)
                    v.tensor_reduce(SCR[0:1, 0:1], P1R[0:1, :], AXX, ALU.max)
                    v.tensor_scalar(P1B[:], P1B[:], 1.0, None, ALU.add)
            with nc.Block() as blk:
                @blk.vector
                def _(v):
                    v.tensor_scalar(P1R[0:1, :], P1R[0:1, :], SCR[0:1, 0:1], None, ALU.is_ge)
            with nc.Block() as blk:
                @blk.vector
                def _(v):
                    v.tensor_mul(P1B[:], P1B[:], P1R[0:1, :])
            with nc.Block() as blk:
                @blk.vector
                def _(v):
                    v.tensor_reduce(SCR[0:1, 1:2], P1B[:], AXX, ALU.max)
            with nc.Block() as blk:
                @blk.vector
                def _(v):
                    v.tensor_scalar(SCR[0:1, 1:2], SCR[0:1, 1:2], -1.0, None, ALU.add)
            with nc.Block() as blk:
                @blk.vector
                def _(v):
                    v.tensor_copy(SCRI[0:1, 0:1], SCR[0:1, 1:2])
                    v.memset(AGIS[0:1, 1025:1056], 0.0)
                    v.tensor_copy(AGIS[0:1, 1024:1025], SCR[0:1, 1:2])

            with nc.Block() as blk:
                @blk.sync
                def _(sync):
                    with sync.register("p1t") as reg:
                        sync.reg_load(reg, SCRI[0:1, 0:1])
                        val = sync.snap(reg, min_val=0, max_val=T - 1)
                        s_in.dma(sync.dma_start(AGIS[0:1, 0:1024], x_in[bass.ds(val, 1), :]))

                @blk.gpsimd
                def _(g):
                    s_in.wait(g)
                    s_g.dma(g.dma_start(AG_in[:], AGIS[:]))
                    g.wait_ge(s_g.sem, s_g.v)
                    if _FAKE_CC[0]:
                        for b in range(NC):
                            s_g.dma(g.dma_start(AG_out[b:b + 1, :], AG_in[:]))
                    else:
                        s_cc.inc(g.collective_compute(
                            "AllGather", ALU.bypass,
                            replica_groups=[list(range(NC))],
                            ins=[AG_in.ap().opt()],
                            outs=[AG_out.ap().opt()]))
                        g.wait_ge(s_cc.sem, s_cc.v)
                    s_g.dma(g.dma_start(AGOS[:], AG_out[:]))
                    g.wait_ge(s_g.sem, s_g.v)

            with nc.Block() as blk:
                @blk.vector
                def _(v):
                    rows = NC
                    src_ap = AGOS[0:NC, 0:1024]
                    v.tensor_reduce(SCR[0:rows, 8:9], src_ap, AXX, ALU.add)
                    v.tensor_reduce(SCR[0:rows, 10:11], src_ap, AXX, ALU.max)
                    v.tensor_reduce(SCR[0:rows, 11:12], src_ap, AXX, ALU.min)
                    v.tensor_mul(GTMP[0:rows, 0:1024], src_ap, src_ap)
            with nc.Block() as blk:
                @blk.vector
                def _(v):
                    rows = NC
                    v.tensor_scalar(SCR[0:rows, 8:9], SCR[0:rows, 8:9], 1.0 / D,
                                    None, ALU.mult)
                    v.tensor_reduce(SCR[0:rows, 9:10], GTMP[0:rows, 0:1024], AXX, ALU.add)
            with nc.Block() as blk:
                @blk.vector
                def _(v):
                    rows = NC
                    v.tensor_scalar(SCR[0:rows, 9:10], SCR[0:rows, 9:10], 1.0 / D,
                                    None, ALU.mult)
                    v.tensor_mul(SCR[0:rows, 12:13], SCR[0:rows, 8:9], SCR[0:rows, 8:9])
            with nc.Block() as blk:
                @blk.vector
                def _(v):
                    rows = NC
                    v.tensor_sub(SCR[0:rows, 9:10], SCR[0:rows, 9:10], SCR[0:rows, 12:13])

            with nc.Block() as blk:
                @blk.scalar
                def _(s):
                    s.activation(SCR[0:NC, 12:13], SCR[0:NC, 9:10], ACT.Ln, bias=1e-5)
            with nc.Block() as blk:
                @blk.scalar
                def _(s):
                    s.activation(SCR[0:NC, 13:14], SCR[0:NC, 12:13], ACT.Exp, scale=-0.5)

            with nc.Block() as blk:
                @blk.vector
                def _(v):
                    v.tensor_sub(SCR[0:NC, 10:11], SCR[0:NC, 10:11], SCR[0:NC, 8:9])
                    v.tensor_sub(SCR[0:NC, 11:12], SCR[0:NC, 8:9], SCR[0:NC, 11:12])
            with nc.Block() as blk:
                @blk.vector
                def _(v):
                    v.tensor_max(SCR[0:NC, 10:11], SCR[0:NC, 10:11], SCR[0:NC, 11:12])
            with nc.Block() as blk:
                @blk.vector
                def _(v):
                    v.tensor_mul(SCR[0:NC, 14:15], SCR[0:NC, 10:11], SCR[0:NC, 13:14])

            with nc.Block() as blk:
                @blk.tensor
                def _(t):
                    t.transpose(ptr8[:], SCR[0:NC, 14:15], IDENT[0:NC, 0:NC])

            with nc.Block() as blk:
                @blk.vector
                def _(v):
                    v.tensor_reduce(SCR[0:1, 15:16], ptr8[0:1, :], AXX, ALU.max)
            with nc.Block() as blk:
                @blk.vector
                def _(v):
                    v.tensor_scalar(SCR[0:1, 16:17], SCR[0:1, 15:16], 1.0 / QMAX,
                                    1e-12, ALU.mult, ALU.max)

            with nc.Block() as blk:
                @blk.scalar
                def _(s):
                    s.activation(SCR[0:1, 17:18], SCR[0:1, 16:17], ACT.Ln)
            with nc.Block() as blk:
                @blk.scalar
                def _(s):
                    s.activation(SCR[0:1, 18:19], SCR[0:1, 17:18], ACT.Exp, scale=-1.0)

            with nc.Block() as blk:
                @blk.sync
                def _(sync):
                    s_in.dma(sync.dma_start(S_sm[0:1, 0:3], SCR[0:1, 16:19]))

                @blk.gpsimd
                def _(g):
                    s_in.wait(g)
                    s_g.dma(g.dma_start(B["s0"][:], S_sm[0:1, 0:1]
                                        .partition_broadcast(128).squeeze(1)))
                    s_g.dma(g.dma_start(B["invs0"][:], S_sm[0:1, 2:3]
                                        .partition_broadcast(128).squeeze(1)))
                    g.wait_ge(s_g.sem, s_g.v)


    if "scales" in debug:
        dbg["scales"] = nc.dram_tensor("dbg_scales", [1, 16], F32, kind="ExternalOutput")
        with nc.Block() as blk:
            @blk.vector
            def _(v):
                for j, n in enumerate(BNAMES):
                    v.tensor_copy(SC1[0:1, j:j + 1], B[n][0:1, 0:1])
                vfence(v, s_v)

            @blk.gpsimd
            def _(g):
                s_g.dma(g.dma_start(dbg["scales"][:], SC1[0:1, 0:16]))
                g.wait_ge(s_g.sem, s_g.v)

    s_sp = Sem(nc, "s_sp")
    s_lo = Sem(nc, "s_lo")
    if upto >= 2:
        with nc.Block() as blk:
            @blk.sync
            def _(sync):
                s_in.dma(sync.dma_start(FS[0:NH, 0:3 * D], wi_in[:]))
                s_in.dma(sync.dma_start(FS[0:NH, 3 * D:6 * D], wf_in[:]))
                s_in.dma(sync.dma_start(BIB[:], bi_in[:]))
                s_in.dma(sync.dma_start(BFB[:], bf_in[:]))

            @blk.vector
            def _(v):
                s_in.wait(v)
                v.tensor_add(FS[0:NH, 6 * D:7 * D], FS[0:NH, 0:D], FS[0:NH, D:2 * D])
                v.tensor_add(FS[0:NH, 7 * D:8 * D], FS[0:NH, 3 * D:4 * D],
                             FS[0:NH, 4 * D:5 * D])
                vfence(v, s_v)
                v.tensor_add(FS[0:NH, 6 * D:7 * D], FS[0:NH, 6 * D:7 * D],
                             FS[0:NH, 2 * D:3 * D])
                v.tensor_add(FS[0:NH, 7 * D:8 * D], FS[0:NH, 7 * D:8 * D],
                             FS[0:NH, 5 * D:6 * D])

        with (
            nc.psum_tensor([128, 8], F32) as pws,
            nc.Block() as blk,
        ):
            p0 = s_pe.v
            a0 = s_a.v
            v0 = s_v.v

            @blk.tensor
            def _(t):
                for c in range(NDC):
                    if c >= 1:
                        t.wait_ge(s_v.sem, v0 + c)
                    t.transpose(pws[:, 0:4], FS[0:NH, 6 * D + c * 128:6 * D + (c + 1) * 128],
                                IDENT[0:NH, 0:NH])
                    s_pe.inc(t.transpose(pws[:, 4:8],
                                         FS[0:NH, 7 * D + c * 128:7 * D + (c + 1) * 128],
                                         IDENT[0:NH, 0:NH]))

            @blk.scalar
            def _(s):
                for c in range(NDC):
                    s.wait_ge(s_pe.sem, p0 + c + 1)
                    s.copy(WSB[:, c * 4:(c + 1) * 4], pws[:, 0:4])
                    s_a.inc(s.copy(WSB[:, 32 + c * 4:32 + (c + 1) * 4], pws[:, 4:8]))

            @blk.vector
            def _(v):
                for c in range(NDC):
                    v.wait_ge(s_a.sem, a0 + c + 1)
                    v.tensor_sub(WSB[:, 64 + c * 4:64 + (c + 1) * 4], pws[:, 0:4],
                                 WSB[:, c * 4:(c + 1) * 4])
                    s_v.inc(v.tensor_sub(WSB[:, 96 + c * 4:96 + (c + 1) * 4], pws[:, 4:8],
                                         WSB[:, 32 + c * 4:32 + (c + 1) * 4]))

        with (
            nc.psum_tensor([128, 512], F32) as qp0,
            nc.psum_tensor([128, 512], F32) as qp1,
            nc.psum_tensor([128, 512], F32) as kp0,
            nc.psum_tensor([128, 512], F32) as kp1,
            nc.Block() as blk,
        ):
            qps = [qp0, qp1]
            kps = [kp0, kp1]
            in0 = s_in.v
            p0 = s_pe.v
            a0 = s_a.v
            g0 = s_g.v
            sp0 = s_sp.v
            lo0 = s_lo.v

            @blk.sync
            def _(sync):
                for c in range(NDC):
                    if c >= 1:
                        sync.wait_ge(s_sp.sem, sp0 + c)
                    s_in.dma(sync.dma_start(FS[:, 0:128], wqd_in[c]))
                    s_in.dma(sync.dma_start(FS[:, 128:256], wkd_in[c]))

            @blk.vector
            def _(v):
                for c in range(NDC):
                    wb = 512 * (c % 2)
                    v.wait_ge(s_in.sem, in0 + 32 * (c + 1))
                    if c >= 2:
                        v.wait_ge(s_pe.sem, p0 + NT5 * (c - 1))
                    v.tensor_copy(WQKL[:, wb:wb + 128], FS[:, 0:128])
                    v.tensor_copy(WQKL[:, wb + 256:wb + 384], FS[:, 128:256])
                    vfence(v, s_v)
                    v.tensor_sub(WQKL[:, wb + 128:wb + 256], FS[:, 0:128], WQKL[:, wb:wb + 128])
                    s_sp.inc(v.tensor_sub(WQKL[:, wb + 384:wb + 512], FS[:, 128:256],
                                          WQKL[:, wb + 256:wb + 384]))
                    for t_ in range(NT5):
                        k = c * NT5 + t_
                        st = 2048 * (k % 2)
                        v.wait_ge(s_a.sem, a0 + 2 * (k + 1))
                        v.scalar_tensor_tensor(BSC[:, st + 512:st + 1024], qps[k % 2][:],
                                               1.0, BSC[:, st:st + 512],
                                               ALU.mult, ALU.subtract)
                        s_lo.inc(v.scalar_tensor_tensor(
                            BSC[:, st + 1536:st + 2048], kps[k % 2][:], 0.0625,
                            BSC[:, st + 1024:st + 1536], ALU.mult, ALU.subtract))

            @blk.tensor
            def _(t):
                for c in range(NDC):
                    for t_ in range(NT5):
                        k = c * NT5 + t_
                        if t_ == 0:
                            t.wait_ge(s_sp.sem, sp0 + c + 1)
                        if k >= 2:
                            t.wait_ge(s_lo.sem, lo0 + k - 1)
                        wb = 512 * (c % 2)
                        xh = XH[:, c * T + t_ * 512:c * T + (t_ + 1) * 512]
                        xl = XL[:, c * T + t_ * 512:c * T + (t_ + 1) * 512]
                        t.matmul(qps[k % 2][:], WQKL[:, wb:wb + 128], xh, start=True, stop=False)
                        t.matmul(qps[k % 2][:], WQKL[:, wb:wb + 128], xl, start=False, stop=False)
                        t.matmul(qps[k % 2][:], WQKL[:, wb + 128:wb + 256], xh, start=False, stop=True)
                        t.matmul(kps[k % 2][:], WQKL[:, wb + 256:wb + 384], xh, start=True, stop=False)
                        t.matmul(kps[k % 2][:], WQKL[:, wb + 256:wb + 384], xl, start=False, stop=False)
                        s_pe.inc(t.matmul(kps[k % 2][:], WQKL[:, wb + 384:wb + 512], xh,
                                          start=False, stop=True))

            sg2b = [sg2[0].v, sg2[1].v]

            @blk.scalar
            def _(s):
                for c in range(NDC):
                    for t_ in range(NT5):
                        k = c * NT5 + t_
                        st = 2048 * (k % 2)
                        s.wait_ge(s_pe.sem, p0 + k + 1)
                        if k >= 2:
                            s.wait_ge(sg2[k % 2].sem,
                                      sg2b[k % 2] + 64 * ((k - 2) // 2 + 1))
                        s.copy(BSC[:, st:st + 512], qps[k % 2][:])
                        s_a.inc(s.activation(BSC[:, st + 1024:st + 1536], kps[k % 2][:],
                                             ACT.Identity, scale=0.0625), 2)

            @blk.gpsimd
            def _(g):
                for c in range(NDC):
                    for t_ in range(NT5):
                        k = c * NT5 + t_
                        st = 2048 * (k % 2)
                        g.wait_ge(s_lo.sem, lo0 + k + 1)
                        rows = slice(c * 128, (c + 1) * 128)
                        cols = slice(t_ * 512, (t_ + 1) * 512)
                        sg2[k % 2].dma(g.dma_start(S_qh[rows, cols], BSC[:, st:st + 512]))
                        sg2[k % 2].dma(g.dma_start(S_ql[rows, cols], BSC[:, st + 512:st + 1024]))
                        sg2[k % 2].dma(g.dma_start(S_kh[rows, cols], BSC[:, st + 1024:st + 1536]))
                        sg2[k % 2].dma(g.dma_start(S_kl[rows, cols], BSC[:, st + 1536:st + 2048]))

        with (
            nc.psum_tensor([128, 128], F32) as vp0,
            nc.psum_tensor([128, 128], F32) as vp1,
            nc.Block() as blk,
        ):
            vps = [vp0, vp1]
            in0 = s_in.v
            p0 = s_pe.v
            a0 = s_a.v
            g0 = s_g.v
            sp0 = s_sp.v

            @blk.sync
            def _(sync):
                for c in range(NDC):
                    if c >= 1:
                        sync.wait_ge(s_sp.sem, sp0 + c)
                    s_in.dma(sync.dma_start(FS[:, 0:128], wvd_in[c]))

            @blk.vector
            def _(v):
                for c in range(NDC):
                    wb = 512 * (c % 2)
                    v.wait_ge(s_in.sem, in0 + 16 * (c + 1))
                    if c >= 2:
                        v.wait_ge(s_pe.sem, p0 + NT * (c - 1))
                    v.tensor_copy(WQKL[:, wb:wb + 128], FS[:, 0:128])
                    vfence(v, s_v)
                    s_sp.inc(v.tensor_sub(WQKL[:, wb + 128:wb + 256], FS[:, 0:128],
                                          WQKL[:, wb:wb + 128]))

            @blk.tensor
            def _(t):
                for c in range(NDC):
                    for ts_ in range(NT):
                        k = c * NT + ts_
                        if ts_ == 0:
                            t.wait_ge(s_sp.sem, sp0 + c + 1)
                        if k >= 2:
                            t.wait_ge(s_a.sem, a0 + k - 1)
                        wb = 512 * (c % 2)
                        xh = XH[:, c * T + ts_ * 128:c * T + (ts_ + 1) * 128]
                        xl = XL[:, c * T + ts_ * 128:c * T + (ts_ + 1) * 128]
                        t.matmul(vps[k % 2][:], xh, WQKL[:, wb:wb + 128], start=True, stop=False)
                        t.matmul(vps[k % 2][:], xh, WQKL[:, wb + 128:wb + 256], start=False, stop=False)
                        s_pe.inc(t.matmul(vps[k % 2][:], xl, WQKL[:, wb:wb + 128],
                                          start=False, stop=True))

            sg2b = [sg2[0].v, sg2[1].v]

            @blk.scalar
            def _(s):
                for c in range(NDC):
                    for ts_ in range(NT):
                        k = c * NT + ts_
                        st = 1024 + 128 * (k % 2)
                        s.wait_ge(s_pe.sem, p0 + k + 1)
                        if k >= 2:
                            s.wait_ge(sg2[k % 2].sem,
                                      sg2b[k % 2] + 16 * ((k - 2) // 2 + 1))
                        s_a.inc(s.copy(FS[:, st:st + 128], vps[k % 2][:]))

            @blk.gpsimd
            def _(g):
                for c in range(NDC):
                    for ts_ in range(NT):
                        k = c * NT + ts_
                        st = 1024 + 128 * (k % 2)
                        g.wait_ge(s_a.sem, a0 + k + 1)
                        h = c // 2
                        half = c % 2
                        sg2[k % 2].dma(g.dma_start(
                            S_v[h, ts_ * 128:(ts_ + 1) * 128, half * 128:(half + 1) * 128],
                            FS[:, st:st + 128]))

        with (
            nc.psum_tensor([NH, 512], F32) as gi0,
            nc.psum_tensor([NH, 512], F32) as gf0,
            nc.psum_tensor([NH, 512], F32) as gi1,
            nc.psum_tensor([NH, 512], F32) as gf1,
            nc.Block() as blk,
        ):
            gis = [gi0, gi1]
            gfs = [gf0, gf1]
            p0 = s_pe.v
            a0 = s_a.v

            @blk.tensor
            def _(t):
                for t_ in range(NT5):
                    if t_ >= 2:
                        t.wait_ge(s_a.sem, a0 + t_ - 1)
                    for c in range(NDC):
                        xh = XH[:, c * T + t_ * 512:c * T + (t_ + 1) * 512]
                        xl = XL[:, c * T + t_ * 512:c * T + (t_ + 1) * 512]
                        st0 = (c == 0)
                        wih = WSB[:, c * 4:(c + 1) * 4]
                        wil = WSB[:, 64 + c * 4:64 + (c + 1) * 4]
                        wfh = WSB[:, 32 + c * 4:32 + (c + 1) * 4]
                        wfl = WSB[:, 96 + c * 4:96 + (c + 1) * 4]
                        t.matmul(gis[t_ % 2][:], wih, xh, start=st0, stop=False)
                        t.matmul(gis[t_ % 2][:], wih, xl, start=False, stop=False)
                        t.matmul(gis[t_ % 2][:], wil, xh, start=False,
                                 stop=(c == NDC - 1))
                        t.matmul(gfs[t_ % 2][:], wfh, xh, start=st0, stop=False)
                        t.matmul(gfs[t_ % 2][:], wfh, xl, start=False, stop=False)
                        last = t.matmul(gfs[t_ % 2][:], wfl, xh, start=False,
                                        stop=(c == NDC - 1))
                    s_pe.inc(last)

            @blk.scalar
            def _(s):
                for t_ in range(NT5):
                    s.wait_ge(s_pe.sem, p0 + t_ + 1)
                    s.activation(GV[:, t_ * 512:(t_ + 1) * 512], gis[t_ % 2][:],
                                 ACT.Identity, bias=BIB[0:NH, 0:1])
                    s_a.inc(s.activation(GF[0:NH, t_ * 512:(t_ + 1) * 512], gfs[t_ % 2][:],
                                         ACT.Identity, bias=BFB[0:NH, 0:1]))

        with nc.Block() as blk:
            @blk.scalar
            def _(s):
                s.activation(FS[0:NH, 0:T], GF[0:NH, :], ACT.Exp, scale=-1.0)
                vfence(s, s_a)
                s.activation(GF[0:NH, :], FS[0:NH, 0:T], ACT.Ln, bias=1.0)

        FS0 = FS[0:NH, 0:T]
        with nc.Block() as blk:
            @blk.vector
            def _(v):
                v.tensor_tensor_scan(FS0, GF[0:NH, :], GF[0:NH, :], 0.0,
                                     ALU.add, ALU.bypass)
                vfence(v, s_v)
                v.tensor_add(GV[:, :], GV[:, :], FS0)
                vfence(v, s_v)
                v.tensor_tensor_scan(GF[0:NH, :], GV[:, :], GV[:, :], -1e30,
                                     ALU.max, ALU.bypass)
                vfence(v, s_v)
                v.tensor_sub(FS0, FS0, GF[0:NH, :])

        with (
            nc.psum_tensor([128, 8], F32) as pgt,
            nc.Block() as blk,
        ):
            p0 = s_pe.v
            a0 = s_a.v

            @blk.tensor
            def _(t):
                for j in range(NT):
                    if j >= 1:
                        t.wait_ge(s_a.sem, a0 + j)
                    t.transpose(pgt[:, 0:4], GV[:, j * 128:(j + 1) * 128], IDENT[0:NH, 0:NH])
                    s_pe.inc(t.transpose(pgt[:, 4:8], FS0[:, j * 128:(j + 1) * 128],
                                         IDENT[0:NH, 0:NH]))

            @blk.scalar
            def _(s):
                for j in range(NT):
                    s.wait_ge(s_pe.sem, p0 + j + 1)
                    s.copy(CT[:, j * 4:(j + 1) * 4], pgt[:, 0:4])
                    s_a.inc(s.copy(ET[:, j * 4:(j + 1) * 4], pgt[:, 4:8]))

        with nc.Block() as blk:
            @blk.scalar
            def _(s):
                s.activation(ET[:], ET[:], ACT.Exp)

            @blk.gpsimd
            def _(g):
                s_g.dma(g.dma_start(
                    S_sm[8:24, :].rearrange("(a b) c -> a (b c)", b=4), GF[0:NH, :]))
                g.wait_ge(s_g.sem, s_g.v)

    if upto >= 2 and ("q" in debug or "k" in debug or "v" in debug or "gates" in debug):
        with nc.Block() as blk:
            @blk.gpsimd
            def _(g):
                g.wait_ge(s_g.sem, s_g.v)

        for nm, shi, slo in (("q", S_qh, S_ql), ("k", S_kh, S_kl)):
            if nm not in debug:
                continue
            if nm not in dbg:
                dbg[nm] = nc.dram_tensor("dbg_" + nm, [D, T], F32, kind="ExternalOutput")
            dd = dbg[nm]
            for c in range(NDC):
                with nc.Block() as blk:
                    @blk.sync
                    def _(sync, c=c, shi=shi, slo=slo):
                        s_in.dma(sync.dma_start(BSC[:, 0:4096], shi[c * 128:(c + 1) * 128, :]))

                    @blk.vector
                    def _(v, c=c):
                        s_in.wait(v)
                        v.tensor_copy(FS[:, 0:4096], BSC[:, 0:4096])
                with nc.Block() as blk:
                    @blk.sync
                    def _(sync, c=c, shi=shi, slo=slo):
                        s_in.dma(sync.dma_start(BSC[:, 0:4096], slo[c * 128:(c + 1) * 128, :]))

                    @blk.vector
                    def _(v, c=c):
                        s_in.wait(v)
                        v.tensor_add(FS[:, 0:4096], FS[:, 0:4096], BSC[:, 0:4096])
                with nc.Block() as blk:
                    @blk.gpsimd
                    def _(g, c=c, dd=dd):
                        s_g.dma(g.dma_start(dd[c * 128:(c + 1) * 128, :], FS[:, 0:4096]))
                        g.wait_ge(s_g.sem, s_g.v)

        if "v" in debug:
            dbg["v"] = nc.dram_tensor("dbg_v", [NH * T, DH], F32, kind="ExternalOutput")
            for h in range(NH):
                for j in range(8):
                    with nc.Block() as blk:
                        @blk.sync
                        def _(sync, h=h, j=j):
                            s_in.dma(sync.dma_start(
                                FS[:, 0:1024].rearrange("p (a b) -> p a b", a=4, b=256),
                                S_v[h, j * 512:(j + 1) * 512, :]
                                .rearrange("(a p) b -> p a b", p=128)))

                        @blk.gpsimd
                        def _(g, h=h, j=j):
                            s_in.wait(g)
                            s_g.dma(g.dma_start(
                                dbg["v"][h * T + j * 512:h * T + (j + 1) * 512, :]
                                .rearrange("(a p) b -> p a b", p=128),
                                FS[:, 0:1024].rearrange("p (a b) -> p a b", a=4, b=256)))
                            g.wait_ge(s_g.sem, s_g.v)

        if "gates" in debug:
            dbg["gates"] = nc.dram_tensor("dbg_gates", [NH, 3 * T], F32, kind="ExternalOutput")
            with nc.Block() as blk:
                @blk.gpsimd
                def _(g):
                    s_g.dma(g.dma_start(dbg["gates"][:, 0:T], GV[:, :]))
                    s_g.dma(g.dma_start(dbg["gates"][:, T:2 * T], GF[0:NH, :]))
                    s_g.dma(g.dma_start(dbg["gates"][:, 2 * T:3 * T], FS[0:NH, 0:T]))
                    g.wait_ge(s_g.sem, s_g.v)

    if upto >= 3:
        with nc.Block() as blk:
            @blk.sync
            def _(sync):
                s_in.dma(sync.dma_start(FS[0:1, 0:D], lnw_in[:]))
                for j in range(4):
                    s_in.dma(sync.dma_start(MASKS[:, j * 512:(j + 1) * 512], masks_in[j]))

            @blk.vector
            def _(v):
                s_in.wait(v)
                v.memset(ONESR[:], 1.0)
                v.memset(ATST[:, 0:512], 0.0)
                v.tensor_scalar(FS[0:1, D:2 * D], FS[0:1, 0:D], 1.0, None, ALU.add)

        with nc.Block() as blk:
            @blk.gpsimd
            def _(g):
                s_g.dma(g.dma_start(S_sm[1:2, 0:D], FS[0:1, D:2 * D]))
                s_g.dma(g.dma_start(S_st[:, :], ATST[:, 0:512]))
                g.wait_ge(s_g.sem, s_g.v)

        for h in range(_LIM[0]):
            with nc.Block() as blk:
                @blk.sync
                def _(sync, h=h):
                    for dc in range(2):
                        r0 = h * DH + dc * 128
                        s_in.dma(sync.dma_start(XH[:, dc * T:(dc + 1) * T],
                                                S_qh[r0:r0 + 128, :]))
                        s_in.dma(sync.dma_start(XH[:, 2 * T + dc * T:2 * T + (dc + 1) * T],
                                                S_ql[r0:r0 + 128, :]))
                        s_in.dma(sync.dma_start(XL[:, dc * T:(dc + 1) * T],
                                                S_kh[r0:r0 + 128, :]))
                        s_in.dma(sync.dma_start(XL[:, 2 * T + dc * T:2 * T + (dc + 1) * T],
                                                S_kl[r0:r0 + 128, :]))
                @blk.gpsimd
                def _(g, h=h):
                    for ts_ in range(NT):
                        s_g.dma(g.dma_start(FS[:, ts_ * 257:ts_ * 257 + 256],
                                            S_v[h, ts_ * 128:(ts_ + 1) * 128, :]))

                @blk.vector
                def _(v, h=h):
                    s_in.wait(v)
                    s_g.wait(v)
                    v.memset(FS[:, 256:8224:257], 1.0)

            def qh_(dc, a, b):
                return XH[:, dc * T + a:dc * T + b]

            def ql_(dc, a, b):
                return XH[:, 2 * T + dc * T + a:2 * T + dc * T + b]

            def kh_(dc, a, b):
                return XL[:, dc * T + a:dc * T + b]

            def kl_(dc, a, b):
                return XL[:, 2 * T + dc * T + a:2 * T + dc * T + b]

            with (
                nc.psum_tensor([128, 512], F32) as qkp0,
                nc.psum_tensor([128, 512], F32) as qkp1,
                nc.psum_tensor([128, 512], F32) as o2a,
                nc.psum_tensor([128, 512], F32) as o2b,
                nc.psum_tensor([128, 512], F32) as o2c,
                nc.psum_tensor([128, 512], F32) as o2d,
                nc.Block() as blk,
            ):
                qkps = [qkp0, qkp1]
                o2 = [o2a, o2b, o2c, o2d]
                P = {}
                pp, ap, vp, gp = s_pe.v, s_a.v, s_v.v, s_g.v
                ip = s_em.v
                for sc in range(_LIM[1]):
                    ntt = 4 * sc + 4
                    ip += 16; P[("mbc", sc)] = ip
                    for tt in range(ntt):
                        pp += 1; P[("qk", sc, tt)] = pp
                        ap += 1; P[("w", sc, tt)] = ap
                        vp += 1; P[("ct", sc, tt)] = vp
                        pp += 1; P[("cv", sc, tt)] = pp
                    vp += 1; P[("v_sc4", sc)] = vp
                    vp += 1; P[("v_mh", sc)] = vp
                    ap += 1; P[("a_sq", sc)] = ap
                    vp += 1; P[("v_q", sc)] = vp
                    gp += 64; P[("st", sc)] = gp

                @blk.sync
                def _(sync, h=h, P=P):
                    for sc in range(_LIM[1]):
                        row = 8 + 4 * h + sc // 2
                        cs = 512 * (sc % 2)
                        if sc >= 1:
                            sync.wait_ge(s_g.sem, P[("st", sc - 1)])
                            sync.wait_ge(s_v.sem, P[("ct", sc - 1, 4 * sc - 1)])
                        s_em.dma(sync.dma_start(
                            ATST[:, 0:512], S_sm[row:row + 1, cs:cs + 512]
                            .partition_broadcast(128).squeeze(1)))

                @blk.tensor
                def _(t, h=h, P=P):
                    for sc in range(_LIM[1]):
                        ntt = 4 * sc + 4
                        for tt in range(ntt):
                            qb = qkps[tt % 2]
                            prev = (sc, tt - 2) if tt >= 2 else (
                                (sc - 1, 4 * sc - 2 + tt) if sc >= 1 else None)
                            if prev is not None and "ct" not in _DIS:
                                t.wait_ge(s_v.sem, P[("ct",) + prev])
                            a_ = tt * 128
                            last = None
                            if "qk" not in _DIS:
                                for dc in range(2):
                                    t.matmul(qb[:], kh_(dc, a_, a_ + 128),
                                             qh_(dc, sc * 512, (sc + 1) * 512),
                                             start=(dc == 0), stop=False)
                                    t.matmul(qb[:], kh_(dc, a_, a_ + 128),
                                             ql_(dc, sc * 512, (sc + 1) * 512),
                                             start=False, stop=False)
                                    last = t.matmul(qb[:], kl_(dc, a_, a_ + 128),
                                                    qh_(dc, sc * 512, (sc + 1) * 512),
                                                    start=False, stop=(dc == 1))
                                s_pe.inc(last)
                            else:
                                s_pe.inc(t.nop())
                            if "ct" not in _DIS:
                                t.wait_ge(s_v.sem, P[("ct", sc, tt)])
                            if tt == 0 and sc >= 1 and "ep" not in _DIS:
                                t.wait_ge(s_v.sem, P[("v_mh", sc - 1)])
                            ctb = CTA[:, 512 * (tt % 2):512 * (tt % 2) + 512]
                            if "cv" not in _DIS:
                                for ss in range(4):
                                    last = t.matmul(o2[ss][:, 0:257], ctb[:, ss * 128:(ss + 1) * 128],
                                                    FS[:, tt * 257:(tt + 1) * 257],
                                                    start=(tt == 0), stop=(tt == ntt - 1))
                                s_pe.inc(last)
                            else:
                                s_pe.inc(t.nop())

                @blk.scalar
                def _(s, h=h, P=P):
                    for sc in range(_LIM[1]):
                        ntt = 4 * sc + 4
                        s.wait_ge(s_em.sem, P[("mbc", sc)])
                        for tt in range(ntt):
                            prev2 = (sc, tt - 2) if tt >= 2 else (
                                (sc - 1, 4 * sc - 2 + tt) if sc >= 1 else None)
                            if prev2 is not None and "cv" not in _DIS:
                                s.wait_ge(s_pe.sem, P[("cv",) + prev2])
                            if sc >= 1 and tt < 2 and "ep" not in _DIS:
                                s.wait_ge(s_v.sem, P[("v_q", sc - 1)])
                            if tt == 0:
                                s.copy(CFB[0][:], CT[:, h:h + 1])
                            if tt + 1 < ntt:
                                c1 = h + 4 * (tt + 1)
                                s.copy(CFB[(tt + 1) % 2][:], CT[:, c1:c1 + 1])
                            vfence(s)
                            ctb = CTA[:, 512 * (tt % 2):512 * (tt % 2) + 512]
                            s_a.inc(s.activation(ctb, ATST[:, 0:512], ACT.Exp,
                                                 scale=-1.0,
                                                 bias=CFB[tt % 2][:, 0:1]))
                        if "ep" in _DIS:
                            s_a.inc(s.nop())
                            continue
                        s.wait_ge(s_v.sem, P[("v_mh", sc)])
                        s_a.inc(s.activation(ATS2[:], ATST[:], ACT.Square))


                @blk.vector
                def _(v, h=h, P=P):
                    for sc in range(_LIM[1]):
                        ntt = 4 * sc + 4
                        for tt in range(ntt):
                            qb = qkps[tt % 2]
                            if "ct" in _DIS:
                                s_v.inc(v.nop())
                                continue
                            v.wait_ge(s_pe.sem, P[("qk", sc, tt)])
                            v.wait_ge(s_a.sem, P[("w", sc, tt)])
                            ctb = CTA[:, 512 * (tt % 2):512 * (tt % 2) + 512]
                            instr = v.tensor_mul(ctb, ctb, qb[:])
                            if tt >= 4 * sc and "msk" not in _DIS:
                                j = tt - 4 * sc
                                vfence(v)
                                instr = v.tensor_mul(ctb, ctb,
                                                     MASKS[:, j * 512:(j + 1) * 512])
                            s_v.inc(instr)
                        if "ep" in _DIS:
                            for _k in range(3):
                                s_v.inc(v.nop())
                            continue
                        if "cv" not in _DIS:
                            v.wait_ge(s_pe.sem, P[("cv", sc, ntt - 1)])
                        if sc >= 1:
                            v.wait_ge(s_g.sem, P[("st", sc - 1)])
                        for ss in range(4):
                            ins = v.tensor_reduce(SC4[:, ss:ss + 1], o2[ss][:, 256:257],
                                                  AXX, ALU.max, apply_absolute_value=True)
                        s_v.inc(ins)
                        for ss in range(4):
                            v.tensor_copy(ATST[:, ss * 256:(ss + 1) * 256],
                                          o2[ss][:, 0:256])
                        for ss in range(4):
                            ins = v.tensor_reduce(MHQ[:, ss:ss + 1],
                                                  ATST[:, ss * 256:(ss + 1) * 256],
                                                  AXX, ALU.add)
                        s_v.inc(ins)
                        v.wait_ge(s_a.sem, P[("a_sq", sc)])
                        for ss in range(4):
                            ins = v.tensor_reduce(QQ4[:, ss:ss + 1],
                                                  ATS2[:, ss * 256:(ss + 1) * 256],
                                                  AXX, ALU.add)
                        s_v.inc(ins)

                @blk.gpsimd
                def _(g, h=h, P=P):
                    for sc in range(_LIM[1]):
                        if "ep" in _DIS:
                            s_g.v += 64
                            g.sem_inc(s_g.sem, 64)
                            continue
                        co = h * 32 + sc * 4
                        g.wait_ge(s_v.sem, P[("v_sc4", sc)])
                        s_g.dma(g.dma_start(S_st[:, co:co + 4], SC4[:]))
                        g.wait_ge(s_v.sem, P[("v_mh", sc)])
                        s_g.dma(g.dma_start(S_st[:, 128 + co:128 + co + 4], MHQ[:]))
                        s_g.dma(g.dma_start(
                            S_attn[h, sc * 512:(sc + 1) * 512, :]
                            .rearrange("(a p) b -> p a b", p=128),
                            ATST[:].rearrange("p (a b) -> p a b", a=4, b=256)))
                        g.wait_ge(s_v.sem, P[("v_q", sc)])
                        s_g.dma(g.dma_start(S_st[:, 256 + co:256 + co + 4], QQ4[:]))

        STT = FS[:, 0:384]
        N1 = FS[:, 384:512]
        T1 = FS[:, 512:640]
        M2T = FS[:, 640:768]
        W1 = FS[:, 768:896]
        W3 = FS[:, 896:1024]
        W4 = FS[:, 1024:1152]
        WT = FS[:, 1152:1280]
        LT = FS[:, 1280:1408]
        RT = FS[:, 1408:1536]
        with nc.Block() as blk:
            @blk.sync
            def _(sync):
                s_in.dma(sync.dma_start(STT, S_st[:, 0:384]))

            @blk.vector
            def _(v):
                s_in.wait(v)
                for h in range(_LIM[0]):
                    v.tensor_max(N1[:, h * 32:h * 32 + 32],
                                 STT[:, h * 32:h * 32 + 32], ET[:, h:h + 125:4])
                v.tensor_mul(W1[:], STT[:, 128:256], STT[:, 128:256])
        with nc.Block() as blk:
            @blk.vector
            def _(v):
                v.tensor_scalar(T1[:], N1[:], 1e-8, None, ALU.add)
                v.tensor_scalar(W3[:], STT[:, 256:384], float(DH), None, ALU.mult)
        with nc.Block() as blk:
            @blk.vector
            def _(v):
                v.tensor_mul(M2T[:], T1[:], T1[:])
                v.tensor_sub(W4[:], W3[:], W1[:])
        with nc.Block() as blk:
            @blk.vector
            def _(v):
                v.tensor_scalar(M2T[:], M2T[:], 1e-5 * float(DH) * float(DH),
                                None, ALU.mult)
        with nc.Block() as blk:
            @blk.vector
            def _(v):
                v.tensor_add(WT[:], W4[:], M2T[:])
        with nc.Block() as blk:
            @blk.scalar
            def _(s):
                s.activation(LT[:], WT[:], ACT.Ln)
        with nc.Block() as blk:
            @blk.scalar
            def _(s):
                s.activation(RT[:], LT[:], ACT.Exp, scale=-0.5)
        with nc.Block() as blk:
            @blk.vector
            def _(v):
                v.tensor_scalar(ABT[:, 0:128], RT[:], float(DH), None, ALU.mult)
                v.tensor_mul(ABT[:, 128:256], STT[:, 128:256], RT[:])
        with nc.Block() as blk:
            @blk.vector
            def _(v):
                v.tensor_scalar(ABT[:, 128:256], ABT[:, 128:256], -1.0, None,
                                ALU.mult)

    if "ep0" in debug:
        dbg["ep0"] = nc.dram_tensor("dbg_ep0", [128, 1024 + 64], F32, kind="ExternalOutput")
        dbg["ctd"] = nc.dram_tensor("dbg_ctd", [128, 1024 + 512 + 32], F32, kind="ExternalOutput")
        dbg["qkl"] = nc.dram_tensor("dbg_qkl", [128, 4 * 512], BF16, kind="ExternalOutput")
        with nc.Block() as blk:
            @blk.gpsimd
            def _(g):
                s_g.dma(g.dma_start(dbg["qkl"][:, 0:512], XH[:, 0:512]))
                s_g.dma(g.dma_start(dbg["qkl"][:, 512:1024], XH[:, 2 * T:2 * T + 512]))
                s_g.dma(g.dma_start(dbg["qkl"][:, 1024:1536], XL[:, 0:512]))
                s_g.dma(g.dma_start(dbg["qkl"][:, 1536:2048], XL[:, 2 * T:2 * T + 512]))
                s_g.dma(g.dma_start(dbg["ctd"][:, 0:1024], CTA[:]))
                s_g.dma(g.dma_start(dbg["ctd"][:, 1024:1280], ABT[:]))
                s_g.dma(g.dma_start(dbg["ctd"][:, 1280:1408], WT))
                s_g.dma(g.dma_start(dbg["ctd"][:, 1408:1536], RT))
                s_g.dma(g.dma_start(dbg["ep0"][:, 0:1024], ATST[:]))
                s_g.dma(g.dma_start(dbg["ep0"][:, 1024:1028], SC4[:]))
                s_g.dma(g.dma_start(dbg["ep0"][:, 1028:1032], MHQ[:]))
                s_g.dma(g.dma_start(dbg["ep0"][:, 1032:1036], QQ4[:]))
                g.wait_ge(s_g.sem, s_g.v)

    if upto >= 3 and "attnln" in debug:
        dbg["attnln"] = nc.dram_tensor("dbg_attnln", [NH * T, DH], F32, kind="ExternalOutput")
        for h in range(NH):
            for j in range(8):
                with nc.Block() as blk:
                    @blk.sync
                    def _(sync, h=h, j=j):
                        s_in.dma(sync.dma_start(
                            FS[:, 0:1024].rearrange("p (a b) -> p a b", a=4, b=256),
                            S_attn[h, j * 512:(j + 1) * 512, :]
                            .rearrange("(a p) b -> p a b", p=128)))

                    @blk.gpsimd
                    def _(g, h=h, j=j):
                        s_in.wait(g)
                        s_g.dma(g.dma_start(
                            dbg["attnln"][h * T + j * 512:h * T + (j + 1) * 512, :]
                            .rearrange("(a p) b -> p a b", p=128),
                            FS[:, 0:1024].rearrange("p (a b) -> p a b", a=4, b=256)))
                        g.wait_ge(s_g.sem, s_g.v)

    if "xnlimbs" in debug:
        dbg["xnh"] = nc.dram_tensor("dbg_xnh", [D, T], F32, kind="ExternalOutput")
        dbg["xnl"] = nc.dram_tensor("dbg_xnl", [D, T], F32, kind="ExternalOutput")
        dh_, dl_ = dbg["xnh"], dbg["xnl"]
        with (
            nc.sbuf_tensor([128, T], F32) as rh,
            nc.sbuf_tensor([128, T], F32) as rl,
            nc.Block() as blk,
        ):
            v0 = s_v.v
            g0 = s_g.v

            @blk.vector
            def _(v):
                for c in range(NDC):
                    if c >= 1:
                        v.wait_ge(s_g.sem, g0 + 32 * c)
                    v.tensor_copy(rh[:], XH[:, c * T:(c + 1) * T])
                    s_v.inc(v.tensor_copy(rl[:], XL[:, c * T:(c + 1) * T]))

            @blk.gpsimd
            def _(g):
                for c in range(NDC):
                    g.wait_ge(s_v.sem, v0 + c + 1)
                    s_g.dma(g.dma_start(dh_[c * 128:(c + 1) * 128, :], rh[:]))
                    s_g.dma(g.dma_start(dl_[c * 128:(c + 1) * 128, :], rl[:]))
                g.wait_ge(s_g.sem, s_g.v)

    if "stats" in debug:
        dbg["stats"] = nc.dram_tensor("dbg_stats", [128, 4 * NT], F32, kind="ExternalOutput")
        with nc.Block() as blk:
            @blk.gpsimd
            def _(g):
                s_g.dma(g.dma_start(dbg["stats"][:, 0:NT], MU[:]))
                s_g.dma(g.dma_start(dbg["stats"][:, NT:2 * NT], RINV[:]))
                s_g.dma(g.dma_start(dbg["stats"][:, 2 * NT:3 * NT], NU[:]))
                s_g.dma(g.dma_start(dbg["stats"][:, 3 * NT:4 * NT], RABS[:]))
                g.wait_ge(s_g.sem, s_g.v)

    if "xn" in debug:
        dxn = dbg_out("xn", [D, T])
        with (
            nc.sbuf_tensor([128, T], F32) as rec0,
            nc.sbuf_tensor([128, T], F32) as rec1,
            nc.Block() as blk,
        ):
            recs = [rec0, rec1]
            v0 = s_v.v
            g0 = s_g.v

            @blk.vector
            def _(v):
                for c in range(NDC):
                    if c >= 2:
                        v.wait_ge(s_g.sem, g0 + 16 * (c - 1))
                    s_v.inc(v.tensor_add(recs[c % 2][:], XH[:, c * T:(c + 1) * T],
                                         XL[:, c * T:(c + 1) * T]))

            @blk.gpsimd
            def _(g):
                for c in range(NDC):
                    g.wait_ge(s_v.sem, v0 + c + 1)
                    s_g.dma(g.dma_start(dxn[c * 128:(c + 1) * 128, :], recs[c % 2][:]))
                g.wait_ge(s_g.sem, s_g.v)


    if upto >= 4:
        s_in = Sem(nc, "f_in"); s_pe = Sem(nc, "f_pe"); s_a = Sem(nc, "f_a")
        s_v = Sem(nc, "f_v"); s_g = Sem(nc, "f_g"); s_em = Sem(nc, "f_em")
        sg2 = [Sem(nc, "f_g2a"), Sem(nc, "f_g2b")]
        sx = [Sem(nc, "f_xa"), Sem(nc, "f_xb")]
        RXI0 = nc.alloc_sbuf_tensor("RXI0", [128, 128], I32)
        RXI1 = nc.alloc_sbuf_tensor("RXI1", [128, 128], I32)
        S_wut = nc.dram_tensor("S_wut", [D, 2 * II], BF16)
        S_wgt = nc.dram_tensor("S_wgt", [D, 2 * II], BF16)
        S_wdt = nc.dram_tensor("S_wdt", [II, 2 * D], BF16)
        UMX = CTA[:, 0:192]
        GMX = CTA[:, 192:384]
        PMX = CTA[:, 384:576]
        P3MX = CTA[:, 576:768]

        def emit_round(v, dst_t, dst0, src_t, src0, tmp_t, tmp0, ncols):
            for o in range(0, ncols, 128):
                sl = src_t[:, src0 + o:src0 + o + 128]
                v.tensor_copy(RXI0[:], sl)
                v.tensor_sub(tmp_t[:, tmp0:tmp0 + 128], sl, RXI0[:])
                v.tensor_copy(RXI1[:], tmp_t[:, tmp0:tmp0 + 128])
                last = v.tensor_add(dst_t[:, dst0 + o:dst0 + o + 128], RXI0[:], RXI1[:])
            return last

        tiles_f0 = []
        for ic in range(NIC):
            tiles_f0.append((wup_in, S_wut, II, ic * 128, 0))
            tiles_f0.append((wgate_in, S_wgt, II, ic * 128, 0))
        for dc in range(NDC):
            for scc in range(3):
                tiles_f0.append((wdown_in, S_wdt, D, dc * 128, scc * 1024))
        with (
            nc.psum_tensor([128, 128], F32) as dq0,
            nc.psum_tensor([128, 128], F32) as dq1,
        ):
            dqp = [dq0, dq1]
            for win, wdst, LOF, r0, c0 in (() if "f0" in _DIS else tiles_f0):
                with nc.Block() as blk:
                    @blk.sync
                    def _(sync, win=win, r0=r0, c0=c0):
                        s_in.dma(sync.dma_start(FS[:, 0:1024],
                                                win[r0:r0 + 128, c0:c0 + 1024]))

                    @blk.vector
                    def _(v):
                        s_in.wait(v)
                        v.tensor_reduce(FS[:, 1024:1056],
                                        FS[:, 0:1024].rearrange("p (g e) -> p g e", e=32),
                                        AXX, ALU.max, apply_absolute_value=True)
                with nc.Block() as blk:
                    @blk.vector
                    def _(v):
                        v.tensor_scalar(FS[:, 1088:1120], FS[:, 1024:1056],
                                        1.0 / 7.0, 1e-12, ALU.mult, ALU.max)
                with nc.Block() as blk:
                    @blk.scalar
                    def _(s):
                        s.activation(FS[:, 1056:1088], FS[:, 1088:1120], ACT.Ln)
                with nc.Block() as blk:
                    @blk.scalar
                    def _(s):
                        s.activation(FS[:, 1120:1152], FS[:, 1056:1088], ACT.Exp,
                                     scale=-1.0)
                with nc.Block() as blk:
                    @blk.vector
                    def _(v):
                        for e in range(32):
                            v.tensor_mul(FS[:, 2048 + e:2048 + e + 993:32],
                                         FS[:, e:e + 993:32], FS[:, 1120:1152])
                with nc.Block() as blk:
                    @blk.vector
                    def _(v):
                        emit_round(v, FS, 3072, FS, 2048, FS, 4096, 1024)
                with nc.Block() as blk:
                    @blk.vector
                    def _(v):
                        for e in range(32):
                            v.tensor_mul(FS[:, 4224 + e:4224 + e + 993:32],
                                         FS[:, 3072 + e:3072 + e + 993:32],
                                         FS[:, 1088:1120])
                with nc.Block() as blk:
                    p0 = s_pe.v; a0 = s_a.v; v0 = s_v.v
                    g2b = [sg2[0].v, sg2[1].v]

                    @blk.tensor
                    def _(t):
                        for j in range(8):
                            if j >= 2:
                                t.wait_ge(s_v.sem, v0 + j - 1)
                            s_pe.inc(t.transpose(dqp[j % 2][:],
                                                 FS[:, 4224 + j * 128:4224 + (j + 1) * 128],
                                                 IDENT[:]))

                    @blk.scalar
                    def _(s):
                        for j in range(8):
                            s.wait_ge(s_pe.sem, p0 + j + 1)
                            if j >= 2:
                                s.wait_ge(sg2[j % 2].sem,
                                          g2b[j % 2] + 32 * ((j - 2) // 2 + 1))
                            s_a.inc(s.copy(BSC[:, (j % 2) * 256:(j % 2) * 256 + 128],
                                           dqp[j % 2][:]))

                    @blk.vector
                    def _(v):
                        for j in range(8):
                            v.wait_ge(s_a.sem, a0 + j + 1)
                            s_v.inc(v.tensor_sub(
                                BSC[:, (j % 2) * 256 + 128:(j % 2) * 256 + 256],
                                dqp[j % 2][:],
                                BSC[:, (j % 2) * 256:(j % 2) * 256 + 128]))

                    @blk.gpsimd
                    def _(g, wdst=wdst, LOF=LOF, r0=r0, c0=c0):
                        for j in range(8):
                            g.wait_ge(s_v.sem, v0 + j + 1)
                            rr = c0 + j * 128
                            sg2[j % 2].dma(g.dma_start(
                                wdst[rr:rr + 128, r0:r0 + 128],
                                BSC[:, (j % 2) * 256:(j % 2) * 256 + 128]))
                            sg2[j % 2].dma(g.dma_start(
                                wdst[rr:rr + 128, LOF + r0:LOF + r0 + 128],
                                BSC[:, (j % 2) * 256 + 128:(j % 2) * 256 + 256]))

        NTF = 0 if "f1" in _DIS else NT
        with (
            nc.psum_tensor([128, 128], F32) as fb0,
            nc.psum_tensor([128, 128], F32) as fb1,
            nc.Block() as blk,
        ):
            fbk = [fb0, fb1]
            ybufs = [FS[:, 0:1024], FS[:, 1024:2048]]
            rbufs = [FS[:, 2048:3072], FS[:, 3072:4096]]
            a0 = s_a.v; p0 = s_pe.v; v0 = s_v.v
            sxb = [sx[0].v, sx[1].v]
            av = {}; lv = {}
            pos = v0
            order = []
            for i in range(min(2, NTF)):
                order.append(("af", i, 0))
            for j in range(NTF):
                for c in range(NDC):
                    order.append(("lo", j, c))
                if j + 2 < NTF:
                    order.append(("af", j + 2, 0))
            for kind, aa, bb in order:
                pos += 1
                if kind == "af":
                    av[aa] = pos
                else:
                    lv[(aa, bb)] = pos

            @blk.sync
            def _(sync):
                for i in range(NTF):
                    if i >= 2:
                        sync.wait_ge(s_v.sem, av[i - 2])
                    sx[i % 2].dma(sync.dma_start(ybufs[i % 2][:],
                                                 x_in[i * 128:(i + 1) * 128, :]))

            @blk.vector
            def _(v):
                def emit_af(i):
                    v.wait_ge(sx[i % 2].sem, sxb[i % 2] + 16 * (i // 2 + 1))
                    if i >= 2:
                        v.wait_ge(s_pe.sem, p0 + 8 * (i - 1))
                    v.tensor_scalar(ybufs[i % 2][:], ybufs[i % 2][:],
                                    RINV[:, i:i + 1], NU[:, i:i + 1],
                                    ALU.mult, ALU.add)
                    v.tensor_scalar(FS[:, 4096:5120], ybufs[i % 2][:],
                                    B["invs0"][:, 0:1], None, ALU.mult)
                    s_v.inc(emit_round(v, FS, 2048 + 1024 * (i % 2),
                                       FS, 4096, FS, 5120, 1024))

                def emit_lo(j, c):
                    k = j * NDC + c
                    v.wait_ge(s_a.sem, a0 + k + 1)
                    col = c * T + j * 128
                    s_v.inc(v.tensor_sub(XL[:, col:col + 128],
                                         fbk[k % 2][:], XH[:, col:col + 128]))

                for kind, aa, bb in order:
                    if kind == "af":
                        emit_af(aa)
                    else:
                        emit_lo(aa, bb)

            @blk.tensor
            def _(t):
                for i in range(NTF):
                    for c in range(NDC):
                        k = i * NDC + c
                        t.wait_ge(s_v.sem, av[i])
                        if k >= 2:
                            kk = k - 2
                            t.wait_ge(s_v.sem, lv[(kk // NDC, kk % NDC)])
                        s_pe.inc(t.transpose(
                            fbk[k % 2][:],
                            rbufs[i % 2][:, c * 128:(c + 1) * 128], IDENT[:]))

            @blk.scalar
            def _(s):
                for i in range(NTF):
                    for c in range(NDC):
                        k = i * NDC + c
                        s.wait_ge(s_pe.sem, p0 + k + 1)
                        col = c * T + i * 128
                        s_a.inc(s.copy(XH[:, col:col + 128], fbk[k % 2][:]))

        for wsrc, tdst, MX in (() if "f2" in _DIS else
                               ((S_wut, S_tup, UMX), (S_wgt, S_tg, GMX))):
            with (
                nc.psum_tensor([128, 512], F32) as up0,
                nc.psum_tensor([128, 512], F32) as up1,
                nc.psum_tensor([128, 512], F32) as up2,
                nc.psum_tensor([128, 512], F32) as up3,
                nc.psum_tensor([128, 512], F32) as up4,
                nc.psum_tensor([128, 512], F32) as up5,
                nc.psum_tensor([128, 512], F32) as up6,
                nc.psum_tensor([128, 512], F32) as up7,
                nc.Block() as blk,
            ):
                upb = [up0, up1, up2, up3, up4, up5, up6, up7]
                in0 = s_in.v; p0 = s_pe.v; v0 = s_v.v
                g2b = [sg2[0].v, sg2[1].v]

                @blk.sync
                def _(sync, wsrc=wsrc):
                    for ic in range(NIC):
                        if ic >= 2:
                            sync.wait_ge(s_pe.sem, p0 + 8 * (ic - 1))
                        sb = 2048 * (ic % 2)
                        for dc in range(NDC):
                            s_in.dma(sync.dma_start(
                                BSC[:, sb + dc * 128:sb + dc * 128 + 128],
                                wsrc[dc * 128:(dc + 1) * 128, ic * 128:(ic + 1) * 128]))
                            s_in.dma(sync.dma_start(
                                BSC[:, sb + 1024 + dc * 128:sb + 1024 + dc * 128 + 128],
                                wsrc[dc * 128:(dc + 1) * 128,
                                     II + ic * 128:II + (ic + 1) * 128]))

                @blk.tensor
                def _(t):
                    for ic in range(NIC):
                        t.wait_ge(s_in.sem, in0 + 256 * (ic + 1))
                        sb = 2048 * (ic % 2)
                        for tc in range(NT5):
                            k = ic * 8 + tc
                            if k >= 2:
                                t.wait_ge(s_v.sem, v0 + k - 1)
                            bs = (k % 2) * 4
                            last = None
                            for dc in range(NDC):
                                g = dc // 2
                                st0 = (dc % 2 == 0)
                                sp = (dc % 2 == 1)
                                sth = BSC[:, sb + dc * 128:sb + dc * 128 + 128]
                                stl = BSC[:, sb + 1024 + dc * 128:sb + 1024 + dc * 128 + 128]
                                mvh = XH[:, dc * T + tc * 512:dc * T + tc * 512 + 512]
                                mvl = XL[:, dc * T + tc * 512:dc * T + tc * 512 + 512]
                                t.matmul(upb[bs + g][:], sth, mvh,
                                         start=st0, stop=False)
                                t.matmul(upb[bs + g][:], sth, mvl,
                                         start=False, stop=False)
                                last = t.matmul(upb[bs + g][:], stl, mvh,
                                                start=False, stop=sp)
                            s_pe.inc(last)

                @blk.vector
                def _(v, MX=MX):
                    for ic in range(NIC):
                        for tc in range(NT5):
                            k = ic * 8 + tc
                            v.wait_ge(s_pe.sem, p0 + k + 1)
                            if k >= 2:
                                v.wait_ge(sg2[k % 2].sem,
                                          g2b[k % 2] + 16 * ((k - 2) // 2 + 1))
                            bs = (k % 2) * 4
                            st = (k % 2) * 512
                            v.tensor_copy(FS[:, st:st + 512], upb[bs][:])
                            for bb in range(1, 4):
                                v.tensor_add(FS[:, st:st + 512], FS[:, st:st + 512],
                                             upb[bs + bb][:])
                            s_v.inc(v.tensor_reduce(MX[:, k:k + 1], FS[:, st:st + 512],
                                                    AXX, ALU.max,
                                                    apply_absolute_value=True))

                @blk.gpsimd
                def _(g, tdst=tdst):
                    for ic in range(NIC):
                        for tc in range(NT5):
                            k = ic * 8 + tc
                            g.wait_ge(s_v.sem, v0 + k + 1)
                            sg2[k % 2].dma(g.dma_start(
                                tdst[ic * 128:(ic + 1) * 128, tc * 512:(tc + 1) * 512],
                                FS[:, (k % 2) * 512:(k % 2) * 512 + 512]))

        def emit_maxex(srcs, tag):
            if "fx" in _DIS:
                return
            """AllGather per-core colmaxes of each [128,192] src; leave the
            global max of src j in SC1[0:1, 16+j]."""
            n = len(srcs)
            with nc.Block() as blk:
                @blk.vector
                def _(v):
                    for j, MX in enumerate(srcs):
                        v.tensor_reduce(MHQ[:, j:j + 1], MX, AXX, ALU.max)
            with nc.Block() as blk:
                @blk.gpsimd
                def _(g):
                    for j in range(n):
                        s_g.dma(g.dma_start(AG_in[0:1, j * 128:(j + 1) * 128],
                                            MHQ[:, j:j + 1],
                                            allow_slow_non_contiguous=True))
                    g.wait_ge(s_g.sem, s_g.v)
                    if _FAKE_CC[0]:
                        for b in range(NC):
                            s_g.dma(g.dma_start(AG_out[b:b + 1, :], AG_in[:]))
                    else:
                        s_cc.inc(g.collective_compute(
                            "AllGather", ALU.bypass,
                            replica_groups=[list(range(NC))],
                            ins=[AG_in.ap().opt()],
                            outs=[AG_out.ap().opt()]))
                        g.wait_ge(s_cc.sem, s_cc.v)
                    s_g.dma(g.dma_start(AGOS[:], AG_out[:]))
                    g.wait_ge(s_g.sem, s_g.v)
            with nc.Block() as blk:
                @blk.vector
                def _(v):
                    for j in range(n):
                        v.tensor_reduce(SCR[0:NC, 8 + j:9 + j],
                                        AGOS[0:NC, j * 128:(j + 1) * 128],
                                        AXX, ALU.max)
            with nc.Block() as blk:
                @blk.gpsimd
                def _(g):
                    s_g.dma(g.dma_start(S_sm[4:5, 40:40 + NC * n], SCR[0:NC, 8:8 + n]))
                    g.wait_ge(s_g.sem, s_g.v)
                    s_g.dma(g.dma_start(SC1[0:1, 0:NC * n], S_sm[4:5, 40:40 + NC * n]))
                    g.wait_ge(s_g.sem, s_g.v)
            with nc.Block() as blk:
                @blk.vector
                def _(v):
                    for j in range(n):
                        v.tensor_reduce(SC1[0:1, 16 + j:17 + j],
                                        SC1[0:1, j:j + (NC - 1) * n + 1:n],
                                        AXX, ALU.max)

        _smc = [0]

        def bc_scales(sc1_cols, names):
            if "fx" in _DIS:
                return
            """SC1 cols -> S_sm row 5 -> partition-broadcast into B[name]s."""
            with nc.Block() as blk:
                @blk.gpsimd
                def _(g, sc1_cols=tuple(sc1_cols), names=tuple(names), c0=_smc[0]):
                    for jj, sc in enumerate(sc1_cols):
                        s_g.dma(g.dma_start(S_sm[5:6, c0 + jj:c0 + jj + 1],
                                            SC1[0:1, sc:sc + 1]))
                    g.wait_ge(s_g.sem, s_g.v)
                    for jj, nm in enumerate(names):
                        s_g.dma(g.dma_start(B[nm][:], S_sm[5:6, c0 + jj:c0 + jj + 1]
                                            .partition_broadcast(128).squeeze(1)))
                    g.wait_ge(s_g.sem, s_g.v)
            _smc[0] += len(sc1_cols)

        def _sblock(eng, fn):
            if "fx" in _DIS:
                return
            with nc.Block() as blk:
                if eng == "v":
                    blk.vector(fn)
                else:
                    blk.scalar(fn)
        emit_maxex([UMX, GMX], "f3")
        with nc.Block() as blk:
            @blk.vector
            def _(v):
                v.tensor_scalar(SC1[0:1, 18:20], SC1[0:1, 16:18],
                                B["s0"][0:1, 0:1], None, ALU.mult)
        with nc.Block() as blk:
            @blk.vector
            def _(v):
                v.tensor_scalar(SC1[0:1, 20:22], SC1[0:1, 18:20],
                                1.0 / QMAX, 1e-12, ALU.mult, ALU.max)
        with nc.Block() as blk:
            @blk.scalar
            def _(s):
                s.activation(SC1[0:1, 24:26], SC1[0:1, 20:22], ACT.Ln)
        with nc.Block() as blk:
            @blk.scalar
            def _(s):
                s.activation(SC1[0:1, 26:27], B["s0"][0:1, 0:1], ACT.Ln)
        with nc.Block() as blk:
            @blk.vector
            def _(v):
                v.tensor_scalar(SC1[0:1, 27:29], SC1[0:1, 24:26],
                                SC1[0:1, 26:27], None, ALU.subtract)
        with nc.Block() as blk:
            @blk.scalar
            def _(s):
                s.activation(SC1[0:1, 22:24], SC1[0:1, 27:29], ACT.Exp,
                             scale=-1.0)
        bc_scales([20, 21, 22, 23], ["sup", "sg", "cup", "cg"])

        NK4 = 0 if "f4" in _DIS else 192
        with nc.Block() as blk:
            in0 = s_in.v; a0 = s_a.v; v0 = s_v.v
            g2b = [sg2[0].v, sg2[1].v]
            sxb2 = [sx[0].v, sx[1].v]

            @blk.sync
            def _(sync):
                for k in range(NK4):
                    ic, tc = k // 8, k % 8
                    if k >= 2:
                        sync.wait_ge(s_v.sem, v0 + 2 * (k - 2) + 1)
                    sx[k % 2].dma(sync.dma_start(
                        FS[:, (k % 2) * 512:(k % 2) * 512 + 512],
                        S_tg[ic * 128:(ic + 1) * 128, tc * 512:(tc + 1) * 512]))

            @blk.vector
            def _(v):
                for k in range(NK4):
                    v.wait_ge(sx[k % 2].sem, sxb2[k % 2] + 16 * (k // 2 + 1))
                    if k >= 2:
                        v.wait_ge(s_a.sem, a0 + k - 1)
                    tin = (k % 2) * 512
                    trg = 1024 + (k % 2) * 512
                    v.tensor_scalar(FS[:, 4096:4608], FS[:, tin:tin + 512],
                                    B["cg"][:, 0:1], None, ALU.mult)
                    emit_round(v, FS, trg, FS, 4096, FS, 5248, 512)
                    s_v.inc(v.tensor_copy(FS[:, 5248:5249], FS[:, 5248:5249]))
                    v.wait_ge(s_a.sem, a0 + k + 1)
                    if k >= 2:
                        v.wait_ge(sg2[k % 2].sem, g2b[k % 2] + 16 * ((k - 2) // 2 + 1))
                    tsg = 2048 + (k % 2) * 512
                    tp2 = 3072 + (k % 2) * 512
                    v.tensor_scalar(FS[:, 4096:4608], FS[:, tsg:tsg + 512],
                                    65536.0, None, ALU.mult)
                    emit_round(v, FS, 4608, FS, 4096, FS, 5248, 512)
                    v.tensor_scalar(FS[:, 4608:5120], FS[:, 4608:5120],
                                    65535.0, None, ALU.min)
                    v.tensor_mul(FS[:, tp2:tp2 + 512], FS[:, trg:trg + 512],
                                 FS[:, 4608:5120])
                    v.tensor_reduce(PMX[:, k:k + 1], FS[:, tp2:tp2 + 512],
                                    AXX, ALU.max, apply_absolute_value=True)
                    s_v.inc(v.tensor_copy(FS[:, 5248:5249], FS[:, 5248:5249]))

            @blk.scalar
            def _(s):
                for k in range(NK4):
                    s.wait_ge(s_v.sem, v0 + 2 * k + 1)
                    trg = 1024 + (k % 2) * 512
                    tsg = 2048 + (k % 2) * 512
                    s_a.inc(s.activation(FS[:, tsg:tsg + 512], FS[:, trg:trg + 512],
                                         ACT.Sigmoid, scale=B["sg"][:, 0:1]))

            @blk.gpsimd
            def _(g):
                for k in range(NK4):
                    ic, tc = k // 8, k % 8
                    g.wait_ge(s_v.sem, v0 + 2 * k + 2)
                    tp2 = 3072 + (k % 2) * 512
                    sg2[k % 2].dma(g.dma_start(
                        S_p2[ic * 128:(ic + 1) * 128, tc * 512:(tc + 1) * 512],
                        FS[:, tp2:tp2 + 512]))

        emit_maxex([PMX], "f4")
        with nc.Block() as blk:
            @blk.vector
            def _(v):
                v.tensor_scalar(SC1[0:1, 18:19], SC1[0:1, 16:17],
                                B["sg"][0:1, 0:1], None, ALU.mult)
        with nc.Block() as blk:
            @blk.vector
            def _(v):
                v.tensor_scalar(SC1[0:1, 20:21], SC1[0:1, 18:19],
                                1.0 / (65536.0 * QMAX), 1e-12, ALU.mult, ALU.max)
        with nc.Block() as blk:
            @blk.scalar
            def _(s):
                s.activation(SC1[0:1, 24:25], SC1[0:1, 20:21], ACT.Ln)
        with nc.Block() as blk:
            @blk.scalar
            def _(s):
                s.activation(SC1[0:1, 25:26], B["sg"][0:1, 0:1], ACT.Ln)
        with nc.Block() as blk:
            @blk.vector
            def _(v):
                v.tensor_scalar(SC1[0:1, 26:27], SC1[0:1, 25:26],
                                SC1[0:1, 24:25], None, ALU.subtract)
        with nc.Block() as blk:
            @blk.vector
            def _(v):
                v.tensor_scalar(SC1[0:1, 21:22], SC1[0:1, 26:27],
                                -11.090354888959125, None, ALU.add)
        with nc.Block() as blk:
            @blk.scalar
            def _(s):
                s.activation(SC1[0:1, 22:23], SC1[0:1, 21:22], ACT.Exp)
        bc_scales([20, 22], ["s2", "c2"])

        NK5 = 0 if "f5" in _DIS else 192
        with nc.Block() as blk:
            in0 = s_in.v; v0 = s_v.v
            g2b = [sg2[0].v, sg2[1].v]
            sxb2 = [sx[0].v, sx[1].v]
            seb = s_em.v

            @blk.sync
            def _(sync):
                for k in range(NK5):
                    ic, tc = k // 8, k % 8
                    if k >= 2:
                        sync.wait_ge(s_v.sem, v0 + (k - 1))
                    rows = slice(ic * 128, (ic + 1) * 128)
                    cols = slice(tc * 512, (tc + 1) * 512)
                    sx[k % 2].dma(sync.dma_start(
                        FS[:, (k % 2) * 512:(k % 2) * 512 + 512], S_p2[rows, cols]))
                    s_em.dma(sync.dma_start(
                        FS[:, 1024 + (k % 2) * 512:1024 + (k % 2) * 512 + 512],
                        S_tup[rows, cols]))

            @blk.vector
            def _(v):
                for k in range(NK5):
                    v.wait_ge(sx[k % 2].sem, sxb2[k % 2] + 16 * (k // 2 + 1))
                    v.wait_ge(s_em.sem, seb + 16 * (k + 1))
                    if k >= 2:
                        v.wait_ge(sg2[k % 2].sem, g2b[k % 2] + 16 * ((k - 2) // 2 + 1))
                    tp2 = (k % 2) * 512
                    tup = 1024 + (k % 2) * 512
                    tp3 = 2048 + (k % 2) * 512
                    v.tensor_scalar(FS[:, 4096:4608], FS[:, tp2:tp2 + 512],
                                    B["c2"][:, 0:1], None, ALU.mult)
                    emit_round(v, FS, 3072, FS, 4096, FS, 5248, 512)
                    v.tensor_scalar(FS[:, 4096:4608], FS[:, tup:tup + 512],
                                    B["cup"][:, 0:1], None, ALU.mult)
                    emit_round(v, FS, 4608, FS, 4096, FS, 5248, 512)
                    v.tensor_mul(FS[:, tp3:tp3 + 512], FS[:, 3072:3584],
                                 FS[:, 4608:5120])
                    v.tensor_reduce(P3MX[:, k:k + 1], FS[:, tp3:tp3 + 512],
                                    AXX, ALU.max, apply_absolute_value=True)
                    s_v.inc(v.tensor_copy(FS[:, 5248:5249], FS[:, 5248:5249]))

            @blk.gpsimd
            def _(g):
                for k in range(NK5):
                    ic, tc = k // 8, k % 8
                    g.wait_ge(s_v.sem, v0 + k + 1)
                    tp3 = 2048 + (k % 2) * 512
                    sg2[k % 2].dma(g.dma_start(
                        S_p3[ic * 128:(ic + 1) * 128, tc * 512:(tc + 1) * 512],
                        FS[:, tp3:tp3 + 512]))

        emit_maxex([P3MX], "f5")
        with nc.Block() as blk:
            @blk.vector
            def _(v):
                v.tensor_scalar(SC1[0:1, 18:19], SC1[0:1, 16:17],
                                B["s2"][0:1, 0:1], None, ALU.mult)
        with nc.Block() as blk:
            @blk.vector
            def _(v):
                v.tensor_scalar(SC1[0:1, 18:19], SC1[0:1, 18:19],
                                B["sup"][0:1, 0:1], None, ALU.mult)
        with nc.Block() as blk:
            @blk.vector
            def _(v):
                v.tensor_scalar(SC1[0:1, 20:21], SC1[0:1, 18:19],
                                1.0 / QMAX, 1e-12, ALU.mult, ALU.max)
        with nc.Block() as blk:
            @blk.scalar
            def _(s):
                s.activation(SC1[0:1, 24:25], SC1[0:1, 20:21], ACT.Ln)
        with nc.Block() as blk:
            @blk.scalar
            def _(s):
                s.activation(SC1[0:1, 25:26], B["s2"][0:1, 0:1], ACT.Ln)
        with nc.Block() as blk:
            @blk.scalar
            def _(s):
                s.activation(SC1[0:1, 26:27], B["sup"][0:1, 0:1], ACT.Ln)
        with nc.Block() as blk:
            @blk.vector
            def _(v):
                v.tensor_add(SC1[0:1, 27:28], SC1[0:1, 25:26], SC1[0:1, 26:27])
        with nc.Block() as blk:
            @blk.vector
            def _(v):
                v.tensor_sub(SC1[0:1, 21:22], SC1[0:1, 27:28], SC1[0:1, 24:25])
        with nc.Block() as blk:
            @blk.scalar
            def _(s):
                s.activation(SC1[0:1, 22:23], SC1[0:1, 21:22], ACT.Exp)
        bc_scales([20, 22], ["s3", "c3"])

        with nc.Block() as blk:
            @blk.gpsimd
            def _(g):
                s_g.dma(g.dma_start(CTA[:, 0:1024], S_sm[1:2, 0:D]
                                    .partition_broadcast(128).squeeze(1)))
                g.wait_ge(s_g.sem, s_g.v)

        for tb in (() if "f6" in _DIS else range(8)):
            with (
                nc.psum_tensor([128, 512], F32) as w0,
                nc.psum_tensor([128, 512], F32) as w1,
                nc.psum_tensor([128, 512], F32) as w2,
                nc.psum_tensor([128, 512], F32) as w3,
                nc.psum_tensor([128, 512], F32) as w4,
                nc.psum_tensor([128, 512], F32) as w5,
                nc.psum_tensor([128, 512], F32) as w6,
                nc.psum_tensor([128, 512], F32) as w7,
                nc.Block() as blk,
            ):
                wps = [w0, w1, w2, w3, w4, w5, w6, w7]
                in0 = s_in.v; p0 = s_pe.v; v0 = s_v.v; g0 = s_g.v
                sxb2 = [sx[0].v, sx[1].v]
                seb = s_em.v

                @blk.sync
                def _(sync, tb=tb):
                    for ic in range(NIC):
                        if ic >= 2:
                            sync.wait_ge(s_v.sem, v0 + 2 * ic)
                            sync.wait_ge(s_pe.sem, p0 + ic - 1)
                        sx[ic % 2].dma(sync.dma_start(
                            FS[:, (ic % 2) * 512:(ic % 2) * 512 + 512],
                            S_p3[ic * 128:(ic + 1) * 128, tb * 512:(tb + 1) * 512]))
                        s_em.dma(sync.dma_start(
                            XH[:, (ic % 2) * 2048:(ic % 2) * 2048 + 1024],
                            S_wdt[ic * 128:(ic + 1) * 128, 0:D]))
                        s_em.dma(sync.dma_start(
                            XH[:, (ic % 2) * 2048 + 1024:(ic % 2) * 2048 + 2048],
                            S_wdt[ic * 128:(ic + 1) * 128, D:2 * D]))
                    for tt in range(4):
                        if tt >= 1:
                            sync.wait_ge(s_v.sem, v0 + 1 + 2 * NIC + tt)
                        sync.wait_ge(s_g.sem, g0 + 16 * tt)
                        s_in.dma(sync.dma_start(
                            FS[:, 5760:6784],
                            x_in[tb * 512 + tt * 128:tb * 512 + (tt + 1) * 128, :]))
                        for h in range(NH):
                            s_in.dma(sync.dma_start(
                                FS[:, 6784 + h * 256:6784 + (h + 1) * 256],
                                S_attn[h, tb * 512 + tt * 128:tb * 512 + (tt + 1) * 128, :]))

                @blk.vector
                def _(v, tb=tb):
                    s_v.inc(v.memset(FS[:, 1664:5760], 0.0))
                    for ic in range(NIC):
                        v.wait_ge(sx[ic % 2].sem, sxb2[ic % 2] + 16 * (ic // 2 + 1))
                        if ic >= 2:
                            v.wait_ge(s_pe.sem, p0 + ic - 1)
                        tin = (ic % 2) * 512
                        v.tensor_scalar(FS[:, 1024:1536], FS[:, tin:tin + 512],
                                        B["c3"][:, 0:1], None, ALU.mult)
                        emit_round(v, FS, 1024, FS, 1024, FS, 1536, 512)
                        v.tensor_copy(XL[:, (ic % 2) * 1024:(ic % 2) * 1024 + 512],
                                      FS[:, 1024:1536])
                        s_v.inc(v.tensor_sub(
                            XL[:, (ic % 2) * 1024 + 512:(ic % 2) * 1024 + 1024],
                            FS[:, 1024:1536],
                            XL[:, (ic % 2) * 1024:(ic % 2) * 1024 + 512]))
                        v.wait_ge(s_pe.sem, p0 + ic + 1)
                        for bb in range(8):
                            ins = v.tensor_add(
                                FS[:, 1664 + bb * 512:1664 + (bb + 1) * 512],
                                FS[:, 1664 + bb * 512:1664 + (bb + 1) * 512],
                                wps[bb][:])
                        s_v.inc(ins)
                    for tt in range(4):
                        v.wait_ge(s_in.sem, in0 + 80 * (tt + 1))
                        ab = 1664 + tt * 1024
                        v.tensor_scalar(FS[:, ab:ab + 1024], FS[:, ab:ab + 1024],
                                        B["s3"][:, 0:1], None, ALU.mult)
                        v.tensor_add(FS[:, ab:ab + 1024], FS[:, ab:ab + 1024],
                                     FS[:, 5760:6784])
                        for h in range(NH):
                            col = h * 32 + tb * 4 + tt
                            v.tensor_scalar(FS[:, 6784 + h * 256:6784 + (h + 1) * 256],
                                            FS[:, 6784 + h * 256:6784 + (h + 1) * 256],
                                            ABT[:, col:col + 1],
                                            ABT[:, 128 + col:128 + col + 1],
                                            ALU.mult, ALU.add)
                        for h in range(NH):
                            v.tensor_mul(FS[:, 6784 + h * 256:6784 + (h + 1) * 256],
                                         FS[:, 6784 + h * 256:6784 + (h + 1) * 256],
                                         CTA[:, h * 256:(h + 1) * 256])
                        for h in range(NH):
                            ins = v.tensor_add(
                                FS[:, ab + h * 256:ab + (h + 1) * 256],
                                FS[:, ab + h * 256:ab + (h + 1) * 256],
                                FS[:, 6784 + h * 256:6784 + (h + 1) * 256])
                        s_v.inc(ins)

                @blk.tensor
                def _(t):
                    for ic in range(NIC):
                        t.wait_ge(s_v.sem, v0 + 2 * ic + 2)
                        t.wait_ge(s_em.sem, seb + 32 * (ic + 1))
                        if ic >= 1:
                            t.wait_ge(s_v.sem, v0 + 2 * ic + 1)
                        last = None
                        sh = XL[:, (ic % 2) * 1024:(ic % 2) * 1024 + 512]
                        sl = XL[:, (ic % 2) * 1024 + 512:(ic % 2) * 1024 + 1024]
                        mh = XH[:, (ic % 2) * 2048:(ic % 2) * 2048 + 1024]
                        ml = XH[:, (ic % 2) * 2048 + 1024:(ic % 2) * 2048 + 2048]
                        for tt in range(4):
                            for hf in range(2):
                                pb = wps[tt * 2 + hf]
                                sth = sh[:, tt * 128:(tt + 1) * 128]
                                stl = sl[:, tt * 128:(tt + 1) * 128]
                                mvh = mh[:, hf * 512:(hf + 1) * 512]
                                mvl = ml[:, hf * 512:(hf + 1) * 512]
                                t.matmul(pb[:], sth, mvh, start=True, stop=False)
                                t.matmul(pb[:], sth, mvl, start=False, stop=False)
                                last = t.matmul(pb[:], stl, mvh, start=False,
                                                stop=True)
                        s_pe.inc(last)

                @blk.gpsimd
                def _(g, tb=tb):
                    for tt in range(4):
                        g.wait_ge(s_v.sem, v0 + 1 + 2 * NIC + tt + 1)
                        r0 = tb * 512 + tt * 128
                        ab = 1664 + tt * 1024
                        s_g.dma(g.dma_start(out_d[r0:r0 + 128, :], FS[:, ab:ab + 1024]))

    if upto < 4:
        with (
            nc.Block() as blk,
        ):
            ob = FS[:, 0:1024]

            @blk.vector
            def _(v):
                s_v.inc(v.memset(ob[:], 0.0))

            @blk.gpsimd
            def _(g):
                s_v.wait(g)
                for i in range(NT):
                    s_g.dma(g.dma_start(out_d[i * 128:(i + 1) * 128, :], ob[:]))
                s_g.wait(g)

    nc.finalize()
    return nc, list(dbg.keys())


def _host_prepare(inputs):
    x = np.ascontiguousarray(inputs["L0_attn_x"], dtype=np.float32)
    wq = np.asarray(inputs["L0_attn_wq"], np.float32)
    wk = np.asarray(inputs["L0_attn_wk"], np.float32)
    wv = np.asarray(inputs["L0_attn_wv"], np.float32)

    def blocks(w):
        out = np.zeros((NDC, 128, 128), np.float32)
        for c in range(NDC):
            for g in range(32):
                out[c, 4 * g:4 * g + 4, 4 * g:4 * g + 4] = w[32 * c + g].T
        return out

    consts = np.zeros((128, 642), np.float32)
    consts[:, 0:128] = np.eye(128, dtype=np.float32)
    consts[:, 128:640] = np.broadcast_to(np.arange(512, dtype=np.float32), (128, 512))
    consts[:, 640] = np.arange(128, dtype=np.float32)
    consts[:, 641] = 1.0

    import ml_dtypes
    masks = np.zeros((4, 128, 512), np.float32)
    ls = np.arange(512)
    for j in range(4):
        for p in range(128):
            masks[j, p, :] = (ls >= 128 * j + p).astype(np.float32)
    masks = masks.astype(ml_dtypes.bfloat16)

    shared = {
        "wqd_in": blocks(wq),
        "wkd_in": blocks(wk),
        "wvd_in": blocks(wv),
        "wi_in": np.asarray(inputs["L0_attn_wi"], np.float32),
        "wf_in": np.asarray(inputs["L0_attn_wf"], np.float32),
        "bi_in": np.asarray(inputs["L0_attn_bi"], np.float32).reshape(NH, 1),
        "bf_in": np.asarray(inputs["L0_attn_bf"], np.float32).reshape(NH, 1),
        "lnw_in": np.asarray(inputs["L0_attn_ln_w"], np.float32).reshape(1, D),
        "wgate_in": np.asarray(inputs["L0_ffn_w_gate"], np.float32),
        "wup_in": np.asarray(inputs["L0_ffn_w_up"], np.float32),
        "wdown_in": np.asarray(inputs["L0_ffn_w_down"], np.float32),
        "consts_in": consts,
        "masks_in": masks,
    }
    in_maps = []
    for b in range(NC):
        m = dict(shared)
        m["x_in"] = np.ascontiguousarray(x[b])
        in_maps.append(m)
    return in_maps


def kernel(**inputs):
    in_maps = _host_prepare(inputs)
    nc, _ = build_nc()
    res = run_bass_kernel_spmd(nc, in_maps, core_ids=list(range(NC)))
    return np.stack([r["out"] for r in res.results], axis=0)


# version 06 (final)
# speedup vs baseline: 1.0650x; 1.0650x over previous
import numpy as np
import concourse.bass as bass
import concourse.mybir as mybir
from concourse.bass_utils import run_bass_kernel_spmd

F32 = mybir.dt.float32
BF16 = mybir.dt.bfloat16
I32 = mybir.dt.int32
ALU = mybir.AluOpType
ACT = mybir.ActivationFunctionType
AXX = mybir.AxisListType.X

T, D, NH, DH, II, NC = 4096, 1024, 4, 256, 3072, 8
QMAX = 32767.0
NT = T // 128
NDC = D // 128
NIC = II // 128
NT5 = T // 512


_SCRAPF = [None]


def vfence(eng, s=None):
    sf = _SCRAPF[0]
    if hasattr(eng, "memset"):
        eng.memset(sf[:, 0:256], 0.0)
    else:
        eng.copy(sf[:, 256:512], sf[:, 0:256])


class Sem:

    def __init__(self, nc, name):
        self.sem = nc.alloc_semaphore(name)
        self.v = 0

    def inc(self, instr, n=1):
        instr.then_inc(self.sem, n)
        self.v += n
        return instr

    def dma(self, instr):
        return self.inc(instr, 16)

    def wait(self, eng, v=None):
        eng.wait_ge(self.sem, self.v if v is None else v)


_FAKE_CC = [False]
_LIM = [NH, NT5]
_DIS = set()


def build_nc(debug=(), upto=99):
    nc = bass.Bass("TRN2", target_bir_lowering=False, debug=False,
                   detect_race_conditions=False)

    x_in = nc.dram_tensor("x_in", [T, D], F32, kind="ExternalInput")
    wqd_in = nc.dram_tensor("wqd_in", [NDC, 128, 128], F32, kind="ExternalInput")
    wkd_in = nc.dram_tensor("wkd_in", [NDC, 128, 128], F32, kind="ExternalInput")
    wvd_in = nc.dram_tensor("wvd_in", [NDC, 128, 128], F32, kind="ExternalInput")
    wi_in = nc.dram_tensor("wi_in", [NH, 3 * D], F32, kind="ExternalInput")
    wf_in = nc.dram_tensor("wf_in", [NH, 3 * D], F32, kind="ExternalInput")
    bi_in = nc.dram_tensor("bi_in", [NH, 1], F32, kind="ExternalInput")
    bf_in = nc.dram_tensor("bf_in", [NH, 1], F32, kind="ExternalInput")
    lnw_in = nc.dram_tensor("lnw_in", [1, D], F32, kind="ExternalInput")
    wgate_in = nc.dram_tensor("wgate_in", [II, D], F32, kind="ExternalInput")
    wup_in = nc.dram_tensor("wup_in", [II, D], F32, kind="ExternalInput")
    wdown_in = nc.dram_tensor("wdown_in", [D, II], F32, kind="ExternalInput")
    consts_in = nc.dram_tensor("consts_in", [128, 642], F32, kind="ExternalInput")
    masks_in = nc.dram_tensor("masks_in", [4, 128, 512], BF16, kind="ExternalInput")

    out_d = nc.dram_tensor("out", [T, D], F32, kind="ExternalOutput")

    dbg = {}

    def dbg_out(name, shape):
        if name in debug:
            dbg[name] = nc.dram_tensor("dbg_" + name, shape, F32, kind="ExternalOutput")
        return dbg.get(name)

    S_tup = nc.dram_tensor("S_tup", [II, T], F32)
    S_tg = nc.dram_tensor("S_tg", [II, T], F32)
    S_p2 = nc.dram_tensor("S_p2", [II, T], F32)
    S_rup = nc.dram_tensor("S_rup", [II, T], F32)
    S_p3 = nc.dram_tensor("S_p3", [II, T], F32)
    S_qh = nc.dram_tensor("S_qh", [D, T], BF16)
    S_ql = nc.dram_tensor("S_ql", [D, T], BF16)
    S_kh = nc.dram_tensor("S_kh", [D, T], BF16)
    S_kl = nc.dram_tensor("S_kl", [D, T], BF16)
    S_v = nc.dram_tensor("S_v", [NH, T, DH], F32)
    S_attn = nc.dram_tensor("S_attn", [NH, T, DH], F32)
    S_st = nc.dram_tensor("S_st", [128, 512], F32)
    S_sm = nc.dram_tensor("S_sm", [64, D], F32)
    AG_in = nc.dram_tensor("AG_in", [1, 1056], F32)
    AG_out = nc.dram_tensor("AG_out", [NC, 1056], F32)

    s_in = Sem(nc, "s_in")
    s_g = Sem(nc, "s_g")
    s_sc = Sem(nc, "s_sc")
    s_pe = Sem(nc, "s_pe")
    s_v = Sem(nc, "s_v")
    s_a = Sem(nc, "s_a")
    s_cc = Sem(nc, "s_cc")
    sx = [Sem(nc, "sx0"), Sem(nc, "sx1")]
    sg2 = [Sem(nc, "sg20"), Sem(nc, "sg21")]
    s_em = Sem(nc, "s_em")

    MU = nc.alloc_sbuf_tensor("MU", [128, NT], F32)
    MSQ = nc.alloc_sbuf_tensor("MSQ", [128, NT], F32)
    MXX = nc.alloc_sbuf_tensor("MXX", [128, NT], F32)
    MNN = nc.alloc_sbuf_tensor("MNN", [128, NT], F32)
    RINV = nc.alloc_sbuf_tensor("RINV", [128, NT], F32)
    NU = nc.alloc_sbuf_tensor("NU", [128, NT], F32)
    RABS = nc.alloc_sbuf_tensor("RABS", [128, NT], F32)
    SC1 = nc.alloc_sbuf_tensor("SC1", [1, 32], F32)
    IDENT = nc.alloc_sbuf_tensor("IDENT", [128, 128], F32)
    IOTA = nc.alloc_sbuf_tensor("IOTA", [128, 64], F32)
    IOTAP = nc.alloc_sbuf_tensor("IOTAP", [128, 1], F32)
    ONESC = nc.alloc_sbuf_tensor("ONESC", [128, 1], F32)
    _cvals = [1e-5, -1.0, 65536.0, 0.5, 0.0625, 1e-8]
    _ctens = []
    for _j, _cv in enumerate(_cvals):
        _t = nc.alloc_sbuf_tensor("CONST%d" % _j, [128, 1], F32)
        _ctens.append(_t)
        nc.const_aps.register(F32, _cv, _t[:, 0:1])
    BNAMES = ["s0", "invs0", "cup", "cg", "sg", "c2", "c3", "s3", "sup", "s2"]
    B = {n: nc.alloc_sbuf_tensor("B_" + n, [128, 1], F32) for n in BNAMES}
    SCR = nc.alloc_sbuf_tensor("SCR", [8, 32], F32)
    SCRI = nc.alloc_sbuf_tensor("SCRI", [1, 8], I32)
    P1A = nc.alloc_sbuf_tensor("P1A", [128, 4], F32)

    XH = nc.alloc_sbuf_tensor("XH", [128, NDC * T], BF16)
    XL = nc.alloc_sbuf_tensor("XL", [128, NDC * T], BF16)
    FS = nc.alloc_sbuf_tensor("FS", [128, 8448], F32)
    GF = nc.alloc_sbuf_tensor("GF", [NH, T], F32)
    GV = FS[0:NH, 4096:8192]
    AGIS = FS[0:1, 0:1056]
    AGOS = FS[0:NC, 1056:2112]
    P1R = FS[0:2, 3136:3264]
    P1B = FS[0:1, 3264:3392]
    CT = nc.alloc_sbuf_tensor("CT", [128, NT * NH], F32)
    ET = nc.alloc_sbuf_tensor("ET", [128, NT * NH], F32)
    WSB = nc.alloc_sbuf_tensor("WSB", [128, 4 * NDC * 4], BF16)
    BIB = nc.alloc_sbuf_tensor("BIB", [NH, 1], F32)
    BFB = nc.alloc_sbuf_tensor("BFB", [NH, 1], F32)
    BSC = nc.alloc_sbuf_tensor("BSC", [128, 4096], BF16)
    WQKL = nc.alloc_sbuf_tensor("WQKL", [128, 1024], BF16)
    SCRAPF = nc.alloc_sbuf_tensor("SCRAPF", [128, 512], BF16)
    _SCRAPF[0] = SCRAPF
    MASKS = nc.alloc_sbuf_tensor("MASKS", [128, 2048], BF16)
    ONESR = nc.alloc_sbuf_tensor("ONESR", [1, 128], F32)
    CFB = [nc.alloc_sbuf_tensor("CFB0", [128, 1], F32),
           nc.alloc_sbuf_tensor("CFB1", [128, 1], F32)]
    CTA = nc.alloc_sbuf_tensor("CTA", [128, 1024], F32)
    SC4 = nc.alloc_sbuf_tensor("SC4", [128, 4], F32)
    MHQ = nc.alloc_sbuf_tensor("MHQ", [128, 4], F32)
    QQ4 = nc.alloc_sbuf_tensor("QQ4", [128, 4], F32)
    ABT = nc.alloc_sbuf_tensor("ABT", [128, 256], F32)
    ATST = nc.alloc_sbuf_tensor("ATST", [128, 1024], F32)
    ATS2 = CTA
    GAMB = nc.alloc_sbuf_tensor("GAMB", [128, 256], F32)
    SCRA = SCRAPF[:, 0:256]
    SCRB = SCRAPF[:, 256:512]

    if True:
        xb0 = FS[:, 0:1024]
        xb1 = FS[:, 1024:2048]
        sqb = FS[:, 2048:3072]
        with nc.Block() as blk:
            xbufs = [xb0, xb1]
            v0 = s_v.v
            in0 = s_in.v
            sxb = [sx[0].v, sx[1].v]

            @blk.sync
            def _(sync):
                s_in.dma(sync.dma_start(IDENT[:], consts_in[:, 0:128]))
                s_in.dma(sync.dma_start(IOTA[:], consts_in[:, 128:192]))
                s_in.dma(sync.dma_start(IOTAP[:], consts_in[:, 640:641],
                                        allow_slow_non_contiguous=True))
                s_in.dma(sync.dma_start(ONESC[:], consts_in[:, 641:642],
                                        allow_slow_non_contiguous=True))
                for i in range(NT):
                    if i >= 2:
                        sync.wait_ge(s_v.sem, v0 + (i - 1))
                    sx[i % 2].dma(sync.dma_start(xbufs[i % 2][:],
                                                 x_in[i * 128:(i + 1) * 128, :]))

            @blk.vector
            def _(v):
                for _j, _cv in enumerate(_cvals):
                    v.memset(_ctens[_j][:], _cv)
                for i in range(NT):
                    xb = xbufs[i % 2]
                    v.wait_ge(sx[i % 2].sem, sxb[i % 2] + 16 * (i // 2 + 1))
                    v.tensor_reduce(MU[:, i:i + 1], xb[:], AXX, ALU.add)
                    v.tensor_reduce(MXX[:, i:i + 1], xb[:], AXX, ALU.max)
                    v.tensor_reduce(MNN[:, i:i + 1], xb[:], AXX, ALU.min)
                    v.tensor_mul(sqb[:], xb[:], xb[:])
                    s_v.inc(v.tensor_reduce(MSQ[:, i:i + 1], sqb[:], AXX, ALU.add))

        with nc.Block() as blk:
            @blk.vector
            def _(v):
                v.tensor_scalar(MU[:], MU[:], 1.0 / D, None, ALU.mult)
                v.tensor_scalar(MSQ[:], MSQ[:], 1.0 / D, None, ALU.mult)
                vfence(v, s_v)
                v.tensor_mul(NU[:], MU[:], MU[:])
                vfence(v, s_v)
                v.tensor_sub(MSQ[:], MSQ[:], NU[:])

        with nc.Block() as blk:
            @blk.scalar
            def _(s):
                s.activation(RABS[:], MSQ[:], ACT.Ln, bias=1e-5)
                vfence(s, s_a)
                s_a.inc(s.activation(RINV[:], RABS[:], ACT.Exp, scale=-0.5))

            @blk.vector
            def _(v):
                s_a.wait(v)
                v.tensor_mul(NU[:], MU[:], RINV[:])
                v.tensor_sub(MXX[:], MXX[:], MU[:])
                v.tensor_sub(MNN[:], MU[:], MNN[:])
                vfence(v, s_v)
                v.tensor_scalar(NU[:], NU[:], -1.0, None, ALU.mult)
                v.tensor_max(MXX[:], MXX[:], MNN[:])
                vfence(v, s_v)
                v.tensor_mul(RABS[:], MXX[:], RINV[:])

    if "stats0" in debug:
        dbg["stats0"] = nc.dram_tensor("dbg_stats0", [128, 4 * NT], F32, kind="ExternalOutput")
        with nc.Block() as blk:
            @blk.gpsimd
            def _(g):
                s_g.dma(g.dma_start(dbg["stats0"][:, 0:NT], NU[:]))
                s_g.dma(g.dma_start(dbg["stats0"][:, NT:2 * NT], RINV[:]))
                s_g.dma(g.dma_start(dbg["stats0"][:, 2 * NT:3 * NT], MU[:]))
                s_g.dma(g.dma_start(dbg["stats0"][:, 3 * NT:4 * NT], MSQ[:]))
                g.wait_ge(s_g.sem, s_g.v)

    dtok = None
    if "xntok" in debug:
        dbg["xntok"] = nc.dram_tensor("dbg_xntok", [T, D], F32, kind="ExternalOutput")
        dtok = dbg["xntok"]
    with (
        nc.psum_tensor([128, 128], F32) as pb0,
        nc.psum_tensor([128, 128], F32) as pb1,
        nc.Block() as blk,
    ):
        ybufs = [FS[:, 0:1024], FS[:, 1024:2048]]
        xnbufs = [FS[:, 2048:3072], FS[:, 3072:4096]]
        pbanks = [pb0, pb1]
        a0 = s_a.v
        p0 = s_pe.v
        v0 = s_v.v
        in0 = s_in.v
        sxb = [sx[0].v, sx[1].v]
        av = {}
        lv = {}
        pos = v0
        order = []
        for i in range(min(2, NT)):
            order.append(("af", i, 0))
        for j in range(NT):
            for c in range(NDC):
                order.append(("lo", j, c))
            if j + 2 < NT:
                order.append(("af", j + 2, 0))
        for kind, aa, bb in order:
            pos += 1
            if kind == "af":
                av[aa] = pos
            else:
                lv[(aa, bb)] = pos

        @blk.sync
        def _(sync):
            for i in range(NT):
                if i >= 2:
                    sync.wait_ge(s_v.sem, av[i - 2])
                sx[i % 2].dma(sync.dma_start(ybufs[i % 2][:],
                                             x_in[i * 128:(i + 1) * 128, :]))

        @blk.vector
        def _(v):
            def emit_af(i):
                v.wait_ge(sx[i % 2].sem, sxb[i % 2] + 16 * (i // 2 + 1))
                if i >= 2:
                    v.wait_ge(s_pe.sem, p0 + 8 * (i - 1))
                s_v.inc(v.tensor_scalar(xnbufs[i % 2][:], ybufs[i % 2][:],
                                        RINV[:, i:i + 1], NU[:, i:i + 1],
                                        ALU.mult, ALU.add))

            def emit_lo(j, c):
                k = j * NDC + c
                v.wait_ge(s_a.sem, a0 + k + 1)
                col = c * T + j * 128
                s_v.inc(v.tensor_sub(XL[:, col:col + 128],
                                     pbanks[k % 2][:], XH[:, col:col + 128]))

            for kind, aa, bb in order:
                if kind == "af":
                    emit_af(aa)
                else:
                    emit_lo(aa, bb)

        @blk.tensor
        def _(t):
            for i in range(NT):
                for c in range(NDC):
                    k = i * NDC + c
                    w = av[i]
                    if k >= 2:
                        w = max(w, lv[(i, c)] - 9 if False else lv.get((i * NDC + c - 2) // NDC if False else 0, 0))
                    t.wait_ge(s_v.sem, w)
                    if k >= 2:
                        kk = k - 2
                        t.wait_ge(s_v.sem, lv[(kk // NDC, kk % NDC)])
                    s_pe.inc(t.transpose(pbanks[k % 2][:],
                                         xnbufs[i % 2][:, c * 128:(c + 1) * 128],
                                         IDENT[:]))

        @blk.scalar
        def _(s):
            for i in range(NT):
                for c in range(NDC):
                    k = i * NDC + c
                    s.wait_ge(s_pe.sem, p0 + k + 1)
                    col = c * T + i * 128
                    s_a.inc(s.copy(XH[:, col:col + 128], pbanks[k % 2][:]))

        if dtok is not None:
            @blk.gpsimd
            def _(g):
                for i in range(NT):
                    g.wait_ge(s_v.sem, av[i])
                    s_g.dma(g.dma_start(dtok[i * 128:(i + 1) * 128, :],
                                        xnbufs[i % 2][:]))
                g.wait_ge(s_g.sem, s_g.v)

    if "t16" in debug:
        dbg["t16"] = nc.dram_tensor("dbg_t16", [128, 3 * D + 4], F32, kind="ExternalOutput")
        with (
            nc.sbuf_tensor([128, D], F32) as tyb,
            nc.sbuf_tensor([128, D], F32) as tnb,
            nc.sbuf_tensor([128, D], F32) as tnb2,
            nc.Block() as blk,
        ):
            i16 = 16
            in0b = s_in.v

            @blk.sync
            def _(sync):
                s_in.dma(sync.dma_start(tyb[:], x_in[i16 * 128:(i16 + 1) * 128, :]))

            @blk.vector
            def _(v):
                v.wait_ge(s_in.sem, in0b + 16)
                v.tensor_scalar(tnb[:], tyb[:], RINV[:, i16:i16 + 1],
                                NU[:, i16:i16 + 1], ALU.mult, ALU.add)
                v.tensor_scalar(tnb2[:], tyb[:], RINV[:, i16:i16 + 1], None, ALU.mult)
                s_v.inc(v.tensor_scalar(tnb2[:], tnb2[:], NU[:, i16:i16 + 1], None, ALU.add))

            @blk.gpsimd
            def _(g):
                s_v.wait(g)
                s_g.dma(g.dma_start(dbg["t16"][:, 0:D], tyb[:]))
                s_g.dma(g.dma_start(dbg["t16"][:, D:2 * D], tnb[:]))
                s_g.dma(g.dma_start(dbg["t16"][:, 2 * D:3 * D], tnb2[:]))
                s_g.dma(g.dma_start(dbg["t16"][:, 3 * D:3 * D + 1], RINV[:, i16:i16 + 1],
                                    allow_slow_non_contiguous=True))
                s_g.dma(g.dma_start(dbg["t16"][:, 3 * D + 1:3 * D + 2], NU[:, i16:i16 + 1],
                                    allow_slow_non_contiguous=True))
                s_g.dma(g.dma_start(dbg["t16"][:, 3 * D + 2:3 * D + 3], MU[:, i16:i16 + 1],
                                    allow_slow_non_contiguous=True))
                g.wait_ge(s_g.sem, s_g.v)

    if upto >= 1:
        with (
            nc.psum_tensor([2, 128], F32) as ptr2,
            nc.psum_tensor([1, 8], F32) as ptr8,
        ):
            GTMP = FS[0:NC, 2112:3136]
            with nc.Block() as blk:
                @blk.vector
                def _(v):
                    v.tensor_reduce(P1A[:, 0:1], RABS[:], AXX, ALU.max)
            with nc.Block() as blk:
                @blk.vector
                def _(v):
                    v.tensor_scalar(MSQ[:], RABS[:], P1A[:, 0:1], None, ALU.is_ge)
            with nc.Block() as blk:
                @blk.vector
                def _(v):
                    v.tensor_mul(MSQ[:], MSQ[:], IOTA[:, 0:NT])
            with nc.Block() as blk:
                @blk.vector
                def _(v):
                    v.tensor_reduce(P1A[:, 1:2], MSQ[:], AXX, ALU.max)
            with nc.Block() as blk:
                @blk.vector
                def _(v):
                    v.tensor_scalar(P1A[:, 2:3], P1A[:, 1:2], 128.0, None, ALU.mult)
            with nc.Block() as blk:
                @blk.vector
                def _(v):
                    v.tensor_add(P1A[:, 1:2], P1A[:, 2:3], IOTAP[:])

            with nc.Block() as blk:
                @blk.tensor
                def _(t):
                    t.transpose(ptr2[:], P1A[:, 0:2], IDENT[:])

            with nc.Block() as blk:
                @blk.vector
                def _(v):
                    s_v.inc(v.tensor_copy(P1R[:], ptr2[:]))

            with nc.Block() as blk:
                @blk.sync
                def _(sync):
                    s_in.dma(sync.dma_start(P1B[:], P1R[1:2, :]))

                @blk.vector
                def _(v):
                    s_in.wait(v)
                    v.tensor_reduce(SCR[0:1, 0:1], P1R[0:1, :], AXX, ALU.max)
                    v.tensor_scalar(P1B[:], P1B[:], 1.0, None, ALU.add)
            with nc.Block() as blk:
                @blk.vector
                def _(v):
                    v.tensor_scalar(P1R[0:1, :], P1R[0:1, :], SCR[0:1, 0:1], None, ALU.is_ge)
            with nc.Block() as blk:
                @blk.vector
                def _(v):
                    v.tensor_mul(P1B[:], P1B[:], P1R[0:1, :])
            with nc.Block() as blk:
                @blk.vector
                def _(v):
                    v.tensor_reduce(SCR[0:1, 1:2], P1B[:], AXX, ALU.max)
            with nc.Block() as blk:
                @blk.vector
                def _(v):
                    v.tensor_scalar(SCR[0:1, 1:2], SCR[0:1, 1:2], -1.0, None, ALU.add)
            with nc.Block() as blk:
                @blk.vector
                def _(v):
                    v.tensor_copy(SCRI[0:1, 0:1], SCR[0:1, 1:2])
                    v.memset(AGIS[0:1, 1025:1056], 0.0)
                    v.tensor_copy(AGIS[0:1, 1024:1025], SCR[0:1, 1:2])

            with nc.Block() as blk:
                @blk.sync
                def _(sync):
                    with sync.register("p1t") as reg:
                        sync.reg_load(reg, SCRI[0:1, 0:1])
                        val = sync.snap(reg, min_val=0, max_val=T - 1)
                        s_in.dma(sync.dma_start(AGIS[0:1, 0:1024], x_in[bass.ds(val, 1), :]))

                @blk.gpsimd
                def _(g):
                    s_in.wait(g)
                    s_g.dma(g.dma_start(AG_in[:], AGIS[:]))
                    g.wait_ge(s_g.sem, s_g.v)
                    if _FAKE_CC[0]:
                        for b in range(NC):
                            s_g.dma(g.dma_start(AG_out[b:b + 1, :], AG_in[:]))
                    else:
                        s_cc.inc(g.collective_compute(
                            "AllGather", ALU.bypass,
                            replica_groups=[list(range(NC))],
                            ins=[AG_in.ap().opt()],
                            outs=[AG_out.ap().opt()]))
                        g.wait_ge(s_cc.sem, s_cc.v)
                    s_g.dma(g.dma_start(AGOS[:], AG_out[:]))
                    g.wait_ge(s_g.sem, s_g.v)

            with nc.Block() as blk:
                @blk.vector
                def _(v):
                    rows = NC
                    src_ap = AGOS[0:NC, 0:1024]
                    v.tensor_reduce(SCR[0:rows, 8:9], src_ap, AXX, ALU.add)
                    v.tensor_reduce(SCR[0:rows, 10:11], src_ap, AXX, ALU.max)
                    v.tensor_reduce(SCR[0:rows, 11:12], src_ap, AXX, ALU.min)
                    v.tensor_mul(GTMP[0:rows, 0:1024], src_ap, src_ap)
            with nc.Block() as blk:
                @blk.vector
                def _(v):
                    rows = NC
                    v.tensor_scalar(SCR[0:rows, 8:9], SCR[0:rows, 8:9], 1.0 / D,
                                    None, ALU.mult)
                    v.tensor_reduce(SCR[0:rows, 9:10], GTMP[0:rows, 0:1024], AXX, ALU.add)
            with nc.Block() as blk:
                @blk.vector
                def _(v):
                    rows = NC
                    v.tensor_scalar(SCR[0:rows, 9:10], SCR[0:rows, 9:10], 1.0 / D,
                                    None, ALU.mult)
                    v.tensor_mul(SCR[0:rows, 12:13], SCR[0:rows, 8:9], SCR[0:rows, 8:9])
            with nc.Block() as blk:
                @blk.vector
                def _(v):
                    rows = NC
                    v.tensor_sub(SCR[0:rows, 9:10], SCR[0:rows, 9:10], SCR[0:rows, 12:13])

            with nc.Block() as blk:
                @blk.scalar
                def _(s):
                    s.activation(SCR[0:NC, 12:13], SCR[0:NC, 9:10], ACT.Ln, bias=1e-5)
            with nc.Block() as blk:
                @blk.scalar
                def _(s):
                    s.activation(SCR[0:NC, 13:14], SCR[0:NC, 12:13], ACT.Exp, scale=-0.5)

            with nc.Block() as blk:
                @blk.vector
                def _(v):
                    v.tensor_sub(SCR[0:NC, 10:11], SCR[0:NC, 10:11], SCR[0:NC, 8:9])
                    v.tensor_sub(SCR[0:NC, 11:12], SCR[0:NC, 8:9], SCR[0:NC, 11:12])
            with nc.Block() as blk:
                @blk.vector
                def _(v):
                    v.tensor_max(SCR[0:NC, 10:11], SCR[0:NC, 10:11], SCR[0:NC, 11:12])
            with nc.Block() as blk:
                @blk.vector
                def _(v):
                    v.tensor_mul(SCR[0:NC, 14:15], SCR[0:NC, 10:11], SCR[0:NC, 13:14])

            with nc.Block() as blk:
                @blk.tensor
                def _(t):
                    t.transpose(ptr8[:], SCR[0:NC, 14:15], IDENT[0:NC, 0:NC])

            with nc.Block() as blk:
                @blk.vector
                def _(v):
                    v.tensor_reduce(SCR[0:1, 15:16], ptr8[0:1, :], AXX, ALU.max)
            with nc.Block() as blk:
                @blk.vector
                def _(v):
                    v.tensor_scalar(SCR[0:1, 16:17], SCR[0:1, 15:16], 1.0 / QMAX,
                                    1e-12, ALU.mult, ALU.max)

            with nc.Block() as blk:
                @blk.scalar
                def _(s):
                    s.activation(SCR[0:1, 17:18], SCR[0:1, 16:17], ACT.Ln)
            with nc.Block() as blk:
                @blk.scalar
                def _(s):
                    s.activation(SCR[0:1, 18:19], SCR[0:1, 17:18], ACT.Exp, scale=-1.0)

            with nc.Block() as blk:
                @blk.sync
                def _(sync):
                    s_in.dma(sync.dma_start(S_sm[0:1, 0:3], SCR[0:1, 16:19]))

                @blk.gpsimd
                def _(g):
                    s_in.wait(g)
                    s_g.dma(g.dma_start(B["s0"][:], S_sm[0:1, 0:1]
                                        .partition_broadcast(128).squeeze(1)))
                    s_g.dma(g.dma_start(B["invs0"][:], S_sm[0:1, 2:3]
                                        .partition_broadcast(128).squeeze(1)))
                    g.wait_ge(s_g.sem, s_g.v)


    if "scales" in debug:
        dbg["scales"] = nc.dram_tensor("dbg_scales", [1, 16], F32, kind="ExternalOutput")
        with nc.Block() as blk:
            @blk.vector
            def _(v):
                for j, n in enumerate(BNAMES):
                    v.tensor_copy(SC1[0:1, j:j + 1], B[n][0:1, 0:1])
                vfence(v, s_v)

            @blk.gpsimd
            def _(g):
                s_g.dma(g.dma_start(dbg["scales"][:], SC1[0:1, 0:16]))
                g.wait_ge(s_g.sem, s_g.v)

    s_sp = Sem(nc, "s_sp")
    s_lo = Sem(nc, "s_lo")
    if upto >= 2:
        with nc.Block() as blk:
            @blk.sync
            def _(sync):
                s_in.dma(sync.dma_start(FS[0:NH, 0:3 * D], wi_in[:]))
                s_in.dma(sync.dma_start(FS[0:NH, 3 * D:6 * D], wf_in[:]))
                s_in.dma(sync.dma_start(BIB[:], bi_in[:]))
                s_in.dma(sync.dma_start(BFB[:], bf_in[:]))

            @blk.vector
            def _(v):
                s_in.wait(v)
                v.tensor_add(FS[0:NH, 6 * D:7 * D], FS[0:NH, 0:D], FS[0:NH, D:2 * D])
                v.tensor_add(FS[0:NH, 7 * D:8 * D], FS[0:NH, 3 * D:4 * D],
                             FS[0:NH, 4 * D:5 * D])
                vfence(v, s_v)
                v.tensor_add(FS[0:NH, 6 * D:7 * D], FS[0:NH, 6 * D:7 * D],
                             FS[0:NH, 2 * D:3 * D])
                v.tensor_add(FS[0:NH, 7 * D:8 * D], FS[0:NH, 7 * D:8 * D],
                             FS[0:NH, 5 * D:6 * D])

        with (
            nc.psum_tensor([128, 8], F32) as pws,
            nc.Block() as blk,
        ):
            p0 = s_pe.v
            a0 = s_a.v
            v0 = s_v.v

            @blk.tensor
            def _(t):
                for c in range(NDC):
                    if c >= 1:
                        t.wait_ge(s_v.sem, v0 + c)
                    t.transpose(pws[:, 0:4], FS[0:NH, 6 * D + c * 128:6 * D + (c + 1) * 128],
                                IDENT[0:NH, 0:NH])
                    s_pe.inc(t.transpose(pws[:, 4:8],
                                         FS[0:NH, 7 * D + c * 128:7 * D + (c + 1) * 128],
                                         IDENT[0:NH, 0:NH]))

            @blk.scalar
            def _(s):
                for c in range(NDC):
                    s.wait_ge(s_pe.sem, p0 + c + 1)
                    s.copy(WSB[:, c * 4:(c + 1) * 4], pws[:, 0:4])
                    s_a.inc(s.copy(WSB[:, 32 + c * 4:32 + (c + 1) * 4], pws[:, 4:8]))

            @blk.vector
            def _(v):
                for c in range(NDC):
                    v.wait_ge(s_a.sem, a0 + c + 1)
                    v.tensor_sub(WSB[:, 64 + c * 4:64 + (c + 1) * 4], pws[:, 0:4],
                                 WSB[:, c * 4:(c + 1) * 4])
                    s_v.inc(v.tensor_sub(WSB[:, 96 + c * 4:96 + (c + 1) * 4], pws[:, 4:8],
                                         WSB[:, 32 + c * 4:32 + (c + 1) * 4]))

        with (
            nc.psum_tensor([128, 512], F32) as qp0,
            nc.psum_tensor([128, 512], F32) as qp1,
            nc.psum_tensor([128, 512], F32) as kp0,
            nc.psum_tensor([128, 512], F32) as kp1,
            nc.Block() as blk,
        ):
            qps = [qp0, qp1]
            kps = [kp0, kp1]
            in0 = s_in.v
            p0 = s_pe.v
            a0 = s_a.v
            g0 = s_g.v
            sp0 = s_sp.v
            lo0 = s_lo.v

            @blk.sync
            def _(sync):
                for c in range(NDC):
                    if c >= 1:
                        sync.wait_ge(s_sp.sem, sp0 + c)
                    s_in.dma(sync.dma_start(FS[:, 0:128], wqd_in[c]))
                    s_in.dma(sync.dma_start(FS[:, 128:256], wkd_in[c]))

            @blk.vector
            def _(v):
                for c in range(NDC):
                    wb = 512 * (c % 2)
                    v.wait_ge(s_in.sem, in0 + 32 * (c + 1))
                    if c >= 2:
                        v.wait_ge(s_pe.sem, p0 + NT5 * (c - 1))
                    v.tensor_copy(WQKL[:, wb:wb + 128], FS[:, 0:128])
                    v.tensor_copy(WQKL[:, wb + 256:wb + 384], FS[:, 128:256])
                    vfence(v, s_v)
                    v.tensor_sub(WQKL[:, wb + 128:wb + 256], FS[:, 0:128], WQKL[:, wb:wb + 128])
                    s_sp.inc(v.tensor_sub(WQKL[:, wb + 384:wb + 512], FS[:, 128:256],
                                          WQKL[:, wb + 256:wb + 384]))
                    for t_ in range(NT5):
                        k = c * NT5 + t_
                        st = 2048 * (k % 2)
                        v.wait_ge(s_a.sem, a0 + 2 * (k + 1))
                        v.scalar_tensor_tensor(BSC[:, st + 512:st + 1024], qps[k % 2][:],
                                               1.0, BSC[:, st:st + 512],
                                               ALU.mult, ALU.subtract)
                        s_lo.inc(v.scalar_tensor_tensor(
                            BSC[:, st + 1536:st + 2048], kps[k % 2][:], 0.0625,
                            BSC[:, st + 1024:st + 1536], ALU.mult, ALU.subtract))

            @blk.tensor
            def _(t):
                for c in range(NDC):
                    for t_ in range(NT5):
                        k = c * NT5 + t_
                        if t_ == 0:
                            t.wait_ge(s_sp.sem, sp0 + c + 1)
                        if k >= 2:
                            t.wait_ge(s_lo.sem, lo0 + k - 1)
                        wb = 512 * (c % 2)
                        xh = XH[:, c * T + t_ * 512:c * T + (t_ + 1) * 512]
                        xl = XL[:, c * T + t_ * 512:c * T + (t_ + 1) * 512]
                        t.matmul(qps[k % 2][:], WQKL[:, wb:wb + 128], xh, start=True, stop=False)
                        t.matmul(qps[k % 2][:], WQKL[:, wb:wb + 128], xl, start=False, stop=False)
                        t.matmul(qps[k % 2][:], WQKL[:, wb + 128:wb + 256], xh, start=False, stop=True)
                        t.matmul(kps[k % 2][:], WQKL[:, wb + 256:wb + 384], xh, start=True, stop=False)
                        t.matmul(kps[k % 2][:], WQKL[:, wb + 256:wb + 384], xl, start=False, stop=False)
                        s_pe.inc(t.matmul(kps[k % 2][:], WQKL[:, wb + 384:wb + 512], xh,
                                          start=False, stop=True))

            sg2b = [sg2[0].v, sg2[1].v]

            @blk.scalar
            def _(s):
                for c in range(NDC):
                    for t_ in range(NT5):
                        k = c * NT5 + t_
                        st = 2048 * (k % 2)
                        s.wait_ge(s_pe.sem, p0 + k + 1)
                        if k >= 2:
                            s.wait_ge(sg2[k % 2].sem,
                                      sg2b[k % 2] + 64 * ((k - 2) // 2 + 1))
                        s.copy(BSC[:, st:st + 512], qps[k % 2][:])
                        s_a.inc(s.activation(BSC[:, st + 1024:st + 1536], kps[k % 2][:],
                                             ACT.Identity, scale=0.0625), 2)

            @blk.gpsimd
            def _(g):
                for c in range(NDC):
                    for t_ in range(NT5):
                        k = c * NT5 + t_
                        st = 2048 * (k % 2)
                        g.wait_ge(s_lo.sem, lo0 + k + 1)
                        rows = slice(c * 128, (c + 1) * 128)
                        cols = slice(t_ * 512, (t_ + 1) * 512)
                        sg2[k % 2].dma(g.dma_start(S_qh[rows, cols], BSC[:, st:st + 512]))
                        sg2[k % 2].dma(g.dma_start(S_ql[rows, cols], BSC[:, st + 512:st + 1024]))
                        sg2[k % 2].dma(g.dma_start(S_kh[rows, cols], BSC[:, st + 1024:st + 1536]))
                        sg2[k % 2].dma(g.dma_start(S_kl[rows, cols], BSC[:, st + 1536:st + 2048]))

        with (
            nc.psum_tensor([128, 128], F32) as vp0,
            nc.psum_tensor([128, 128], F32) as vp1,
            nc.Block() as blk,
        ):
            vps = [vp0, vp1]
            in0 = s_in.v
            p0 = s_pe.v
            a0 = s_a.v
            g0 = s_g.v
            sp0 = s_sp.v

            @blk.sync
            def _(sync):
                for c in range(NDC):
                    if c >= 1:
                        sync.wait_ge(s_sp.sem, sp0 + c)
                    s_in.dma(sync.dma_start(FS[:, 0:128], wvd_in[c]))

            @blk.vector
            def _(v):
                for c in range(NDC):
                    wb = 512 * (c % 2)
                    v.wait_ge(s_in.sem, in0 + 16 * (c + 1))
                    if c >= 2:
                        v.wait_ge(s_pe.sem, p0 + NT * (c - 1))
                    v.tensor_copy(WQKL[:, wb:wb + 128], FS[:, 0:128])
                    vfence(v, s_v)
                    s_sp.inc(v.tensor_sub(WQKL[:, wb + 128:wb + 256], FS[:, 0:128],
                                          WQKL[:, wb:wb + 128]))

            @blk.tensor
            def _(t):
                for c in range(NDC):
                    for ts_ in range(NT):
                        k = c * NT + ts_
                        if ts_ == 0:
                            t.wait_ge(s_sp.sem, sp0 + c + 1)
                        if k >= 2:
                            t.wait_ge(s_a.sem, a0 + k - 1)
                        wb = 512 * (c % 2)
                        xh = XH[:, c * T + ts_ * 128:c * T + (ts_ + 1) * 128]
                        xl = XL[:, c * T + ts_ * 128:c * T + (ts_ + 1) * 128]
                        t.matmul(vps[k % 2][:], xh, WQKL[:, wb:wb + 128], start=True, stop=False)
                        t.matmul(vps[k % 2][:], xh, WQKL[:, wb + 128:wb + 256], start=False, stop=False)
                        s_pe.inc(t.matmul(vps[k % 2][:], xl, WQKL[:, wb:wb + 128],
                                          start=False, stop=True))

            sg2b = [sg2[0].v, sg2[1].v]

            @blk.scalar
            def _(s):
                for c in range(NDC):
                    for ts_ in range(NT):
                        k = c * NT + ts_
                        st = 1024 + 128 * (k % 2)
                        s.wait_ge(s_pe.sem, p0 + k + 1)
                        if k >= 2:
                            s.wait_ge(sg2[k % 2].sem,
                                      sg2b[k % 2] + 16 * ((k - 2) // 2 + 1))
                        s_a.inc(s.copy(FS[:, st:st + 128], vps[k % 2][:]))

            @blk.gpsimd
            def _(g):
                for c in range(NDC):
                    for ts_ in range(NT):
                        k = c * NT + ts_
                        st = 1024 + 128 * (k % 2)
                        g.wait_ge(s_a.sem, a0 + k + 1)
                        h = c // 2
                        half = c % 2
                        sg2[k % 2].dma(g.dma_start(
                            S_v[h, ts_ * 128:(ts_ + 1) * 128, half * 128:(half + 1) * 128],
                            FS[:, st:st + 128]))

        with (
            nc.psum_tensor([NH, 512], F32) as gi0,
            nc.psum_tensor([NH, 512], F32) as gf0,
            nc.psum_tensor([NH, 512], F32) as gi1,
            nc.psum_tensor([NH, 512], F32) as gf1,
            nc.Block() as blk,
        ):
            gis = [gi0, gi1]
            gfs = [gf0, gf1]
            p0 = s_pe.v
            a0 = s_a.v

            @blk.tensor
            def _(t):
                for t_ in range(NT5):
                    if t_ >= 2:
                        t.wait_ge(s_a.sem, a0 + t_ - 1)
                    for c in range(NDC):
                        xh = XH[:, c * T + t_ * 512:c * T + (t_ + 1) * 512]
                        xl = XL[:, c * T + t_ * 512:c * T + (t_ + 1) * 512]
                        st0 = (c == 0)
                        wih = WSB[:, c * 4:(c + 1) * 4]
                        wil = WSB[:, 64 + c * 4:64 + (c + 1) * 4]
                        wfh = WSB[:, 32 + c * 4:32 + (c + 1) * 4]
                        wfl = WSB[:, 96 + c * 4:96 + (c + 1) * 4]
                        t.matmul(gis[t_ % 2][:], wih, xh, start=st0, stop=False)
                        t.matmul(gis[t_ % 2][:], wih, xl, start=False, stop=False)
                        t.matmul(gis[t_ % 2][:], wil, xh, start=False,
                                 stop=(c == NDC - 1))
                        t.matmul(gfs[t_ % 2][:], wfh, xh, start=st0, stop=False)
                        t.matmul(gfs[t_ % 2][:], wfh, xl, start=False, stop=False)
                        last = t.matmul(gfs[t_ % 2][:], wfl, xh, start=False,
                                        stop=(c == NDC - 1))
                    s_pe.inc(last)

            @blk.scalar
            def _(s):
                for t_ in range(NT5):
                    s.wait_ge(s_pe.sem, p0 + t_ + 1)
                    s.activation(GV[:, t_ * 512:(t_ + 1) * 512], gis[t_ % 2][:],
                                 ACT.Identity, bias=BIB[0:NH, 0:1])
                    s_a.inc(s.activation(GF[0:NH, t_ * 512:(t_ + 1) * 512], gfs[t_ % 2][:],
                                         ACT.Identity, bias=BFB[0:NH, 0:1]))

        with nc.Block() as blk:
            @blk.scalar
            def _(s):
                s.activation(FS[0:NH, 0:T], GF[0:NH, :], ACT.Exp, scale=-1.0)
                vfence(s, s_a)
                s.activation(GF[0:NH, :], FS[0:NH, 0:T], ACT.Ln, bias=1.0)

        FS0 = FS[0:NH, 0:T]
        with nc.Block() as blk:
            @blk.vector
            def _(v):
                v.tensor_tensor_scan(FS0, GF[0:NH, :], GF[0:NH, :], 0.0,
                                     ALU.add, ALU.bypass)
                vfence(v, s_v)
                v.tensor_add(GV[:, :], GV[:, :], FS0)
                vfence(v, s_v)
                v.tensor_tensor_scan(GF[0:NH, :], GV[:, :], GV[:, :], -1e30,
                                     ALU.max, ALU.bypass)
                vfence(v, s_v)
                v.tensor_sub(FS0, FS0, GF[0:NH, :])

        with (
            nc.psum_tensor([128, 8], F32) as pgt,
            nc.Block() as blk,
        ):
            p0 = s_pe.v
            a0 = s_a.v

            @blk.tensor
            def _(t):
                for j in range(NT):
                    if j >= 1:
                        t.wait_ge(s_a.sem, a0 + j)
                    t.transpose(pgt[:, 0:4], GV[:, j * 128:(j + 1) * 128], IDENT[0:NH, 0:NH])
                    s_pe.inc(t.transpose(pgt[:, 4:8], FS0[:, j * 128:(j + 1) * 128],
                                         IDENT[0:NH, 0:NH]))

            @blk.scalar
            def _(s):
                for j in range(NT):
                    s.wait_ge(s_pe.sem, p0 + j + 1)
                    s.copy(CT[:, j * 4:(j + 1) * 4], pgt[:, 0:4])
                    s_a.inc(s.copy(ET[:, j * 4:(j + 1) * 4], pgt[:, 4:8]))

        with nc.Block() as blk:
            @blk.scalar
            def _(s):
                s.activation(ET[:], ET[:], ACT.Exp)

            @blk.gpsimd
            def _(g):
                s_g.dma(g.dma_start(
                    S_sm[8:24, :].rearrange("(a b) c -> a (b c)", b=4), GF[0:NH, :]))
                g.wait_ge(s_g.sem, s_g.v)

    if upto >= 2 and ("q" in debug or "k" in debug or "v" in debug or "gates" in debug):
        with nc.Block() as blk:
            @blk.gpsimd
            def _(g):
                g.wait_ge(s_g.sem, s_g.v)

        for nm, shi, slo in (("q", S_qh, S_ql), ("k", S_kh, S_kl)):
            if nm not in debug:
                continue
            if nm not in dbg:
                dbg[nm] = nc.dram_tensor("dbg_" + nm, [D, T], F32, kind="ExternalOutput")
            dd = dbg[nm]
            for c in range(NDC):
                with nc.Block() as blk:
                    @blk.sync
                    def _(sync, c=c, shi=shi, slo=slo):
                        s_in.dma(sync.dma_start(BSC[:, 0:4096], shi[c * 128:(c + 1) * 128, :]))

                    @blk.vector
                    def _(v, c=c):
                        s_in.wait(v)
                        v.tensor_copy(FS[:, 0:4096], BSC[:, 0:4096])
                with nc.Block() as blk:
                    @blk.sync
                    def _(sync, c=c, shi=shi, slo=slo):
                        s_in.dma(sync.dma_start(BSC[:, 0:4096], slo[c * 128:(c + 1) * 128, :]))

                    @blk.vector
                    def _(v, c=c):
                        s_in.wait(v)
                        v.tensor_add(FS[:, 0:4096], FS[:, 0:4096], BSC[:, 0:4096])
                with nc.Block() as blk:
                    @blk.gpsimd
                    def _(g, c=c, dd=dd):
                        s_g.dma(g.dma_start(dd[c * 128:(c + 1) * 128, :], FS[:, 0:4096]))
                        g.wait_ge(s_g.sem, s_g.v)

        if "v" in debug:
            dbg["v"] = nc.dram_tensor("dbg_v", [NH * T, DH], F32, kind="ExternalOutput")
            for h in range(NH):
                for j in range(8):
                    with nc.Block() as blk:
                        @blk.sync
                        def _(sync, h=h, j=j):
                            s_in.dma(sync.dma_start(
                                FS[:, 0:1024].rearrange("p (a b) -> p a b", a=4, b=256),
                                S_v[h, j * 512:(j + 1) * 512, :]
                                .rearrange("(a p) b -> p a b", p=128)))

                        @blk.gpsimd
                        def _(g, h=h, j=j):
                            s_in.wait(g)
                            s_g.dma(g.dma_start(
                                dbg["v"][h * T + j * 512:h * T + (j + 1) * 512, :]
                                .rearrange("(a p) b -> p a b", p=128),
                                FS[:, 0:1024].rearrange("p (a b) -> p a b", a=4, b=256)))
                            g.wait_ge(s_g.sem, s_g.v)

        if "gates" in debug:
            dbg["gates"] = nc.dram_tensor("dbg_gates", [NH, 3 * T], F32, kind="ExternalOutput")
            with nc.Block() as blk:
                @blk.gpsimd
                def _(g):
                    s_g.dma(g.dma_start(dbg["gates"][:, 0:T], GV[:, :]))
                    s_g.dma(g.dma_start(dbg["gates"][:, T:2 * T], GF[0:NH, :]))
                    s_g.dma(g.dma_start(dbg["gates"][:, 2 * T:3 * T], FS[0:NH, 0:T]))
                    g.wait_ge(s_g.sem, s_g.v)

    if upto >= 3:
        with nc.Block() as blk:
            @blk.sync
            def _(sync):
                s_in.dma(sync.dma_start(FS[0:1, 0:D], lnw_in[:]))
                for j in range(4):
                    s_in.dma(sync.dma_start(MASKS[:, j * 512:(j + 1) * 512], masks_in[j]))

            @blk.vector
            def _(v):
                s_in.wait(v)
                v.memset(ONESR[:], 1.0)
                v.memset(ATST[:, 0:512], 0.0)
                v.tensor_scalar(FS[0:1, D:2 * D], FS[0:1, 0:D], 1.0, None, ALU.add)

        with nc.Block() as blk:
            @blk.gpsimd
            def _(g):
                s_g.dma(g.dma_start(S_sm[1:2, 0:D], FS[0:1, D:2 * D]))
                s_g.dma(g.dma_start(S_st[:, :], ATST[:, 0:512]))
                g.wait_ge(s_g.sem, s_g.v)

        for h in range(_LIM[0]):
            with nc.Block() as blk:
                @blk.sync
                def _(sync, h=h):
                    for dc in range(2):
                        r0 = h * DH + dc * 128
                        s_in.dma(sync.dma_start(XH[:, dc * T:(dc + 1) * T],
                                                S_qh[r0:r0 + 128, :]))
                        s_in.dma(sync.dma_start(XH[:, 2 * T + dc * T:2 * T + (dc + 1) * T],
                                                S_ql[r0:r0 + 128, :]))
                        s_in.dma(sync.dma_start(XL[:, dc * T:(dc + 1) * T],
                                                S_kh[r0:r0 + 128, :]))
                        s_in.dma(sync.dma_start(XL[:, 2 * T + dc * T:2 * T + (dc + 1) * T],
                                                S_kl[r0:r0 + 128, :]))
                @blk.gpsimd
                def _(g, h=h):
                    for ts_ in range(NT):
                        s_g.dma(g.dma_start(FS[:, ts_ * 257:ts_ * 257 + 256],
                                            S_v[h, ts_ * 128:(ts_ + 1) * 128, :]))

                @blk.vector
                def _(v, h=h):
                    s_in.wait(v)
                    s_g.wait(v)
                    v.memset(FS[:, 256:8224:257], 1.0)

            def qh_(dc, a, b):
                return XH[:, dc * T + a:dc * T + b]

            def ql_(dc, a, b):
                return XH[:, 2 * T + dc * T + a:2 * T + dc * T + b]

            def kh_(dc, a, b):
                return XL[:, dc * T + a:dc * T + b]

            def kl_(dc, a, b):
                return XL[:, 2 * T + dc * T + a:2 * T + dc * T + b]

            with (
                nc.psum_tensor([128, 512], F32) as qkp0,
                nc.psum_tensor([128, 512], F32) as qkp1,
                nc.psum_tensor([128, 512], F32) as o2a,
                nc.psum_tensor([128, 512], F32) as o2b,
                nc.psum_tensor([128, 512], F32) as o2c,
                nc.psum_tensor([128, 512], F32) as o2d,
                nc.Block() as blk,
            ):
                qkps = [qkp0, qkp1]
                o2 = [o2a, o2b, o2c, o2d]
                P = {}
                pp, ap, vp, gp = s_pe.v, s_a.v, s_v.v, s_g.v
                ip = s_em.v
                for sc in range(_LIM[1]):
                    ntt = 4 * sc + 4
                    ip += 16; P[("mbc", sc)] = ip
                    for tt in range(ntt):
                        pp += 1; P[("qk", sc, tt)] = pp
                        ap += 1; P[("w", sc, tt)] = ap
                        vp += 1; P[("ct", sc, tt)] = vp
                        pp += 1; P[("cv", sc, tt)] = pp
                    vp += 1; P[("v_sc4", sc)] = vp
                    vp += 1; P[("v_mh", sc)] = vp
                    ap += 1; P[("a_sq", sc)] = ap
                    vp += 1; P[("v_q", sc)] = vp
                    gp += 64; P[("st", sc)] = gp

                @blk.sync
                def _(sync, h=h, P=P):
                    for sc in range(_LIM[1]):
                        row = 8 + 4 * h + sc // 2
                        cs = 512 * (sc % 2)
                        if sc >= 1:
                            sync.wait_ge(s_g.sem, P[("st", sc - 1)])
                            sync.wait_ge(s_v.sem, P[("ct", sc - 1, 4 * sc - 1)])
                        s_em.dma(sync.dma_start(
                            ATST[:, 0:512], S_sm[row:row + 1, cs:cs + 512]
                            .partition_broadcast(128).squeeze(1)))

                @blk.tensor
                def _(t, h=h, P=P):
                    for sc in range(_LIM[1]):
                        ntt = 4 * sc + 4
                        for tt in range(ntt):
                            qb = qkps[tt % 2]
                            prev = (sc, tt - 2) if tt >= 2 else (
                                (sc - 1, 4 * sc - 2 + tt) if sc >= 1 else None)
                            if prev is not None and "ct" not in _DIS:
                                t.wait_ge(s_v.sem, P[("ct",) + prev])
                            a_ = tt * 128
                            last = None
                            if "qk" not in _DIS:
                                for dc in range(2):
                                    t.matmul(qb[:], kh_(dc, a_, a_ + 128),
                                             qh_(dc, sc * 512, (sc + 1) * 512),
                                             start=(dc == 0), stop=False)
                                    t.matmul(qb[:], kh_(dc, a_, a_ + 128),
                                             ql_(dc, sc * 512, (sc + 1) * 512),
                                             start=False, stop=False)
                                    last = t.matmul(qb[:], kl_(dc, a_, a_ + 128),
                                                    qh_(dc, sc * 512, (sc + 1) * 512),
                                                    start=False, stop=(dc == 1))
                                s_pe.inc(last)
                            else:
                                s_pe.inc(t.nop())
                            if "ct" not in _DIS:
                                t.wait_ge(s_v.sem, P[("ct", sc, tt)])
                            if tt == 0 and sc >= 1 and "ep" not in _DIS:
                                t.wait_ge(s_v.sem, P[("v_mh", sc - 1)])
                            ctb = CTA[:, 512 * (tt % 2):512 * (tt % 2) + 512]
                            if "cv" not in _DIS:
                                for ss in range(4):
                                    last = t.matmul(o2[ss][:, 0:257], ctb[:, ss * 128:(ss + 1) * 128],
                                                    FS[:, tt * 257:(tt + 1) * 257],
                                                    start=(tt == 0), stop=(tt == ntt - 1))
                                s_pe.inc(last)
                            else:
                                s_pe.inc(t.nop())

                @blk.scalar
                def _(s, h=h, P=P):
                    for sc in range(_LIM[1]):
                        ntt = 4 * sc + 4
                        s.wait_ge(s_em.sem, P[("mbc", sc)])
                        for tt in range(ntt):
                            prev2 = (sc, tt - 2) if tt >= 2 else (
                                (sc - 1, 4 * sc - 2 + tt) if sc >= 1 else None)
                            if prev2 is not None and "cv" not in _DIS:
                                s.wait_ge(s_pe.sem, P[("cv",) + prev2])
                            if sc >= 1 and tt < 2 and "ep" not in _DIS:
                                s.wait_ge(s_v.sem, P[("v_q", sc - 1)])
                            if tt == 0:
                                s.copy(CFB[0][:], CT[:, h:h + 1])
                            if tt + 1 < ntt:
                                c1 = h + 4 * (tt + 1)
                                s.copy(CFB[(tt + 1) % 2][:], CT[:, c1:c1 + 1])
                            vfence(s)
                            ctb = CTA[:, 512 * (tt % 2):512 * (tt % 2) + 512]
                            s_a.inc(s.activation(ctb, ATST[:, 0:512], ACT.Exp,
                                                 scale=-1.0,
                                                 bias=CFB[tt % 2][:, 0:1]))
                        if "ep" in _DIS:
                            s_a.inc(s.nop())
                            continue
                        s.wait_ge(s_v.sem, P[("v_mh", sc)])
                        s_a.inc(s.activation(ATS2[:], ATST[:], ACT.Square))


                @blk.vector
                def _(v, h=h, P=P):
                    for sc in range(_LIM[1]):
                        ntt = 4 * sc + 4
                        for tt in range(ntt):
                            qb = qkps[tt % 2]
                            if "ct" in _DIS:
                                s_v.inc(v.nop())
                                continue
                            v.wait_ge(s_pe.sem, P[("qk", sc, tt)])
                            v.wait_ge(s_a.sem, P[("w", sc, tt)])
                            ctb = CTA[:, 512 * (tt % 2):512 * (tt % 2) + 512]
                            instr = v.tensor_mul(ctb, ctb, qb[:])
                            if tt >= 4 * sc and "msk" not in _DIS:
                                j = tt - 4 * sc
                                vfence(v)
                                instr = v.tensor_mul(ctb, ctb,
                                                     MASKS[:, j * 512:(j + 1) * 512])
                            s_v.inc(instr)
                        if "ep" in _DIS:
                            for _k in range(3):
                                s_v.inc(v.nop())
                            continue
                        if "cv" not in _DIS:
                            v.wait_ge(s_pe.sem, P[("cv", sc, ntt - 1)])
                        if sc >= 1:
                            v.wait_ge(s_g.sem, P[("st", sc - 1)])
                        for ss in range(4):
                            ins = v.tensor_reduce(SC4[:, ss:ss + 1], o2[ss][:, 256:257],
                                                  AXX, ALU.max, apply_absolute_value=True)
                        s_v.inc(ins)
                        for ss in range(4):
                            v.tensor_copy(ATST[:, ss * 256:(ss + 1) * 256],
                                          o2[ss][:, 0:256])
                        for ss in range(4):
                            ins = v.tensor_reduce(MHQ[:, ss:ss + 1],
                                                  ATST[:, ss * 256:(ss + 1) * 256],
                                                  AXX, ALU.add)
                        s_v.inc(ins)
                        v.wait_ge(s_a.sem, P[("a_sq", sc)])
                        for ss in range(4):
                            ins = v.tensor_reduce(QQ4[:, ss:ss + 1],
                                                  ATS2[:, ss * 256:(ss + 1) * 256],
                                                  AXX, ALU.add)
                        s_v.inc(ins)

                @blk.gpsimd
                def _(g, h=h, P=P):
                    for sc in range(_LIM[1]):
                        if "ep" in _DIS:
                            s_g.v += 64
                            g.sem_inc(s_g.sem, 64)
                            continue
                        co = h * 32 + sc * 4
                        g.wait_ge(s_v.sem, P[("v_sc4", sc)])
                        s_g.dma(g.dma_start(S_st[:, co:co + 4], SC4[:]))
                        g.wait_ge(s_v.sem, P[("v_mh", sc)])
                        s_g.dma(g.dma_start(S_st[:, 128 + co:128 + co + 4], MHQ[:]))
                        s_g.dma(g.dma_start(
                            S_attn[h, sc * 512:(sc + 1) * 512, :]
                            .rearrange("(a p) b -> p a b", p=128),
                            ATST[:].rearrange("p (a b) -> p a b", a=4, b=256)))
                        g.wait_ge(s_v.sem, P[("v_q", sc)])
                        s_g.dma(g.dma_start(S_st[:, 256 + co:256 + co + 4], QQ4[:]))

        STT = FS[:, 0:384]
        N1 = FS[:, 384:512]
        T1 = FS[:, 512:640]
        M2T = FS[:, 640:768]
        W1 = FS[:, 768:896]
        W3 = FS[:, 896:1024]
        W4 = FS[:, 1024:1152]
        WT = FS[:, 1152:1280]
        LT = FS[:, 1280:1408]
        RT = FS[:, 1408:1536]
        with nc.Block() as blk:
            @blk.sync
            def _(sync):
                s_in.dma(sync.dma_start(STT, S_st[:, 0:384]))

            @blk.vector
            def _(v):
                s_in.wait(v)
                for h in range(_LIM[0]):
                    v.tensor_max(N1[:, h * 32:h * 32 + 32],
                                 STT[:, h * 32:h * 32 + 32], ET[:, h:h + 125:4])
                v.tensor_mul(W1[:], STT[:, 128:256], STT[:, 128:256])
        with nc.Block() as blk:
            @blk.vector
            def _(v):
                v.tensor_scalar(T1[:], N1[:], 1e-8, None, ALU.add)
                v.tensor_scalar(W3[:], STT[:, 256:384], float(DH), None, ALU.mult)
        with nc.Block() as blk:
            @blk.vector
            def _(v):
                v.tensor_mul(M2T[:], T1[:], T1[:])
                v.tensor_sub(W4[:], W3[:], W1[:])
        with nc.Block() as blk:
            @blk.vector
            def _(v):
                v.tensor_scalar(M2T[:], M2T[:], 1e-5 * float(DH) * float(DH),
                                None, ALU.mult)
        with nc.Block() as blk:
            @blk.vector
            def _(v):
                v.tensor_add(WT[:], W4[:], M2T[:])
        with nc.Block() as blk:
            @blk.scalar
            def _(s):
                s.activation(LT[:], WT[:], ACT.Ln)
        with nc.Block() as blk:
            @blk.scalar
            def _(s):
                s.activation(RT[:], LT[:], ACT.Exp, scale=-0.5)
        with nc.Block() as blk:
            @blk.vector
            def _(v):
                v.tensor_scalar(ABT[:, 0:128], RT[:], float(DH), None, ALU.mult)
                v.tensor_mul(ABT[:, 128:256], STT[:, 128:256], RT[:])
        with nc.Block() as blk:
            @blk.vector
            def _(v):
                v.tensor_scalar(ABT[:, 128:256], ABT[:, 128:256], -1.0, None,
                                ALU.mult)

    if "ep0" in debug:
        dbg["ep0"] = nc.dram_tensor("dbg_ep0", [128, 1024 + 64], F32, kind="ExternalOutput")
        dbg["ctd"] = nc.dram_tensor("dbg_ctd", [128, 1024 + 512 + 32], F32, kind="ExternalOutput")
        dbg["qkl"] = nc.dram_tensor("dbg_qkl", [128, 4 * 512], BF16, kind="ExternalOutput")
        with nc.Block() as blk:
            @blk.gpsimd
            def _(g):
                s_g.dma(g.dma_start(dbg["qkl"][:, 0:512], XH[:, 0:512]))
                s_g.dma(g.dma_start(dbg["qkl"][:, 512:1024], XH[:, 2 * T:2 * T + 512]))
                s_g.dma(g.dma_start(dbg["qkl"][:, 1024:1536], XL[:, 0:512]))
                s_g.dma(g.dma_start(dbg["qkl"][:, 1536:2048], XL[:, 2 * T:2 * T + 512]))
                s_g.dma(g.dma_start(dbg["ctd"][:, 0:1024], CTA[:]))
                s_g.dma(g.dma_start(dbg["ctd"][:, 1024:1280], ABT[:]))
                s_g.dma(g.dma_start(dbg["ctd"][:, 1280:1408], WT))
                s_g.dma(g.dma_start(dbg["ctd"][:, 1408:1536], RT))
                s_g.dma(g.dma_start(dbg["ep0"][:, 0:1024], ATST[:]))
                s_g.dma(g.dma_start(dbg["ep0"][:, 1024:1028], SC4[:]))
                s_g.dma(g.dma_start(dbg["ep0"][:, 1028:1032], MHQ[:]))
                s_g.dma(g.dma_start(dbg["ep0"][:, 1032:1036], QQ4[:]))
                g.wait_ge(s_g.sem, s_g.v)

    if upto >= 3 and "attnln" in debug:
        dbg["attnln"] = nc.dram_tensor("dbg_attnln", [NH * T, DH], F32, kind="ExternalOutput")
        for h in range(NH):
            for j in range(8):
                with nc.Block() as blk:
                    @blk.sync
                    def _(sync, h=h, j=j):
                        s_in.dma(sync.dma_start(
                            FS[:, 0:1024].rearrange("p (a b) -> p a b", a=4, b=256),
                            S_attn[h, j * 512:(j + 1) * 512, :]
                            .rearrange("(a p) b -> p a b", p=128)))

                    @blk.gpsimd
                    def _(g, h=h, j=j):
                        s_in.wait(g)
                        s_g.dma(g.dma_start(
                            dbg["attnln"][h * T + j * 512:h * T + (j + 1) * 512, :]
                            .rearrange("(a p) b -> p a b", p=128),
                            FS[:, 0:1024].rearrange("p (a b) -> p a b", a=4, b=256)))
                        g.wait_ge(s_g.sem, s_g.v)

    if "xnlimbs" in debug:
        dbg["xnh"] = nc.dram_tensor("dbg_xnh", [D, T], F32, kind="ExternalOutput")
        dbg["xnl"] = nc.dram_tensor("dbg_xnl", [D, T], F32, kind="ExternalOutput")
        dh_, dl_ = dbg["xnh"], dbg["xnl"]
        with (
            nc.sbuf_tensor([128, T], F32) as rh,
            nc.sbuf_tensor([128, T], F32) as rl,
            nc.Block() as blk,
        ):
            v0 = s_v.v
            g0 = s_g.v

            @blk.vector
            def _(v):
                for c in range(NDC):
                    if c >= 1:
                        v.wait_ge(s_g.sem, g0 + 32 * c)
                    v.tensor_copy(rh[:], XH[:, c * T:(c + 1) * T])
                    s_v.inc(v.tensor_copy(rl[:], XL[:, c * T:(c + 1) * T]))

            @blk.gpsimd
            def _(g):
                for c in range(NDC):
                    g.wait_ge(s_v.sem, v0 + c + 1)
                    s_g.dma(g.dma_start(dh_[c * 128:(c + 1) * 128, :], rh[:]))
                    s_g.dma(g.dma_start(dl_[c * 128:(c + 1) * 128, :], rl[:]))
                g.wait_ge(s_g.sem, s_g.v)

    if "stats" in debug:
        dbg["stats"] = nc.dram_tensor("dbg_stats", [128, 4 * NT], F32, kind="ExternalOutput")
        with nc.Block() as blk:
            @blk.gpsimd
            def _(g):
                s_g.dma(g.dma_start(dbg["stats"][:, 0:NT], MU[:]))
                s_g.dma(g.dma_start(dbg["stats"][:, NT:2 * NT], RINV[:]))
                s_g.dma(g.dma_start(dbg["stats"][:, 2 * NT:3 * NT], NU[:]))
                s_g.dma(g.dma_start(dbg["stats"][:, 3 * NT:4 * NT], RABS[:]))
                g.wait_ge(s_g.sem, s_g.v)

    if "xn" in debug:
        dxn = dbg_out("xn", [D, T])
        with (
            nc.sbuf_tensor([128, T], F32) as rec0,
            nc.sbuf_tensor([128, T], F32) as rec1,
            nc.Block() as blk,
        ):
            recs = [rec0, rec1]
            v0 = s_v.v
            g0 = s_g.v

            @blk.vector
            def _(v):
                for c in range(NDC):
                    if c >= 2:
                        v.wait_ge(s_g.sem, g0 + 16 * (c - 1))
                    s_v.inc(v.tensor_add(recs[c % 2][:], XH[:, c * T:(c + 1) * T],
                                         XL[:, c * T:(c + 1) * T]))

            @blk.gpsimd
            def _(g):
                for c in range(NDC):
                    g.wait_ge(s_v.sem, v0 + c + 1)
                    s_g.dma(g.dma_start(dxn[c * 128:(c + 1) * 128, :], recs[c % 2][:]))
                g.wait_ge(s_g.sem, s_g.v)


    if upto >= 4:
        s_in = Sem(nc, "f_in"); s_pe = Sem(nc, "f_pe"); s_a = Sem(nc, "f_a")
        s_v = Sem(nc, "f_v"); s_g = Sem(nc, "f_g"); s_em = Sem(nc, "f_em")
        sg2 = [Sem(nc, "f_g2a"), Sem(nc, "f_g2b")]
        sx = [Sem(nc, "f_xa"), Sem(nc, "f_xb")]
        RXI0 = nc.alloc_sbuf_tensor("RXI0", [128, 128], I32)
        RXI1 = nc.alloc_sbuf_tensor("RXI1", [128, 128], I32)
        S_wut = nc.dram_tensor("S_wut", [D, 2 * II], BF16)
        S_wgt = nc.dram_tensor("S_wgt", [D, 2 * II], BF16)
        S_wdt = nc.dram_tensor("S_wdt", [II, 2 * D], BF16)
        UMX = CTA[:, 0:192]
        GMX = CTA[:, 192:384]
        PMX = CTA[:, 384:576]
        P3MX = CTA[:, 576:768]

        def emit_round(v, dst_t, dst0, src_t, src0, tmp_t, tmp0, ncols):
            for o in range(0, ncols, 128):
                sl = src_t[:, src0 + o:src0 + o + 128]
                v.tensor_copy(RXI0[:], sl)
                v.tensor_sub(tmp_t[:, tmp0:tmp0 + 128], sl, RXI0[:])
                v.tensor_copy(RXI1[:], tmp_t[:, tmp0:tmp0 + 128])
                last = v.tensor_add(dst_t[:, dst0 + o:dst0 + o + 128], RXI0[:], RXI1[:])
            return last

        tiles_f0 = []
        for ic in range(NIC):
            tiles_f0.append((wup_in, S_wut, II, ic * 128, 0))
            tiles_f0.append((wgate_in, S_wgt, II, ic * 128, 0))
        for dc in range(NDC):
            for scc in range(3):
                tiles_f0.append((wdown_in, S_wdt, D, dc * 128, scc * 1024))
        with (
            nc.psum_tensor([128, 128], F32) as dq0,
            nc.psum_tensor([128, 128], F32) as dq1,
        ):
            dqp = [dq0, dq1]
            for win, wdst, LOF, r0, c0 in (() if "f0" in _DIS else tiles_f0):
                with nc.Block() as blk:
                    @blk.sync
                    def _(sync, win=win, r0=r0, c0=c0):
                        s_in.dma(sync.dma_start(FS[:, 0:1024],
                                                win[r0:r0 + 128, c0:c0 + 1024]))

                    @blk.vector
                    def _(v):
                        s_in.wait(v)
                        v.tensor_reduce(FS[:, 1024:1056],
                                        FS[:, 0:1024].rearrange("p (g e) -> p g e", e=32),
                                        AXX, ALU.max, apply_absolute_value=True)
                with nc.Block() as blk:
                    @blk.vector
                    def _(v):
                        v.tensor_scalar(FS[:, 1088:1120], FS[:, 1024:1056],
                                        1.0 / 7.0, 1e-12, ALU.mult, ALU.max)
                with nc.Block() as blk:
                    @blk.scalar
                    def _(s):
                        s.activation(FS[:, 1056:1088], FS[:, 1088:1120], ACT.Ln)
                with nc.Block() as blk:
                    @blk.scalar
                    def _(s):
                        s.activation(FS[:, 1120:1152], FS[:, 1056:1088], ACT.Exp,
                                     scale=-1.0)
                with nc.Block() as blk:
                    @blk.vector
                    def _(v):
                        for e in range(32):
                            v.tensor_mul(FS[:, 2048 + e:2048 + e + 993:32],
                                         FS[:, e:e + 993:32], FS[:, 1120:1152])
                with nc.Block() as blk:
                    @blk.vector
                    def _(v):
                        emit_round(v, FS, 3072, FS, 2048, FS, 4096, 1024)
                with nc.Block() as blk:
                    @blk.vector
                    def _(v):
                        for e in range(32):
                            v.tensor_mul(FS[:, 4224 + e:4224 + e + 993:32],
                                         FS[:, 3072 + e:3072 + e + 993:32],
                                         FS[:, 1088:1120])
                with nc.Block() as blk:
                    p0 = s_pe.v; a0 = s_a.v; v0 = s_v.v
                    g2b = [sg2[0].v, sg2[1].v]

                    @blk.tensor
                    def _(t):
                        for j in range(8):
                            if j >= 2:
                                t.wait_ge(s_v.sem, v0 + j - 1)
                            s_pe.inc(t.transpose(dqp[j % 2][:],
                                                 FS[:, 4224 + j * 128:4224 + (j + 1) * 128],
                                                 IDENT[:]))

                    @blk.scalar
                    def _(s):
                        for j in range(8):
                            s.wait_ge(s_pe.sem, p0 + j + 1)
                            if j >= 2:
                                s.wait_ge(sg2[j % 2].sem,
                                          g2b[j % 2] + 32 * ((j - 2) // 2 + 1))
                            s_a.inc(s.copy(BSC[:, (j % 2) * 256:(j % 2) * 256 + 128],
                                           dqp[j % 2][:]))

                    @blk.vector
                    def _(v):
                        for j in range(8):
                            v.wait_ge(s_a.sem, a0 + j + 1)
                            s_v.inc(v.tensor_sub(
                                BSC[:, (j % 2) * 256 + 128:(j % 2) * 256 + 256],
                                dqp[j % 2][:],
                                BSC[:, (j % 2) * 256:(j % 2) * 256 + 128]))

                    @blk.gpsimd
                    def _(g, wdst=wdst, LOF=LOF, r0=r0, c0=c0):
                        for j in range(8):
                            g.wait_ge(s_v.sem, v0 + j + 1)
                            rr = c0 + j * 128
                            sg2[j % 2].dma(g.dma_start(
                                wdst[rr:rr + 128, r0:r0 + 128],
                                BSC[:, (j % 2) * 256:(j % 2) * 256 + 128]))
                            sg2[j % 2].dma(g.dma_start(
                                wdst[rr:rr + 128, LOF + r0:LOF + r0 + 128],
                                BSC[:, (j % 2) * 256 + 128:(j % 2) * 256 + 256]))

        NTF = 0 if "f1" in _DIS else NT
        with (
            nc.psum_tensor([128, 128], F32) as fb0,
            nc.psum_tensor([128, 128], F32) as fb1,
            nc.Block() as blk,
        ):
            fbk = [fb0, fb1]
            ybufs = [FS[:, 0:1024], FS[:, 1024:2048]]
            rbufs = [FS[:, 2048:3072], FS[:, 3072:4096]]
            a0 = s_a.v; p0 = s_pe.v; v0 = s_v.v
            sxb = [sx[0].v, sx[1].v]
            av = {}; lv = {}
            pos = v0
            order = []
            for i in range(min(2, NTF)):
                order.append(("af", i, 0))
            for j in range(NTF):
                for c in range(NDC):
                    order.append(("lo", j, c))
                if j + 2 < NTF:
                    order.append(("af", j + 2, 0))
            for kind, aa, bb in order:
                pos += 1
                if kind == "af":
                    av[aa] = pos
                else:
                    lv[(aa, bb)] = pos

            @blk.sync
            def _(sync):
                for i in range(NTF):
                    if i >= 2:
                        sync.wait_ge(s_v.sem, av[i - 2])
                    sx[i % 2].dma(sync.dma_start(ybufs[i % 2][:],
                                                 x_in[i * 128:(i + 1) * 128, :]))

            @blk.vector
            def _(v):
                def emit_af(i):
                    v.wait_ge(sx[i % 2].sem, sxb[i % 2] + 16 * (i // 2 + 1))
                    if i >= 2:
                        v.wait_ge(s_pe.sem, p0 + 8 * (i - 1))
                    v.tensor_scalar(ybufs[i % 2][:], ybufs[i % 2][:],
                                    RINV[:, i:i + 1], NU[:, i:i + 1],
                                    ALU.mult, ALU.add)
                    v.tensor_scalar(FS[:, 4096:5120], ybufs[i % 2][:],
                                    B["invs0"][:, 0:1], None, ALU.mult)
                    s_v.inc(emit_round(v, FS, 2048 + 1024 * (i % 2),
                                       FS, 4096, FS, 5120, 1024))

                def emit_lo(j, c):
                    k = j * NDC + c
                    v.wait_ge(s_a.sem, a0 + k + 1)
                    col = c * T + j * 128
                    s_v.inc(v.tensor_sub(XL[:, col:col + 128],
                                         fbk[k % 2][:], XH[:, col:col + 128]))

                for kind, aa, bb in order:
                    if kind == "af":
                        emit_af(aa)
                    else:
                        emit_lo(aa, bb)

            @blk.tensor
            def _(t):
                for i in range(NTF):
                    for c in range(NDC):
                        k = i * NDC + c
                        t.wait_ge(s_v.sem, av[i])
                        if k >= 2:
                            kk = k - 2
                            t.wait_ge(s_v.sem, lv[(kk // NDC, kk % NDC)])
                        s_pe.inc(t.transpose(
                            fbk[k % 2][:],
                            rbufs[i % 2][:, c * 128:(c + 1) * 128], IDENT[:]))

            @blk.scalar
            def _(s):
                for i in range(NTF):
                    for c in range(NDC):
                        k = i * NDC + c
                        s.wait_ge(s_pe.sem, p0 + k + 1)
                        col = c * T + i * 128
                        s_a.inc(s.copy(XH[:, col:col + 128], fbk[k % 2][:]))

        for wsrc, tdst, MX in (() if "f2" in _DIS else
                               ((S_wut, S_tup, UMX), (S_wgt, S_tg, GMX))):
            with (
                nc.psum_tensor([128, 512], F32) as up0,
                nc.psum_tensor([128, 512], F32) as up1,
                nc.psum_tensor([128, 512], F32) as up2,
                nc.psum_tensor([128, 512], F32) as up3,
                nc.psum_tensor([128, 512], F32) as up4,
                nc.psum_tensor([128, 512], F32) as up5,
                nc.psum_tensor([128, 512], F32) as up6,
                nc.psum_tensor([128, 512], F32) as up7,
                nc.Block() as blk,
            ):
                upb = [up0, up1, up2, up3, up4, up5, up6, up7]
                in0 = s_in.v; p0 = s_pe.v; v0 = s_v.v
                g2b = [sg2[0].v, sg2[1].v]

                @blk.sync
                def _(sync, wsrc=wsrc):
                    for ic in range(NIC):
                        if ic >= 2:
                            sync.wait_ge(s_pe.sem, p0 + 8 * (ic - 1))
                        sb = 2048 * (ic % 2)
                        for dc in range(NDC):
                            s_in.dma(sync.dma_start(
                                BSC[:, sb + dc * 128:sb + dc * 128 + 128],
                                wsrc[dc * 128:(dc + 1) * 128, ic * 128:(ic + 1) * 128]))
                            s_in.dma(sync.dma_start(
                                BSC[:, sb + 1024 + dc * 128:sb + 1024 + dc * 128 + 128],
                                wsrc[dc * 128:(dc + 1) * 128,
                                     II + ic * 128:II + (ic + 1) * 128]))

                @blk.tensor
                def _(t):
                    for ic in range(NIC):
                        t.wait_ge(s_in.sem, in0 + 256 * (ic + 1))
                        sb = 2048 * (ic % 2)
                        for tc in range(NT5):
                            k = ic * 8 + tc
                            if k >= 2:
                                t.wait_ge(s_v.sem, v0 + k - 1)
                            bs = (k % 2) * 4
                            last = None
                            for dc in range(NDC):
                                g = dc // 2
                                st0 = (dc % 2 == 0)
                                sp = (dc % 2 == 1)
                                sth = BSC[:, sb + dc * 128:sb + dc * 128 + 128]
                                stl = BSC[:, sb + 1024 + dc * 128:sb + 1024 + dc * 128 + 128]
                                mvh = XH[:, dc * T + tc * 512:dc * T + tc * 512 + 512]
                                mvl = XL[:, dc * T + tc * 512:dc * T + tc * 512 + 512]
                                t.matmul(upb[bs + g][:], sth, mvh,
                                         start=st0, stop=False)
                                t.matmul(upb[bs + g][:], sth, mvl,
                                         start=False, stop=False)
                                last = t.matmul(upb[bs + g][:], stl, mvh,
                                                start=False, stop=sp)
                            s_pe.inc(last)

                @blk.vector
                def _(v, MX=MX):
                    for ic in range(NIC):
                        for tc in range(NT5):
                            k = ic * 8 + tc
                            v.wait_ge(s_pe.sem, p0 + k + 1)
                            if k >= 2:
                                v.wait_ge(sg2[k % 2].sem,
                                          g2b[k % 2] + 16 * ((k - 2) // 2 + 1))
                            bs = (k % 2) * 4
                            st = (k % 2) * 512
                            v.tensor_copy(FS[:, st:st + 512], upb[bs][:])
                            for bb in range(1, 4):
                                v.tensor_add(FS[:, st:st + 512], FS[:, st:st + 512],
                                             upb[bs + bb][:])
                            s_v.inc(v.tensor_reduce(MX[:, k:k + 1], FS[:, st:st + 512],
                                                    AXX, ALU.max,
                                                    apply_absolute_value=True))

                @blk.gpsimd
                def _(g, tdst=tdst):
                    for ic in range(NIC):
                        for tc in range(NT5):
                            k = ic * 8 + tc
                            g.wait_ge(s_v.sem, v0 + k + 1)
                            sg2[k % 2].dma(g.dma_start(
                                tdst[ic * 128:(ic + 1) * 128, tc * 512:(tc + 1) * 512],
                                FS[:, (k % 2) * 512:(k % 2) * 512 + 512]))

        def emit_maxex(srcs, tag):
            if "fx" in _DIS:
                return
            """AllGather per-core colmaxes of each [128,192] src; leave the
            global max of src j in SC1[0:1, 16+j]."""
            n = len(srcs)
            with nc.Block() as blk:
                @blk.vector
                def _(v):
                    for j, MX in enumerate(srcs):
                        v.tensor_reduce(MHQ[:, j:j + 1], MX, AXX, ALU.max)
            with nc.Block() as blk:
                @blk.gpsimd
                def _(g):
                    for j in range(n):
                        s_g.dma(g.dma_start(AG_in[0:1, j * 128:(j + 1) * 128],
                                            MHQ[:, j:j + 1],
                                            allow_slow_non_contiguous=True))
                    g.wait_ge(s_g.sem, s_g.v)
                    if _FAKE_CC[0]:
                        for b in range(NC):
                            s_g.dma(g.dma_start(AG_out[b:b + 1, :], AG_in[:]))
                    else:
                        s_cc.inc(g.collective_compute(
                            "AllGather", ALU.bypass,
                            replica_groups=[list(range(NC))],
                            ins=[AG_in.ap().opt()],
                            outs=[AG_out.ap().opt()]))
                        g.wait_ge(s_cc.sem, s_cc.v)
                    s_g.dma(g.dma_start(AGOS[:], AG_out[:]))
                    g.wait_ge(s_g.sem, s_g.v)
            with nc.Block() as blk:
                @blk.vector
                def _(v):
                    for j in range(n):
                        v.tensor_reduce(SCR[0:NC, 8 + j:9 + j],
                                        AGOS[0:NC, j * 128:(j + 1) * 128],
                                        AXX, ALU.max)
            with nc.Block() as blk:
                @blk.gpsimd
                def _(g):
                    s_g.dma(g.dma_start(S_sm[4:5, 40:40 + NC * n], SCR[0:NC, 8:8 + n]))
                    g.wait_ge(s_g.sem, s_g.v)
                    s_g.dma(g.dma_start(SC1[0:1, 0:NC * n], S_sm[4:5, 40:40 + NC * n]))
                    g.wait_ge(s_g.sem, s_g.v)
            with nc.Block() as blk:
                @blk.vector
                def _(v):
                    for j in range(n):
                        v.tensor_reduce(SC1[0:1, 16 + j:17 + j],
                                        SC1[0:1, j:j + (NC - 1) * n + 1:n],
                                        AXX, ALU.max)

        _smc = [0]

        def bc_scales(sc1_cols, names):
            if "fx" in _DIS:
                return
            """SC1 cols -> S_sm row 5 -> partition-broadcast into B[name]s."""
            with nc.Block() as blk:
                @blk.gpsimd
                def _(g, sc1_cols=tuple(sc1_cols), names=tuple(names), c0=_smc[0]):
                    for jj, sc in enumerate(sc1_cols):
                        s_g.dma(g.dma_start(S_sm[5:6, c0 + jj:c0 + jj + 1],
                                            SC1[0:1, sc:sc + 1]))
                    g.wait_ge(s_g.sem, s_g.v)
                    for jj, nm in enumerate(names):
                        s_g.dma(g.dma_start(B[nm][:], S_sm[5:6, c0 + jj:c0 + jj + 1]
                                            .partition_broadcast(128).squeeze(1)))
                    g.wait_ge(s_g.sem, s_g.v)
            _smc[0] += len(sc1_cols)

        def _sblock(eng, fn):
            if "fx" in _DIS:
                return
            with nc.Block() as blk:
                if eng == "v":
                    blk.vector(fn)
                else:
                    blk.scalar(fn)
        emit_maxex([UMX, GMX], "f3")
        with nc.Block() as blk:
            @blk.vector
            def _(v):
                v.tensor_scalar(SC1[0:1, 18:20], SC1[0:1, 16:18],
                                B["s0"][0:1, 0:1], None, ALU.mult)
        with nc.Block() as blk:
            @blk.vector
            def _(v):
                v.tensor_scalar(SC1[0:1, 20:22], SC1[0:1, 18:20],
                                1.0 / QMAX, 1e-12, ALU.mult, ALU.max)
        with nc.Block() as blk:
            @blk.scalar
            def _(s):
                s.activation(SC1[0:1, 24:26], SC1[0:1, 20:22], ACT.Ln)
        with nc.Block() as blk:
            @blk.scalar
            def _(s):
                s.activation(SC1[0:1, 26:27], B["s0"][0:1, 0:1], ACT.Ln)
        with nc.Block() as blk:
            @blk.vector
            def _(v):
                v.tensor_scalar(SC1[0:1, 27:29], SC1[0:1, 24:26],
                                SC1[0:1, 26:27], None, ALU.subtract)
        with nc.Block() as blk:
            @blk.scalar
            def _(s):
                s.activation(SC1[0:1, 22:24], SC1[0:1, 27:29], ACT.Exp,
                             scale=-1.0)
        bc_scales([20, 21, 22, 23], ["sup", "sg", "cup", "cg"])

        NK4 = 0 if "f4" in _DIS else 192
        with nc.Block() as blk:
            in0 = s_in.v; a0 = s_a.v; v0 = s_v.v
            g2b = [sg2[0].v, sg2[1].v]
            sxb2 = [sx[0].v, sx[1].v]

            @blk.sync
            def _(sync):
                for k in range(NK4):
                    ic, tc = k // 8, k % 8
                    if k >= 2:
                        sync.wait_ge(s_v.sem, v0 + 2 * (k - 2) + 1)
                    sx[k % 2].dma(sync.dma_start(
                        FS[:, (k % 2) * 512:(k % 2) * 512 + 512],
                        S_tg[ic * 128:(ic + 1) * 128, tc * 512:(tc + 1) * 512]))

            @blk.vector
            def _(v):
                for k in range(NK4):
                    v.wait_ge(sx[k % 2].sem, sxb2[k % 2] + 16 * (k // 2 + 1))
                    if k >= 2:
                        v.wait_ge(s_a.sem, a0 + k - 1)
                    tin = (k % 2) * 512
                    trg = 1024 + (k % 2) * 512
                    v.tensor_scalar(FS[:, 4096:4608], FS[:, tin:tin + 512],
                                    B["cg"][:, 0:1], None, ALU.mult)
                    emit_round(v, FS, trg, FS, 4096, FS, 5248, 512)
                    s_v.inc(v.tensor_copy(FS[:, 5248:5249], FS[:, 5248:5249]))
                    v.wait_ge(s_a.sem, a0 + k + 1)
                    if k >= 2:
                        v.wait_ge(sg2[k % 2].sem, g2b[k % 2] + 16 * ((k - 2) // 2 + 1))
                    tsg = 2048 + (k % 2) * 512
                    tp2 = 3072 + (k % 2) * 512
                    v.tensor_scalar(FS[:, 4096:4608], FS[:, tsg:tsg + 512],
                                    65536.0, None, ALU.mult)
                    emit_round(v, FS, 4608, FS, 4096, FS, 5248, 512)
                    v.tensor_scalar(FS[:, 4608:5120], FS[:, 4608:5120],
                                    65535.0, None, ALU.min)
                    v.tensor_mul(FS[:, tp2:tp2 + 512], FS[:, trg:trg + 512],
                                 FS[:, 4608:5120])
                    v.tensor_reduce(PMX[:, k:k + 1], FS[:, tp2:tp2 + 512],
                                    AXX, ALU.max, apply_absolute_value=True)
                    s_v.inc(v.tensor_copy(FS[:, 5248:5249], FS[:, 5248:5249]))

            @blk.scalar
            def _(s):
                for k in range(NK4):
                    s.wait_ge(s_v.sem, v0 + 2 * k + 1)
                    trg = 1024 + (k % 2) * 512
                    tsg = 2048 + (k % 2) * 512
                    s_a.inc(s.activation(FS[:, tsg:tsg + 512], FS[:, trg:trg + 512],
                                         ACT.Sigmoid, scale=B["sg"][:, 0:1]))

            @blk.gpsimd
            def _(g):
                for k in range(NK4):
                    ic, tc = k // 8, k % 8
                    g.wait_ge(s_v.sem, v0 + 2 * k + 2)
                    tp2 = 3072 + (k % 2) * 512
                    sg2[k % 2].dma(g.dma_start(
                        S_p2[ic * 128:(ic + 1) * 128, tc * 512:(tc + 1) * 512],
                        FS[:, tp2:tp2 + 512]))

        emit_maxex([PMX], "f4")
        with nc.Block() as blk:
            @blk.vector
            def _(v):
                v.tensor_scalar(SC1[0:1, 18:19], SC1[0:1, 16:17],
                                B["sg"][0:1, 0:1], None, ALU.mult)
        with nc.Block() as blk:
            @blk.vector
            def _(v):
                v.tensor_scalar(SC1[0:1, 20:21], SC1[0:1, 18:19],
                                1.0 / (65536.0 * QMAX), 1e-12, ALU.mult, ALU.max)
        with nc.Block() as blk:
            @blk.scalar
            def _(s):
                s.activation(SC1[0:1, 24:25], SC1[0:1, 20:21], ACT.Ln)
        with nc.Block() as blk:
            @blk.scalar
            def _(s):
                s.activation(SC1[0:1, 25:26], B["sg"][0:1, 0:1], ACT.Ln)
        with nc.Block() as blk:
            @blk.vector
            def _(v):
                v.tensor_scalar(SC1[0:1, 26:27], SC1[0:1, 25:26],
                                SC1[0:1, 24:25], None, ALU.subtract)
        with nc.Block() as blk:
            @blk.vector
            def _(v):
                v.tensor_scalar(SC1[0:1, 21:22], SC1[0:1, 26:27],
                                -11.090354888959125, None, ALU.add)
        with nc.Block() as blk:
            @blk.scalar
            def _(s):
                s.activation(SC1[0:1, 22:23], SC1[0:1, 21:22], ACT.Exp)
        bc_scales([20, 22], ["s2", "c2"])

        NK5 = 0 if "f5" in _DIS else 192
        with nc.Block() as blk:
            in0 = s_in.v; v0 = s_v.v
            g2b = [sg2[0].v, sg2[1].v]
            sxb2 = [sx[0].v, sx[1].v]
            seb = s_em.v

            @blk.sync
            def _(sync):
                for k in range(NK5):
                    ic, tc = k // 8, k % 8
                    if k >= 2:
                        sync.wait_ge(s_v.sem, v0 + (k - 1))
                    rows = slice(ic * 128, (ic + 1) * 128)
                    cols = slice(tc * 512, (tc + 1) * 512)
                    sx[k % 2].dma(sync.dma_start(
                        FS[:, (k % 2) * 512:(k % 2) * 512 + 512], S_p2[rows, cols]))
                    s_em.dma(sync.dma_start(
                        FS[:, 1024 + (k % 2) * 512:1024 + (k % 2) * 512 + 512],
                        S_tup[rows, cols]))

            @blk.vector
            def _(v):
                for k in range(NK5):
                    v.wait_ge(sx[k % 2].sem, sxb2[k % 2] + 16 * (k // 2 + 1))
                    v.wait_ge(s_em.sem, seb + 16 * (k + 1))
                    if k >= 2:
                        v.wait_ge(sg2[k % 2].sem, g2b[k % 2] + 16 * ((k - 2) // 2 + 1))
                    tp2 = (k % 2) * 512
                    tup = 1024 + (k % 2) * 512
                    tp3 = 2048 + (k % 2) * 512
                    v.tensor_scalar(FS[:, 4096:4608], FS[:, tp2:tp2 + 512],
                                    B["c2"][:, 0:1], None, ALU.mult)
                    emit_round(v, FS, 3072, FS, 4096, FS, 5248, 512)
                    v.tensor_scalar(FS[:, 4096:4608], FS[:, tup:tup + 512],
                                    B["cup"][:, 0:1], None, ALU.mult)
                    emit_round(v, FS, 4608, FS, 4096, FS, 5248, 512)
                    v.tensor_mul(FS[:, tp3:tp3 + 512], FS[:, 3072:3584],
                                 FS[:, 4608:5120])
                    v.tensor_reduce(P3MX[:, k:k + 1], FS[:, tp3:tp3 + 512],
                                    AXX, ALU.max, apply_absolute_value=True)
                    s_v.inc(v.tensor_copy(FS[:, 5248:5249], FS[:, 5248:5249]))

            @blk.gpsimd
            def _(g):
                for k in range(NK5):
                    ic, tc = k // 8, k % 8
                    g.wait_ge(s_v.sem, v0 + k + 1)
                    tp3 = 2048 + (k % 2) * 512
                    sg2[k % 2].dma(g.dma_start(
                        S_p3[ic * 128:(ic + 1) * 128, tc * 512:(tc + 1) * 512],
                        FS[:, tp3:tp3 + 512]))

        emit_maxex([P3MX], "f5")
        with nc.Block() as blk:
            @blk.vector
            def _(v):
                v.tensor_scalar(SC1[0:1, 18:19], SC1[0:1, 16:17],
                                B["s2"][0:1, 0:1], None, ALU.mult)
        with nc.Block() as blk:
            @blk.vector
            def _(v):
                v.tensor_scalar(SC1[0:1, 18:19], SC1[0:1, 18:19],
                                B["sup"][0:1, 0:1], None, ALU.mult)
        with nc.Block() as blk:
            @blk.vector
            def _(v):
                v.tensor_scalar(SC1[0:1, 20:21], SC1[0:1, 18:19],
                                1.0 / QMAX, 1e-12, ALU.mult, ALU.max)
        with nc.Block() as blk:
            @blk.scalar
            def _(s):
                s.activation(SC1[0:1, 24:25], SC1[0:1, 20:21], ACT.Ln)
        with nc.Block() as blk:
            @blk.scalar
            def _(s):
                s.activation(SC1[0:1, 25:26], B["s2"][0:1, 0:1], ACT.Ln)
        with nc.Block() as blk:
            @blk.scalar
            def _(s):
                s.activation(SC1[0:1, 26:27], B["sup"][0:1, 0:1], ACT.Ln)
        with nc.Block() as blk:
            @blk.vector
            def _(v):
                v.tensor_add(SC1[0:1, 27:28], SC1[0:1, 25:26], SC1[0:1, 26:27])
        with nc.Block() as blk:
            @blk.vector
            def _(v):
                v.tensor_sub(SC1[0:1, 21:22], SC1[0:1, 27:28], SC1[0:1, 24:25])
        with nc.Block() as blk:
            @blk.scalar
            def _(s):
                s.activation(SC1[0:1, 22:23], SC1[0:1, 21:22], ACT.Exp)
        bc_scales([20, 22], ["s3", "c3"])

        with nc.Block() as blk:
            @blk.gpsimd
            def _(g):
                s_g.dma(g.dma_start(CTA[:, 0:1024], S_sm[1:2, 0:D]
                                    .partition_broadcast(128).squeeze(1)))
                g.wait_ge(s_g.sem, s_g.v)

        for tb in (() if "f6" in _DIS else range(8)):
            with (
                nc.psum_tensor([128, 512], F32) as w0,
                nc.psum_tensor([128, 512], F32) as w1,
                nc.psum_tensor([128, 512], F32) as w2,
                nc.psum_tensor([128, 512], F32) as w3,
                nc.psum_tensor([128, 512], F32) as w4,
                nc.psum_tensor([128, 512], F32) as w5,
                nc.psum_tensor([128, 512], F32) as w6,
                nc.psum_tensor([128, 512], F32) as w7,
                nc.Block() as blk,
            ):
                wps = [w0, w1, w2, w3, w4, w5, w6, w7]
                in0 = s_in.v; p0 = s_pe.v; v0 = s_v.v; g0 = s_g.v
                sxb2 = [sx[0].v, sx[1].v]
                seb = s_em.v

                @blk.sync
                def _(sync, tb=tb):
                    for ic in range(NIC):
                        if ic >= 2:
                            sync.wait_ge(s_v.sem, v0 + 1 + (ic - 2) + (ic - 2) // 2 + 1)
                            sync.wait_ge(s_pe.sem, p0 + ic // 2)
                        sx[ic % 2].dma(sync.dma_start(
                            FS[:, (ic % 2) * 512:(ic % 2) * 512 + 512],
                            S_p3[ic * 128:(ic + 1) * 128, tb * 512:(tb + 1) * 512]))
                        s_em.dma(sync.dma_start(
                            XH[:, (ic % 2) * 2048:(ic % 2) * 2048 + 1024],
                            S_wdt[ic * 128:(ic + 1) * 128, 0:D]))
                        s_em.dma(sync.dma_start(
                            XH[:, (ic % 2) * 2048 + 1024:(ic % 2) * 2048 + 2048],
                            S_wdt[ic * 128:(ic + 1) * 128, D:2 * D]))
                    for tt in range(4):
                        if tt >= 1:
                            sync.wait_ge(s_v.sem, v0 + 1 + NIC + NIC // 2 + tt)
                        sync.wait_ge(s_g.sem, g0 + 16 * tt)
                        s_in.dma(sync.dma_start(
                            FS[:, 5760:6784],
                            x_in[tb * 512 + tt * 128:tb * 512 + (tt + 1) * 128, :]))
                        for h in range(NH):
                            s_in.dma(sync.dma_start(
                                FS[:, 6784 + h * 256:6784 + (h + 1) * 256],
                                S_attn[h, tb * 512 + tt * 128:tb * 512 + (tt + 1) * 128, :]))

                @blk.vector
                def _(v, tb=tb):
                    s_v.inc(v.memset(FS[:, 1664:5760], 0.0))
                    for ic in range(NIC):
                        v.wait_ge(sx[ic % 2].sem, sxb2[ic % 2] + 16 * (ic // 2 + 1))
                        if ic >= 2:
                            v.wait_ge(s_pe.sem, p0 + ic // 2)
                        tin = (ic % 2) * 512
                        v.tensor_scalar(FS[:, 1024:1536], FS[:, tin:tin + 512],
                                        B["c3"][:, 0:1], None, ALU.mult)
                        emit_round(v, FS, 1024, FS, 1024, FS, 1536, 512)
                        v.tensor_copy(XL[:, (ic % 2) * 1024:(ic % 2) * 1024 + 512],
                                      FS[:, 1024:1536])
                        s_v.inc(v.tensor_sub(
                            XL[:, (ic % 2) * 1024 + 512:(ic % 2) * 1024 + 1024],
                            FS[:, 1024:1536],
                            XL[:, (ic % 2) * 1024:(ic % 2) * 1024 + 512]))
                        if ic % 2 == 1:
                            v.wait_ge(s_pe.sem, p0 + ic // 2 + 1)
                            for bb in range(8):
                                ins = v.tensor_add(
                                    FS[:, 1664 + bb * 512:1664 + (bb + 1) * 512],
                                    FS[:, 1664 + bb * 512:1664 + (bb + 1) * 512],
                                    wps[bb][:])
                            s_v.inc(ins)
                    for tt in range(4):
                        v.wait_ge(s_in.sem, in0 + 80 * (tt + 1))
                        ab = 1664 + tt * 1024
                        v.tensor_scalar(FS[:, ab:ab + 1024], FS[:, ab:ab + 1024],
                                        B["s3"][:, 0:1], None, ALU.mult)
                        v.tensor_add(FS[:, ab:ab + 1024], FS[:, ab:ab + 1024],
                                     FS[:, 5760:6784])
                        for h in range(NH):
                            col = h * 32 + tb * 4 + tt
                            v.tensor_scalar(FS[:, 6784 + h * 256:6784 + (h + 1) * 256],
                                            FS[:, 6784 + h * 256:6784 + (h + 1) * 256],
                                            ABT[:, col:col + 1],
                                            ABT[:, 128 + col:128 + col + 1],
                                            ALU.mult, ALU.add)
                        for h in range(NH):
                            v.tensor_mul(FS[:, 6784 + h * 256:6784 + (h + 1) * 256],
                                         FS[:, 6784 + h * 256:6784 + (h + 1) * 256],
                                         CTA[:, h * 256:(h + 1) * 256])
                        for h in range(NH):
                            ins = v.tensor_add(
                                FS[:, ab + h * 256:ab + (h + 1) * 256],
                                FS[:, ab + h * 256:ab + (h + 1) * 256],
                                FS[:, 6784 + h * 256:6784 + (h + 1) * 256])
                        s_v.inc(ins)

                @blk.tensor
                def _(t):
                    for ic in range(NIC):
                        jp = ic // 2
                        t.wait_ge(s_v.sem, v0 + 1 + ic + jp + 1)
                        t.wait_ge(s_em.sem, seb + 32 * (ic + 1))
                        if ic % 2 == 0 and jp >= 1:
                            t.wait_ge(s_v.sem, v0 + 3 * jp + 1)
                        last = None
                        sh = XL[:, (ic % 2) * 1024:(ic % 2) * 1024 + 512]
                        sl = XL[:, (ic % 2) * 1024 + 512:(ic % 2) * 1024 + 1024]
                        mh = XH[:, (ic % 2) * 2048:(ic % 2) * 2048 + 1024]
                        ml = XH[:, (ic % 2) * 2048 + 1024:(ic % 2) * 2048 + 2048]
                        for tt in range(4):
                            for hf in range(2):
                                pb = wps[tt * 2 + hf]
                                sth = sh[:, tt * 128:(tt + 1) * 128]
                                stl = sl[:, tt * 128:(tt + 1) * 128]
                                mvh = mh[:, hf * 512:(hf + 1) * 512]
                                mvl = ml[:, hf * 512:(hf + 1) * 512]
                                t.matmul(pb[:], sth, mvh, start=(ic % 2 == 0),
                                         stop=False)
                                t.matmul(pb[:], sth, mvl, start=False, stop=False)
                                last = t.matmul(pb[:], stl, mvh, start=False,
                                                stop=(ic % 2 == 1))
                        if ic % 2 == 1:
                            s_pe.inc(last)

                @blk.gpsimd
                def _(g, tb=tb):
                    for tt in range(4):
                        g.wait_ge(s_v.sem, v0 + 1 + NIC + NIC // 2 + tt + 1)
                        r0 = tb * 512 + tt * 128
                        ab = 1664 + tt * 1024
                        s_g.dma(g.dma_start(out_d[r0:r0 + 128, :], FS[:, ab:ab + 1024]))

    if upto < 4:
        with (
            nc.Block() as blk,
        ):
            ob = FS[:, 0:1024]

            @blk.vector
            def _(v):
                s_v.inc(v.memset(ob[:], 0.0))

            @blk.gpsimd
            def _(g):
                s_v.wait(g)
                for i in range(NT):
                    s_g.dma(g.dma_start(out_d[i * 128:(i + 1) * 128, :], ob[:]))
                s_g.wait(g)

    nc.finalize()
    return nc, list(dbg.keys())


def _host_prepare(inputs):
    x = np.ascontiguousarray(inputs["L0_attn_x"], dtype=np.float32)
    wq = np.asarray(inputs["L0_attn_wq"], np.float32)
    wk = np.asarray(inputs["L0_attn_wk"], np.float32)
    wv = np.asarray(inputs["L0_attn_wv"], np.float32)

    def blocks(w):
        out = np.zeros((NDC, 128, 128), np.float32)
        for c in range(NDC):
            for g in range(32):
                out[c, 4 * g:4 * g + 4, 4 * g:4 * g + 4] = w[32 * c + g].T
        return out

    consts = np.zeros((128, 642), np.float32)
    consts[:, 0:128] = np.eye(128, dtype=np.float32)
    consts[:, 128:640] = np.broadcast_to(np.arange(512, dtype=np.float32), (128, 512))
    consts[:, 640] = np.arange(128, dtype=np.float32)
    consts[:, 641] = 1.0

    import ml_dtypes
    masks = np.zeros((4, 128, 512), np.float32)
    ls = np.arange(512)
    for j in range(4):
        for p in range(128):
            masks[j, p, :] = (ls >= 128 * j + p).astype(np.float32)
    masks = masks.astype(ml_dtypes.bfloat16)

    shared = {
        "wqd_in": blocks(wq),
        "wkd_in": blocks(wk),
        "wvd_in": blocks(wv),
        "wi_in": np.asarray(inputs["L0_attn_wi"], np.float32),
        "wf_in": np.asarray(inputs["L0_attn_wf"], np.float32),
        "bi_in": np.asarray(inputs["L0_attn_bi"], np.float32).reshape(NH, 1),
        "bf_in": np.asarray(inputs["L0_attn_bf"], np.float32).reshape(NH, 1),
        "lnw_in": np.asarray(inputs["L0_attn_ln_w"], np.float32).reshape(1, D),
        "wgate_in": np.asarray(inputs["L0_ffn_w_gate"], np.float32),
        "wup_in": np.asarray(inputs["L0_ffn_w_up"], np.float32),
        "wdown_in": np.asarray(inputs["L0_ffn_w_down"], np.float32),
        "consts_in": consts,
        "masks_in": masks,
    }
    in_maps = []
    for b in range(NC):
        m = dict(shared)
        m["x_in"] = np.ascontiguousarray(x[b])
        in_maps.append(m)
    return in_maps


def kernel(**inputs):
    in_maps = _host_prepare(inputs)
    nc, _ = build_nc()
    res = run_bass_kernel_spmd(nc, in_maps, core_ids=list(range(NC)))
    return np.stack([r["out"] for r in res.results], axis=0)
